# version 03
# baseline (speedup 1.0000x reference)
import numpy as np
import ml_dtypes
import concourse.bass as bass
import concourse.mybir as mybir
from concourse.bass_utils import run_bass_kernel_spmd

F32 = mybir.dt.float32
BF16 = mybir.dt.bfloat16
AF = mybir.ActivationFunctionType
ALU = mybir.AluOpType
AX = mybir.AxisListType

NCORES = 8
D = 1024
T = 4096
PT = 128
DT = D // PT
LN_EPS = 1e-5

BF = ml_dtypes.bfloat16


class Prog:

    ENGINES = ("sync", "tensor", "vector", "scalar", "gpsimd", "sync2")

    def __init__(self, nc):
        self.nc = nc
        self.q = {e: [] for e in self.ENGINES}
        self.sems = {}
        self.cnt = {}
        self.last_wait = {}
        for e in self.ENGINES:
            h = nc.ctx.enter_context(nc.semaphore(f"sem_{e}"))
            self.sems[e] = h
            self.cnt[e] = 0

    def emit(self, eng, fn, deps=(), dma=False, inc=None):
        if inc is None:
            inc = 16 if dma else 1
        self.cnt[eng] += inc
        tok = (eng, self.cnt[eng])
        sem = self.sems[eng]
        deps = tuple(deps)

        def code(e, fn=fn, deps=deps, sem=sem, inc=inc, eng=eng):
            for (s, c) in deps:
                key = (eng, s)
                if self.last_wait.get(key, 0) < c:
                    e.wait_ge(self.sems[s], c)
                    self.last_wait[key] = c
            instr = fn(e)
            instr.then_inc(sem, inc)
        self.q[eng].append(code)
        return tok

    def gate(self, eng):
        tot = self.cnt[eng]
        return self.emit(eng, lambda e: e.nop(), deps=[(eng, tot)])

    def barrier_tokens(self):
        return [(e, self.cnt[e]) for e in self.ENGINES if self.cnt[e] > 0]

    def finalize(self, block):
        for e in self.ENGINES:
            ops = self.q[e]
            if not ops:
                continue

            def run(engine, ops=ops):
                for op in ops:
                    op(engine)
            attr = "gpsimd" if e == "sync2" else e
            getattr(block, attr)(run)


def ceil_div(a, b):
    return -(-a // b)


def build_ffn0(scale_b1=1.0):
    H = 4096
    HT = H // PT
    TB = 256
    NB = T // TB

    nc = bass.Bass("TRN2", target_bir_lowering=False)
    x_in = nc.dram_tensor("x_in", [D, T], F32, kind="ExternalInput")
    w1 = nc.dram_tensor("w1", [D, H], F32, kind="ExternalInput")
    b1c = nc.dram_tensor("b1c", [PT, HT], F32, kind="ExternalInput")
    w2 = nc.dram_tensor("w2", [H, D], F32, kind="ExternalInput")
    b2c = nc.dram_tensor("b2c", [PT, DT], F32, kind="ExternalInput")
    x_out = nc.dram_tensor("x_out", [D, T], F32, kind="ExternalOutput")

    from contextlib import ExitStack
    es = nc.ctx
    sb = lambda n, s, d=F32: es.enter_context(nc.sbuf_tensor(n, s, d))
    ps = lambda n, s, d=F32: es.enter_context(nc.psum_tensor(n, s, d))

    w1s = [sb(f"w1_{i}", [PT, H], BF16) for i in range(DT)]
    w2s = [sb(f"w2_{i}", [PT, D], BF16) for i in range(HT)]
    b1s = sb("b1s", [PT, HT])
    b2s = sb("b2s", [PT, DT])
    b2h = sb("b2h", [PT, DT])
    ones_col = sb("ones_col", [PT, 1], BF16)
    eps_col = sb("eps_col", [PT, 1])
    ones_row = sb("ones_row", [1, PT], BF16)

    x_sb = [sb(f"x_{i}", [PT, TB]) for i in range(DT)]
    xb16 = [sb(f"xb16_{i}", [PT, TB], BF16) for i in range(DT)]
    sq16 = [sb(f"sq16_{i}", [PT, TB], BF16) for i in range(DT)]
    xn = [sb(f"xn_{i}", [PT, TB], BF16) for i in range(DT)]
    h16 = [sb(f"h16_{i}", [PT, TB], BF16) for i in range(HT)]
    ssum = sb("ssum", [1, TB])
    ssq = sb("ssq", [1, TB])
    mrow = sb("mrow", [1, TB])[0:1, :]
    rrow = sb("rrow", [1, TB])[0:1, :]
    m16 = sb("m16", [1, TB], BF16)[0:1, :]
    r16 = sb("r16", [1, TB], BF16)[0:1, :]
    mb = sb("mb", [PT, TB])
    rb = sb("rb", [PT, TB])
    tmp = sb("tmpx", [PT, TB])
    xo = [sb(f"xo_{i}", [PT, TB]) for i in range(DT)]

    pstat_s = ps("pstat_s", [1, TB])
    pstat_q = ps("pstat_q", [1, TB])
    pbc = ps("pbc", [PT, TB])
    pbc2 = ps("pbc2", [PT, TB])
    pmm = [ps(f"pmm_{i}", [PT, TB]) for i in range(4)]

    p = Prog(nc)

    wtoks = []
    for i in range(DT):
        wtoks.append(p.emit("gpsimd", lambda g, i=i: g.dma_start(
            out=w1s[i][:, :], in_=w1[i * PT:(i + 1) * PT, :]), dma=True))
    for i in range(HT):
        wtoks.append(p.emit("gpsimd", lambda g, i=i: g.dma_start(
            out=w2s[i][:, :], in_=w2[i * PT:(i + 1) * PT, :]), dma=True))
    tb1 = p.emit("sync", lambda s: s.dma_start(out=b1s[:, :], in_=b1c[:, :]), dma=True)
    tb2 = p.emit("sync", lambda s: s.dma_start(out=b2s[:, :], in_=b2c[:, :]), dma=True)
    wgate = p.gate("gpsimd")
    sg0 = p.gate("sync")
    tb1, tb2 = sg0, sg0
    tb2h = p.emit("vector", lambda v: v.tensor_single_scalar(
        b2h[:, :], b2s[:, :], 0.5, ALU.mult), deps=[tb2])
    tco = p.emit("vector", lambda v: v.memset(ones_col[:, :], 1.0))
    tep = p.emit("vector", lambda v: v.memset(eps_col[:, :], LN_EPS))
    tro = p.emit("vector", lambda v: v.memset(ones_row[:, :], 1.0))

    w1tok = wgate
    w2tok = wgate

    prev_end = []
    for blk in range(NB):
        ts_ = slice(blk * TB, (blk + 1) * TB)
        for i in range(DT):
            p.emit("sync", lambda s, i=i, ts_=ts_: s.dma_start(
                out=x_sb[i][:, :], in_=x_in[i * PT:(i + 1) * PT, ts_]),
                deps=prev_end, dma=True)
        gx = p.gate("sync")
        xt = [gx] * DT
        cp = []
        for i in range(DT):
            c1 = p.emit("vector", lambda v, i=i: v.tensor_copy(
                xb16[i][:, :], x_sb[i][:, :]), deps=[xt[i]])
            c2 = p.emit("scalar", lambda s, i=i: s.activation(
                sq16[i][:, :], x_sb[i][:, :], AF.Square), deps=[xt[i]])
            cp.append((c1, c2))
        st = None
        for i in range(DT):
            st = p.emit("tensor", lambda t, i=i: t.matmul(
                pstat_s[0:1, :], ones_col[:, :], xb16[i][:, :],
                start=(i == 0), stop=(i == DT - 1)),
                deps=[cp[i][0], tco])
        for i in range(DT):
            st = p.emit("tensor", lambda t, i=i: t.matmul(
                pstat_q[0:1, :], ones_col[:, :], sq16[i][:, :],
                start=(i == 0), stop=(i == DT - 1)),
                deps=[cp[i][1], tco])
        e2 = p.emit("vector", lambda v: v.tensor_single_scalar(
            mrow, pstat_s[0:1, :], 1.0 / D, ALU.mult), deps=[st])
        e3 = p.emit("scalar", lambda s: s.activation(
            tmp[0:1, 0:TB], mrow, AF.Square), deps=[e2])
        e4 = p.emit("vector", lambda v: v.tensor_single_scalar(
            rrow, pstat_q[0:1, :], 1.0 / D, ALU.mult), deps=[st])
        e5 = p.emit("vector", lambda v: v.tensor_sub(
            rrow, rrow, tmp[0:1, 0:TB]), deps=[e4, e3])
        e6 = p.emit("scalar", lambda s: s.activation(
            tmp[0:1, 0:TB], rrow, AF.Sqrt, bias=eps_col[0:1, 0:1]), deps=[e5, tep])
        e7 = p.emit("vector", lambda v: v.reciprocal(rrow, tmp[0:1, 0:TB]),
                    deps=[e6])
        e8 = p.emit("vector", lambda v: v.tensor_copy(m16, mrow),
                    deps=[e2])
        e9 = p.emit("vector", lambda v: v.tensor_copy(r16, rrow),
                    deps=[e7])
        bc1 = p.emit("tensor", lambda t: t.matmul(
            pbc[:, :], ones_row[:, :], m16, start=True, stop=True),
            deps=[e8, tro])
        bc2 = p.emit("tensor", lambda t: t.matmul(
            pbc2[:, :], ones_row[:, :], r16, start=True, stop=True),
            deps=[e9, tro])
        bce1 = p.emit("vector", lambda v: v.tensor_copy(mb[:, :], pbc[:, :]),
                      deps=[bc1])
        bce2 = p.emit("vector", lambda v: v.tensor_copy(rb[:, :], pbc2[:, :]),
                      deps=[bc2])
        xnt = []
        for i in range(DT):
            s1 = p.emit("vector", lambda v, i=i: v.tensor_sub(
                tmp[:, :], x_sb[i][:, :], mb[:, :]), deps=[bce1, xt[i]])
            s2 = p.emit("vector", lambda v, i=i: v.tensor_mul(
                xn[i][:, :], tmp[:, :], rb[:, :]), deps=[bce2, s1])
            xnt.append(s2)
        htoks = []
        for j in range(HT):
            bank = pmm[j % 2]
            mt = None
            for k in range(DT):
                mt = p.emit("tensor", lambda t, j=j, k=k, bank=bank: t.matmul(
                    bank[:, :], w1s[k][:, j * PT:(j + 1) * PT], xn[k][:, :],
                    start=(k == 0), stop=(k == DT - 1)),
                    deps=[xnt[k], w1tok] + ([htoks[j - 2]] if j >= 2 else []))
            ht = p.emit("scalar", lambda s, j=j, bank=bank: s.activation(
                h16[j][:, :], bank[:, :], AF.Gelu, bias=b1s[:, j:j + 1],
                scale=scale_b1), deps=[mt, tb1])
            htoks.append(ht)
        otoks = []
        for i in range(DT):
            bank = pmm[2 + i % 2]
            mt = None
            for k in range(HT):
                mt = p.emit("tensor", lambda t, i=i, k=k, bank=bank: t.matmul(
                    bank[:, :], w2s[k][:, i * PT:(i + 1) * PT], h16[k][:, :],
                    start=(k == 0), stop=(k == HT - 1)),
                    deps=[htoks[k], w2tok] + ([otoks[i - 2][0]] if i >= 2 else []))
            o1 = p.emit("vector", lambda v, i=i, bank=bank: v.scalar_tensor_tensor(
                xo[i][:, :], bank[:, :], 0.5, x_sb[i][:, :],
                ALU.mult, ALU.add), deps=[mt])
            o2 = p.emit("vector", lambda v, i=i: v.tensor_scalar(
                xo[i][:, :], xo[i][:, :], b2h[:, i:i + 1], None,
                ALU.add), deps=[o1, tb2h])
            otoks.append((o1, o2))
            p.emit("sync2", lambda s, i=i, ts_=ts_: s.dma_start(
                out=x_out[i * PT:(i + 1) * PT, ts_], in_=xo[i][:, :]),
                deps=[o2], dma=True)
            if i == DT - 1:
                prev_end = [p.gate("sync2")]

    toks = p.barrier_tokens()
    p.emit("sync", lambda s: s.nop(), deps=toks)

    with nc.Block() as block:
        p.finalize(block)
    return nc


def _b2_scale_note():
    pass


def _prep_ffn0_inputs(xcore, w1, b1, w2, b2):
    b1c = np.ascontiguousarray(b1.reshape(32, 128).T)
    b2c = np.ascontiguousarray(b2.reshape(8, 128).T)
    return {
        "x_in": np.ascontiguousarray(xcore),
        "w1": np.ascontiguousarray(w1),
        "b1c": b1c,
        "w2": np.ascontiguousarray(w2),
        "b2c": b2c,
    }


_PROGS = {}


def _get_prog(name):
    if name not in _PROGS:
        _PROGS[name] = {"ffn0": build_ffn0, "mamba1": build_mamba1,
                        "moe_stats": build_moe_stats, "moe_main": build_moe_main,
                        "ssd": build_ssd}[name]()
    return _PROGS[name]


def build_mamba1(dbg=False):
    DI = 2048
    IT = DI // PT
    S = 16
    Tc = 512
    NCH = T // Tc
    EPS2 = float(np.finfo(np.float32).eps)

    nc = bass.Bass("TRN2", target_bir_lowering=False)
    x_in = nc.dram_tensor("x_in", [D, T], F32, kind="ExternalInput")
    nw_c = nc.dram_tensor("nw_c", [PT, DT], F32, kind="ExternalInput")
    ipT = nc.dram_tensor("ipT", [D, 2 * DI], F32, kind="ExternalInput")
    convw_c = nc.dram_tensor("convw_c", [PT, IT * 4], F32, kind="ExternalInput")
    convb_c = nc.dram_tensor("convb_c", [PT, IT], F32, kind="ExternalInput")
    xpT_r = nc.dram_tensor("xpT_r", [DI, 128], F32, kind="ExternalInput")
    dtpT = nc.dram_tensor("dtpT", [64, DI], F32, kind="ExternalInput")
    dtb_c = nc.dram_tensor("dtb_c", [PT, IT], F32, kind="ExternalInput")
    Alog_c = nc.dram_tensor("Alog_c", [PT, IT * S], F32, kind="ExternalInput")
    D_c = nc.dram_tensor("D_c", [PT, IT], F32, kind="ExternalInput")
    opT = nc.dram_tensor("opT", [DI, D], F32, kind="ExternalInput")
    ind_in = nc.dram_tensor("ind_in", [S, S * PT], F32, kind="ExternalInput")
    x_out = nc.dram_tensor("x_out", [D, T], F32, kind="ExternalOutput")
    if dbg:
        dbg_xn = nc.dram_tensor("dbg_xn", [D, 512], F32, kind="ExternalOutput")
        dbg_u = nc.dram_tensor("dbg_u", [2048, 512], F32, kind="ExternalOutput")
        dbg_delta = nc.dram_tensor("dbg_delta", [PT, 16 * 512], F32, kind="ExternalOutput")
        dbg_B = nc.dram_tensor("dbg_B", [16, 512], F32, kind="ExternalOutput")
        dbg_C = nc.dram_tensor("dbg_C", [16, 512], F32, kind="ExternalOutput")
        dbg_y = nc.dram_tensor("dbg_y", [2048, 512], F32, kind="ExternalOutput")
        dbg_hcar = nc.dram_tensor("dbg_hcar", [PT, 256], F32, kind="ExternalOutput")
        dbg_u1 = nc.dram_tensor("dbg_u1", [2048, 512], F32, kind="ExternalOutput")
        dbg_d1 = nc.dram_tensor("dbg_d1", [PT, 16 * 512], F32, kind="ExternalOutput")

    es = nc.ctx
    sb = lambda n, s, d=F32: es.enter_context(nc.sbuf_tensor(n, s, d))
    psf = lambda n, s, d=F32: es.enter_context(nc.psum_tensor(n, s, d))

    opTs = [sb(f"opT_{k}", [PT, D], BF16) for k in range(IT)]
    xpTs = [sb(f"xpT_{k}", [PT, 128], BF16) for k in range(IT)]
    dtpTs = sb("dtpTs", [64, DI], BF16)
    A_sb = sb("A_sb", [PT, IT * S])
    nw = sb("nw", [PT, DT])
    convw = sb("convw", [PT, IT * 4])
    convb = sb("convb", [PT, IT])
    dtb = sb("dtb", [PT, IT])
    Dp = sb("Dp", [PT, IT])
    ind16 = sb("ind16", [S, S * PT], BF16)
    ones_row = sb("ones_row", [1, PT], BF16)
    ones_col = sb("ones_colm", [PT, 1], BF16)
    epsc = sb("epsc", [PT, 1])
    eps2c = sb("eps2c", [PT, 1])
    ipg = [[sb(f"ipg_{b}_{k}", [PT, 512], BF16) for k in range(DT)] for b in range(2)]
    x_sb = [sb(f"x_{i}", [PT, Tc]) for i in range(DT)]
    xn16 = [sb(f"xn_{i}", [PT, Tc], BF16) for i in range(DT)]
    xsb = [sb(f"xs_{i}", [PT, Tc + 4]) for i in range(IT)]
    halb = sb("halb", [PT, 3 * IT])
    z16 = [sb(f"z16_{i}", [PT, Tc], BF16) for i in range(IT)]
    u16 = [sb(f"u16_{i}", [PT, Tc], BF16) for i in range(IT)]
    delta = sb("delta", [PT, IT * Tc])
    dtraw16 = sb("dtraw16", [64, Tc], BF16)
    B16 = sb("B16", [S, Tc], BF16)
    C16 = sb("C16", [S, Tc], BF16)
    sc_a = sb("sc_a", [PT, Tc])
    sc_b = sb("sc_b", [PT, Tc])
    sc_c = sb("sc_c", [PT, Tc])
    sc_d = sb("sc_d", [PT, Tc])
    hcar = sb("hcar", [PT, IT * S])
    mrow = sb("m_row", [1, Tc])[0:1, :]
    vrow = sb("v_row", [1, Tc])[0:1, :]
    rrow = sb("r_row", [1, Tc])[0:1, :]
    prow = sb("p_row", [1, Tc])[0:1, :]
    m16 = sb("m16r", [1, Tc], BF16)[0:1, :]
    p16 = sb("p16r", [1, Tc], BF16)[0:1, :]
    r16m = sb("r16m", [1, Tc], BF16)[0:1, :]
    xq16 = [sb(f"xq_{i}", [PT, Tc], BF16) for i in range(2)]

    pstat_s = psf("pstat_s", [1, Tc])
    pstat_q = psf("pstat_q", [1, Tc])
    pbc = psf("pbc", [PT, Tc])
    pbc2 = psf("pbc2", [PT, Tc])
    pk = [psf(f"pk_{i}", [PT, Tc]) for i in range(4)]

    p = Prog(nc)

    rt = []
    for k in range(IT):
        rt.append(p.emit("gpsimd", lambda g, k=k: g.dma_start(
            out=opTs[k][:, :], in_=opT[k * PT:(k + 1) * PT, :]), dma=True))
        rt.append(p.emit("gpsimd", lambda g, k=k: g.dma_start(
            out=xpTs[k][:, :], in_=xpT_r[k * PT:(k + 1) * PT, :]), dma=True))
    rt.append(p.emit("gpsimd", lambda g: g.dma_start(
        out=dtpTs[:, :], in_=dtpT[:, :]), dma=True))
    rt.append(p.emit("gpsimd", lambda g: g.dma_start(
        out=ind16[:, :], in_=ind_in[:, :]), dma=True))
    talog = p.emit("sync", lambda s: s.dma_start(out=A_sb[:, :], in_=Alog_c[:, :]), dma=True)
    for dst, src in [(nw, nw_c), (convw, convw_c), (convb, convb_c),
                     (dtb, dtb_c), (Dp, D_c)]:
        rt.append(p.emit("sync", lambda s, dst=dst, src=src: s.dma_start(
            out=dst[:, :], in_=src[:, :]), dma=True))
    rg = p.gate("gpsimd")
    sg0 = p.gate("sync")
    t1 = p.emit("scalar", lambda s: s.activation(A_sb[:, :], A_sb[:, :], AF.Exp),
                deps=[sg0])
    tA = p.emit("vector", lambda v: v.tensor_single_scalar(
        A_sb[:, :], A_sb[:, :], -1.0, ALU.mult), deps=[t1])
    tro = p.emit("vector", lambda v: v.memset(ones_row[:, :], 1.0))
    tco = p.emit("vector", lambda v: v.memset(ones_col[:, :], 1.0))
    tep = p.emit("vector", lambda v: v.memset(epsc[:, :], LN_EPS))
    tep2 = p.emit("vector", lambda v: v.memset(eps2c[:, :], EPS2))
    thc = p.emit("vector", lambda v: v.memset(hcar[:, :], 0.0))
    thal = p.emit("vector", lambda v: v.memset(halb[:, :], 0.0))
    rtok = rg

    bank_free = {}
    ipg_free = [[], []]
    prev_end = []

    for ch in range(NCH):
        ts_ = slice(ch * Tc, (ch + 1) * Tc)
        for i in range(DT):
            p.emit("sync", lambda s, i=i, ts_=ts_: s.dma_start(
                out=x_sb[i][:, :], in_=x_in[i * PT:(i + 1) * PT, ts_]),
                deps=prev_end, dma=True)
        gx = p.gate("sync")
        xt = [gx] * DT
        stat_toks = []
        for i in range(DT):
            c1 = p.emit("vector", lambda v, i=i: v.tensor_copy(
                xq16[0][:, :], x_sb[i][:, :]), deps=[xt[i]] + ([stat_toks[-1]] if stat_toks else []))
            c2 = p.emit("scalar", lambda s, i=i: s.activation(
                xq16[1][:, :], x_sb[i][:, :], AF.Square), deps=[xt[i]] + ([stat_toks[-1]] if stat_toks else []))
            m1 = p.emit("tensor", lambda t, i=i: t.matmul(
                pstat_s[0:1, :], ones_col[:, 0:1], xq16[0][:, :],
                start=(i == 0), stop=(i == DT - 1)),
                deps=[c1, tco] + ([bank_free["pstat_s"]] if i == 0 and "pstat_s" in bank_free else []))
            m2 = p.emit("tensor", lambda t, i=i: t.matmul(
                pstat_q[0:1, :], ones_col[:, 0:1], xq16[1][:, :],
                start=(i == 0), stop=(i == DT - 1)),
                deps=[c2] + ([bank_free["pstat_q"]] if i == 0 and "pstat_q" in bank_free else []))
            stat_toks.append(m2)
        st = stat_toks[-1]
        e2 = p.emit("vector", lambda v: v.tensor_single_scalar(
            mrow, pstat_s[0:1, :], 1.0 / D, ALU.mult), deps=[st])
        bank_free["pstat_s"] = e2
        e3 = p.emit("scalar", lambda s: s.activation(sc_a[0:1, 0:Tc], mrow, AF.Square),
                    deps=[e2])
        e4 = p.emit("vector", lambda v: v.tensor_single_scalar(
            vrow, pstat_q[0:1, :], 1.0 / D, ALU.mult), deps=[st])
        bank_free["pstat_q"] = e4
        e5 = p.emit("vector", lambda v: v.tensor_sub(vrow, vrow, sc_a[0:1, 0:Tc]),
                    deps=[e4, e3])
        e6 = p.emit("scalar", lambda s: s.activation(
            sc_a[0:1, 0:Tc], vrow, AF.Sqrt, bias=epsc[0:1, 0:1]), deps=[e5, tep])
        e7 = p.emit("vector", lambda v: v.reciprocal(rrow, sc_a[0:1, 0:Tc]), deps=[e6])
        e8 = p.emit("scalar", lambda s: s.activation(sc_a[0:1, 0:Tc], rrow, AF.Square),
                    deps=[e7])
        e9 = p.emit("vector", lambda v: v.tensor_mul(prow, vrow, sc_a[0:1, 0:Tc]),
                    deps=[e8, e5])
        e10 = p.emit("scalar", lambda s: s.activation(
            sc_a[0:1, 0:Tc], prow, AF.Sqrt, bias=eps2c[0:1, 0:1]), deps=[e9, tep2])
        e11 = p.emit("vector", lambda v: v.reciprocal(prow, sc_a[0:1, 0:Tc]), deps=[e10])
        e12 = p.emit("vector", lambda v: v.tensor_mul(prow, prow, rrow), deps=[e11, e7])
        c8 = p.emit("vector", lambda v: v.tensor_copy(m16, mrow), deps=[e2])
        c9 = p.emit("vector", lambda v: v.tensor_copy(p16, prow), deps=[e12])
        bc1 = p.emit("tensor", lambda t: t.matmul(
            pbc[:, :], ones_row[0:1, :], m16, start=True, stop=True),
            deps=[c8, tro] + ([bank_free["pbc"]] if "pbc" in bank_free else []))
        bc2 = p.emit("tensor", lambda t: t.matmul(
            pbc2[:, :], ones_row[0:1, :], p16, start=True, stop=True),
            deps=[c9] + ([bank_free["pbc2"]] if "pbc2" in bank_free else []))
        xnt = []
        last_s2 = None
        for i in range(DT):
            s1 = p.emit("vector", lambda v, i=i: v.tensor_sub(
                sc_a[:, :], x_sb[i][:, :], pbc[:, :]),
                deps=[bc1, xt[i]] + ([last_s2] if last_s2 else []))
            s2 = p.emit("vector", lambda v, i=i: v.scalar_tensor_tensor(
                xn16[i][:, :], sc_a[:, :], nw[:, i:i + 1], pbc2[:, :],
                ALU.mult, ALU.mult), deps=[bc2, s1])
            last_s2 = s2
            xnt.append(s2)
        bank_free["pbc"] = xnt[-1]
        bank_free["pbc2"] = xnt[-1]
        if dbg and ch == 0:
            for i in range(DT):
                p.emit("gpsimd", lambda s, i=i: s.dma_start(
                    out=dbg_xn[i * PT:(i + 1) * PT, :], in_=xn16[i][:, :]),
                    deps=[xnt[i]], dma=True)
        xz_ev = []
        for fg in range(8):
            buf = fg % 2
            for k in range(DT):
                p.emit("gpsimd", lambda g, k=k, fg=fg, buf=buf: g.dma_start(
                    out=ipg[buf][k][:, :],
                    in_=ipT[k * PT:(k + 1) * PT, fg * 512:(fg + 1) * 512]),
                    deps=ipg_free[buf], dma=True)
            gfg = p.gate("gpsimd")
            lt = [gfg] * DT
            ipg_free[buf] = []
            for jj in range(4):
                j = fg * 4 + jj
                bank = pk[jj % 2]
                bname = f"pk_{jj % 2}"
                mt = None
                for k in range(DT):
                    mt = p.emit("tensor", lambda t, k=k, jj=jj, buf=buf, bank=bank: t.matmul(
                        bank[:, :], ipg[buf][k][:, jj * PT:(jj + 1) * PT], xn16[k][:, :],
                        start=(k == 0), stop=(k == DT - 1)),
                        deps=[lt[k], xnt[k]] + ([bank_free[bname]] if bname in bank_free and k == 0 else []))
                ipg_free[buf].append(mt)
                if j < IT:
                    ev = p.emit("vector", lambda v, j=j, bank=bank: v.tensor_copy(
                        xsb[j][:, 3:3 + Tc], bank[:, :]), deps=[mt])
                else:
                    ev = p.emit("scalar", lambda s, j=j, bank=bank: s.activation(
                        z16[j - IT][:, :], bank[:, :], AF.Silu), deps=[mt])
                bank_free[bname] = ev
                xz_ev.append(ev)
        conv_toks = []
        for it in range(IT):
            h0 = p.emit("vector", lambda v, it=it: v.tensor_copy(
                xsb[it][:, 0:3], halb[:, it * 3:(it + 1) * 3]),
                deps=[xz_ev[it], thal] + ([conv_toks[-1][0]] if conv_toks else []))
            a1 = p.emit("vector", lambda v, it=it: v.tensor_scalar(
                sc_a[:, :], xsb[it][:, 0:Tc], convw[:, it * 4:it * 4 + 1], None,
                ALU.mult), deps=[h0])
            for kk in range(1, 4):
                a1 = p.emit("vector", lambda v, it=it, kk=kk: v.scalar_tensor_tensor(
                    sc_a[:, :], xsb[it][:, kk:kk + Tc], convw[:, it * 4 + kk:it * 4 + kk + 1],
                    sc_a[:, :], ALU.mult, ALU.add), deps=[a1])
            uu = p.emit("scalar", lambda s, it=it: s.activation(
                u16[it][:, :], sc_a[:, :], AF.Silu, bias=convb[:, it:it + 1]),
                deps=[a1])
            hs = p.emit("vector", lambda v, it=it: v.tensor_copy(
                halb[:, it * 3:(it + 1) * 3], xsb[it][:, Tc:Tc + 3]), deps=[xz_ev[it], h0])
            conv_toks.append((uu, hs))
        if dbg and ch == 0:
            for it in range(IT):
                p.emit("gpsimd", lambda s, it=it: s.dma_start(
                    out=dbg_u[it * PT:(it + 1) * PT, :], in_=u16[it][:, :]),
                    deps=[conv_toks[it][0]], dma=True)
        if dbg and ch == 1:
            for it in range(IT):
                p.emit("gpsimd", lambda s, it=it: s.dma_start(
                    out=dbg_u1[it * PT:(it + 1) * PT, :], in_=u16[it][:, :]),
                    deps=[conv_toks[it][0]], dma=True)
        mt = None
        for k in range(IT):
            mt = p.emit("tensor", lambda t, k=k: t.matmul(
                pk[2][:, :], xpTs[k][:, :], u16[k][:, :],
                start=(k == 0), stop=(k == IT - 1)),
                deps=[conv_toks[k][0], rtok] + ([bank_free["pk_2"]] if "pk_2" in bank_free and k == 0 else []))
        evC = p.emit("vector", lambda v: v.tensor_copy(C16[:, :], pk[2][0:S, :]), deps=[mt])
        evB = p.emit("vector", lambda v: v.tensor_copy(B16[:, :], pk[2][32:32 + S, :]), deps=[mt])
        evd = p.emit("vector", lambda v: v.tensor_copy(dtraw16[:, :], pk[2][64:128, :]), deps=[mt])
        bank_free["pk_2"] = evd
        dphase = []
        for it in range(IT):
            bank = pk[3 - (it % 2)]
            bname = f"pk_{3 - (it % 2)}"
            mt = p.emit("tensor", lambda t, it=it, bank=bank: t.matmul(
                bank[:, :], dtpTs[:, it * PT:(it + 1) * PT], dtraw16[:, :],
                start=True, stop=True),
                deps=[evd, rtok] + ([bank_free[bname]] if bname in bank_free else []))
            r1 = p.emit("vector", lambda v, it=it, bank=bank: v.tensor_scalar(
                sc_a[:, :], bank[:, :], dtb[:, it:it + 1], None, ALU.add),
                deps=[mt] + ([dphase[-1]] if dphase else []))
            bank_free[bname] = r1
            r2 = p.emit("scalar", lambda s: s.activation(sc_b[:, :], sc_a[:, :], AF.Abs),
                        deps=[r1])
            r3 = p.emit("scalar", lambda s: s.activation(sc_b[:, :], sc_b[:, :], AF.Exp,
                                                         scale=-1.0), deps=[r2])
            r4 = p.emit("scalar", lambda s: s.activation(sc_b[:, :], sc_b[:, :], AF.Ln,
                                                         bias=1.0), deps=[r3])
            r5 = p.emit("vector", lambda v: v.tensor_single_scalar(
                sc_a[:, :], sc_a[:, :], 0.0, ALU.max), deps=[r4, r1])
            r6 = p.emit("vector", lambda v, it=it: v.tensor_add(
                delta[:, it * Tc:(it + 1) * Tc], sc_a[:, :], sc_b[:, :]), deps=[r5])
            dphase.append(r6)
        if dbg and ch == 0:
            p.emit("gpsimd", lambda s: s.dma_start(
                out=dbg_B[:, :], in_=B16[:, :]), deps=[evB], dma=True)
            p.emit("gpsimd", lambda s: s.dma_start(
                out=dbg_C[:, :], in_=C16[:, :]), deps=[evC], dma=True)
            p.emit("gpsimd", lambda s: s.dma_start(
                out=dbg_delta[:, :], in_=delta[:, :]), deps=[dphase[-1]], dma=True)
        op_in_toks = []
        prev_scan = dphase[-1]
        bb_tok = {0: None, 1: None}
        for it in range(IT):
            dsl = delta[:, it * Tc:(it + 1) * Tc]
            dut = p.emit("vector", lambda v, it=it, dsl=dsl: v.tensor_mul(
                sc_d[:, :], dsl, u16[it][:, :]), deps=[dphase[it], conv_toks[it][0], prev_scan])
            yac = xsb[it]
            ytok = None
            for s_ in range(S):
                pb = pk[s_ % 2]
                pbn = f"pk_{s_ % 2}"
                pc = pk[2 + s_ % 2]
                pcn = f"pk_{2 + s_ % 2}"
                col = it * S + s_
                tb = p.emit("tensor", lambda t, s_=s_, pb=pb: t.matmul(
                    pb[:, :], ind16[:, s_ * PT:(s_ + 1) * PT], B16[:, :],
                    start=True, stop=True),
                    deps=[evB, rtok] + ([bank_free[pbn]] if pbn in bank_free else []))
                tc = p.emit("tensor", lambda t, s_=s_, pc=pc: t.matmul(
                    pc[:, :], ind16[:, s_ * PT:(s_ + 1) * PT], C16[:, :],
                    start=True, stop=True),
                    deps=[evC] + ([bank_free[pcn]] if pcn in bank_free else []))
                da = p.emit("scalar", lambda s, col=col, dsl=dsl: s.activation(
                    sc_a[:, :], dsl, AF.Exp, scale=A_sb[:, col:col + 1]),
                    deps=[tA] + ([ytok] if ytok else [dut]))
                db = p.emit("vector", lambda v, pb=pb: v.tensor_mul(
                    sc_b[:, :], sc_d[:, :], pb[:, :]), deps=[tb, dut] + ([ytok] if ytok else []))
                bank_free[pbn] = db
                hk = p.emit("vector", lambda v, col=col: v.tensor_tensor_scan(
                    sc_c[:, :], sc_a[:, :], sc_b[:, :], hcar[:, col:col + 1],
                    ALU.mult, ALU.add), deps=[da, db, thc])
                hc = p.emit("vector", lambda v, col=col: v.tensor_copy(
                    hcar[:, col:col + 1], sc_c[:, 511:512]), deps=[hk])
                if s_ == 0:
                    ytok = p.emit("vector", lambda v, yac=yac, pc=pc: v.tensor_mul(
                        yac[:, 0:Tc], sc_c[:, :], pc[:, :]), deps=[hk, tc, conv_toks[it][1], hc])
                else:
                    t1_ = p.emit("vector", lambda v, pc=pc: v.tensor_mul(
                        sc_b[:, :], sc_c[:, :], pc[:, :]), deps=[hk, tc, hc])
                    ytok = p.emit("vector", lambda v, yac=yac: v.tensor_add(
                        yac[:, 0:Tc], yac[:, 0:Tc], sc_b[:, :]), deps=[t1_, ytok])
                bank_free[pcn] = ytok
            q0 = p.emit("vector", lambda v, it=it: v.tensor_copy(
                sc_b[:, :], u16[it][:, :]), deps=[ytok, conv_toks[it][0]])
            q1 = p.emit("vector", lambda v, it=it: v.tensor_scalar(
                sc_a[:, :], sc_b[:, :], Dp[:, it:it + 1], None, ALU.mult),
                deps=[q0])
            q2 = p.emit("vector", lambda v, yac=yac: v.tensor_add(
                yac[:, 0:Tc], yac[:, 0:Tc], sc_a[:, :]), deps=[q1, ytok])
            q3 = p.emit("vector", lambda v, it=it, yac=yac: v.tensor_mul(
                u16[it][:, :], yac[:, 0:Tc], z16[it][:, :]), deps=[q2])
            if dbg and ch == 0:
                p.emit("gpsimd", lambda s, it=it: s.dma_start(
                    out=dbg_y[it * PT:(it + 1) * PT, :], in_=u16[it][:, :]),
                    deps=[q3], dma=True)
            op_in_toks.append(q3)
            prev_scan = q3
        if dbg and ch == 1:
            p.emit("gpsimd", lambda s: s.dma_start(
                out=dbg_hcar[:, :], in_=hcar[:, :]), deps=[op_in_toks[-1]], dma=True)
            p.emit("gpsimd", lambda s: s.dma_start(
                out=dbg_d1[:, :], in_=delta[:, :]), deps=[dphase[-1]], dma=True)
        rb2 = p.emit("vector", lambda v: v.tensor_copy(r16m, rrow), deps=[prev_scan])
        rbc1 = p.emit("tensor", lambda t: t.matmul(
            pbc[:, :], ones_row[0:1, :], m16, start=True, stop=True),
            deps=[rb2, bank_free["pbc"]] if "pbc" in bank_free else [rb2])
        rbc2 = p.emit("tensor", lambda t: t.matmul(
            pbc2[:, :], ones_row[0:1, :], r16m, start=True, stop=True),
            deps=[rb2, bank_free["pbc2"]] if "pbc2" in bank_free else [rb2])
        for i in range(DT):
            bank = pk[i % 2]
            bname = f"pk_{i % 2}"
            mt = None
            for k in range(IT):
                mt = p.emit("tensor", lambda t, i=i, k=k, bank=bank: t.matmul(
                    bank[:, :], opTs[k][:, i * PT:(i + 1) * PT], u16[k][:, :],
                    start=(k == 0), stop=(k == IT - 1)),
                    deps=[op_in_toks[k], rtok] + ([bank_free[bname]] if bname in bank_free and k == 0 else []))
            l1 = p.emit("vector", lambda v, i=i: v.tensor_sub(
                sc_a[:, :], x_sb[i][:, :], pbc[:, :]),
                deps=[rbc1, prev_scan] + ([bank_free["lnr"]] if "lnr" in bank_free else []))
            l2 = p.emit("vector", lambda v, i=i: v.tensor_mul(
                sc_a[:, :], sc_a[:, :], pbc2[:, :]), deps=[rbc2, l1])
            o1 = p.emit("vector", lambda v, i=i, bank=bank: v.scalar_tensor_tensor(
                x_sb[i][:, :], bank[:, :], 1.0, x_sb[i][:, :], ALU.mult, ALU.add),
                deps=[mt, l2])
            o2 = p.emit("vector", lambda v, i=i: v.tensor_add(
                x_sb[i][:, :], x_sb[i][:, :], sc_a[:, :]), deps=[o1])
            bank_free[bname] = o1
            bank_free["lnr"] = o2
            p.emit("gpsimd", lambda s, i=i, ts_=ts_: s.dma_start(
                out=x_out[i * PT:(i + 1) * PT, ts_], in_=x_sb[i][:, :]),
                deps=[o2], dma=True)
            if i == DT - 1:
                prev_end = [p.gate("gpsimd")]
        bank_free["pbc"] = bank_free["lnr"]
        bank_free["pbc2"] = bank_free["lnr"]

    toks = p.barrier_tokens()
    p.emit("sync", lambda s: s.nop(), deps=toks)
    with nc.Block() as block:
        p.finalize(block)
    return nc


def _prep_mamba1_inputs(xcore, norm_w, in_proj_w, conv_w, conv_b, x_proj_w,
                        dt_proj_w, dt_proj_b, A_log, D_param, out_proj_w):
    DI, S, DTR = 2048, 16, 64
    xp = np.asarray(x_proj_w)
    xpT_r = np.zeros((DI, 128), np.float32)
    xpT_r[:, 0:16] = xp[80:96].T
    xpT_r[:, 32:48] = xp[64:80].T
    xpT_r[:, 64:128] = xp[0:64].T
    ind = np.zeros((S, S * 128), np.float32)
    for s in range(S):
        ind[s, s * 128:(s + 1) * 128] = 1.0
    return {
        "x_in": np.ascontiguousarray(xcore),
        "nw_c": np.ascontiguousarray(np.asarray(norm_w).reshape(8, 128).T),
        "ipT": np.ascontiguousarray(np.asarray(in_proj_w).T),
        "convw_c": np.ascontiguousarray(
            np.asarray(conv_w).reshape(16, 128, 4).transpose(1, 0, 2).reshape(128, 64)),
        "convb_c": np.ascontiguousarray(np.asarray(conv_b).reshape(16, 128).T),
        "xpT_r": xpT_r,
        "dtpT": np.ascontiguousarray(np.asarray(dt_proj_w).T),
        "dtb_c": np.ascontiguousarray(np.asarray(dt_proj_b).reshape(16, 128).T),
        "Alog_c": np.ascontiguousarray(
            np.asarray(A_log).reshape(16, 128, 16).transpose(1, 0, 2).reshape(128, 256)),
        "D_c": np.ascontiguousarray(np.asarray(D_param).reshape(16, 128).T),
        "opT": np.ascontiguousarray(np.asarray(out_proj_w).T),
        "ind_in": ind,
    }


def build_moe_stats():
    TB = 512
    NB = T // TB

    nc = bass.Bass("TRN2", target_bir_lowering=False)
    x_in = nc.dram_tensor("x_in", [D, T], F32, kind="ExternalInput")
    waffT = nc.dram_tensor("waffT", [D, 8], F32, kind="ExternalInput")
    xn_d = nc.dram_tensor("xn_d", [D, T], BF16, kind="ExternalOutput")
    aff_d = nc.dram_tensor("aff_d", [8, T], F32, kind="ExternalOutput")
    asum_d = nc.dram_tensor("asum_d", [1, 1], F32, kind="ExternalOutput")

    es = nc.ctx
    sb = lambda n, s, d=F32: es.enter_context(nc.sbuf_tensor(n, s, d))
    psf = lambda n, s, d=F32: es.enter_context(nc.psum_tensor(n, s, d))

    waf = [sb(f"waf_{k}", [PT, 8]) for k in range(DT)]
    wsq = sb("wsq", [PT, 8])
    wn8 = sb("wn8", [8, 1])
    ones_col = sb("ones_colS", [PT, 1])
    ones_row = sb("ones_rowS", [1, PT])
    eps_col = sb("eps_colS", [PT, 1])
    x_sb = [sb(f"x_{i}", [PT, TB]) for i in range(DT)]
    xsq = sb("xsq", [PT, TB])
    xnf = sb("xnf", [PT, TB])
    xn16 = [sb(f"xn_{i}", [PT, TB], BF16) for i in range(DT)]
    mrow = sb("m_row", [1, TB])[0:1, :]
    vrow = sb("v_row", [1, TB])[0:1, :]
    rrow = sb("r_row", [1, TB])[0:1, :]
    xnr = sb("xnr_row", [1, TB])[0:1, :]
    srow = sb("s_row", [1, TB])[0:1, :]
    aff_sb = sb("aff_sb", [8, TB])
    den_sb = sb("den_sb", [8, TB])
    asum8 = sb("asum8", [8, 1])
    acc8 = sb("acc8", [8, 1])
    ones8f = sb("ones8f", [8, 1])
    one1 = sb("one1", [1, 1])

    pstat_s = psf("pstat_s", [1, TB])
    pstat_q = psf("pstat_q", [1, TB])
    pbc = psf("pbc", [PT, TB])
    pbc2 = psf("pbc2", [PT, TB])
    pnum = psf("pnum", [8, TB])
    pden = psf("pden", [8, TB])
    pwn = psf("pwn", [8, 1])
    ptot = psf("ptot", [1, 1])

    p = Prog(nc)
    for k in range(DT):
        p.emit("sync", lambda s, k=k: s.dma_start(
            out=waf[k][:, :], in_=waffT[k * PT:(k + 1) * PT, :]), dma=True)
    gw = p.gate("sync")
    wts = [gw] * DT
    tco = p.emit("vector", lambda v: v.memset(ones_col[:, :], 1.0))
    tro = p.emit("vector", lambda v: v.memset(ones_row[:, :], 1.0))
    t8f = p.emit("vector", lambda v: v.memset(ones8f[:, :], 1.0))
    tep = p.emit("vector", lambda v: v.memset(eps_col[:, :], LN_EPS))
    tz8 = p.emit("vector", lambda v: v.memset(acc8[:, :], 0.0))
    mt = None
    for k in range(DT):
        tq = p.emit("scalar", lambda s, k=k: s.activation(
            wsq[:, :], waf[k][:, :], AF.Square), deps=[wts[k]] + ([mt] if mt else []))
        mt = p.emit("tensor", lambda t, k=k: t.matmul(
            pwn[:, 0:1], wsq[:, :], ones_col[:, 0:1],
            start=(k == 0), stop=(k == DT - 1)), deps=[tq, tco])
    tw1 = p.emit("scalar", lambda s: s.activation(wn8[:, :], pwn[:, 0:1], AF.Sqrt),
                 deps=[mt])

    prev_end = []
    asum_toks = []
    prev_stat_s = prev_stat_q = None
    xn_toks = []
    afft = dent = None
    for blk in range(NB):
        ts_ = slice(blk * TB, (blk + 1) * TB)
        for i in range(DT):
            p.emit("sync", lambda s, i=i, ts_=ts_: s.dma_start(
                out=x_sb[i][:, :], in_=x_in[i * PT:(i + 1) * PT, ts_]),
                deps=prev_end, dma=True)
        gx = p.gate("sync")
        xt = [gx] * DT
        prev_end = []
        stat_toks = []
        for i in range(DT):
            c2 = p.emit("scalar", lambda s, i=i: s.activation(
                xsq[:, :], x_sb[i][:, :], AF.Square),
                deps=[xt[i]] + ([stat_toks[-1]] if stat_toks else []))
            m1 = p.emit("tensor", lambda t, i=i: t.matmul(
                pstat_s[0:1, :], x_sb[i][:, 0:1].bitcast(F32) if False else ones_col[:, 0:1],
                x_sb[i][:, :], start=(i == 0), stop=(i == DT - 1)),
                deps=[xt[i], tco] + ([prev_stat_s] if i == 0 and blk > 0 else []))
            m2 = p.emit("tensor", lambda t, i=i: t.matmul(
                pstat_q[0:1, :], ones_col[:, 0:1], xsq[:, :],
                start=(i == 0), stop=(i == DT - 1)),
                deps=[c2] + ([prev_stat_q] if i == 0 and blk > 0 else []))
            stat_toks.append(m2)
        st = stat_toks[-1]
        e2 = p.emit("vector", lambda v: v.tensor_single_scalar(
            mrow, pstat_s[0:1, :], 1.0 / D, ALU.mult), deps=[st])
        prev_stat_s = e2
        e3 = p.emit("scalar", lambda s: s.activation(srow, mrow, AF.Square), deps=[e2])
        e4 = p.emit("vector", lambda v: v.tensor_single_scalar(
            vrow, pstat_q[0:1, :], 1.0 / D, ALU.mult), deps=[st])
        prev_stat_q = e4
        e5 = p.emit("vector", lambda v: v.tensor_sub(vrow, vrow, srow), deps=[e4, e3])
        e6 = p.emit("scalar", lambda s: s.activation(
            srow, vrow, AF.Sqrt, bias=eps_col[0:1, 0:1]), deps=[e5, tep])
        e7 = p.emit("vector", lambda v: v.reciprocal(rrow, srow), deps=[e6])
        e8 = p.emit("scalar", lambda s: s.activation(srow, rrow, AF.Square), deps=[e7])
        e9 = p.emit("vector", lambda v: v.tensor_mul(xnr, vrow, srow), deps=[e8, e5])
        e10 = p.emit("vector", lambda v: v.tensor_single_scalar(
            xnr, xnr, float(D), ALU.mult), deps=[e9])
        e11 = p.emit("scalar", lambda s: s.activation(xnr, xnr, AF.Sqrt), deps=[e10])
        bc1 = p.emit("tensor", lambda t: t.matmul(
            pbc[:, :], ones_row[0:1, :], mrow, start=True, stop=True),
            deps=[e2, tro] + ([xn_toks[-1]] if blk > 0 else []))
        bc2 = p.emit("tensor", lambda t: t.matmul(
            pbc2[:, :], ones_row[0:1, :], rrow, start=True, stop=True),
            deps=[e7] + ([xn_toks[-1]] if blk > 0 else []))
        bc3 = p.emit("tensor", lambda t: t.matmul(
            pden[0:8, :], ones_row[0:1, 0:8], xnr, start=True, stop=True),
            deps=[e11] + ([dent] if blk > 0 else []))
        xn_toks = []
        numt = None
        for i in range(DT):
            s1 = p.emit("vector", lambda v, i=i: v.tensor_sub(
                x_sb[i][:, :], x_sb[i][:, :], pbc[:, :]), deps=[bc1, xt[i]])
            s2 = p.emit("vector", lambda v, i=i: v.tensor_mul(
                xnf[:, :], x_sb[i][:, :], pbc2[:, :]),
                deps=[bc2, s1] + ([numt] if numt else []))
            s3 = p.emit("vector", lambda v, i=i: v.tensor_copy(
                xn16[i][:, :], xnf[:, :]), deps=[s2])
            xn_toks.append(s3)
            p.emit("gpsimd", lambda g, i=i, ts_=ts_: g.dma_start(
                out=xn_d[i * PT:(i + 1) * PT, ts_], in_=xn16[i][:, :]),
                deps=[s3], dma=True)
            numt = p.emit("tensor", lambda t, i=i: t.matmul(
                pnum[0:8, :], waf[i][:, :], xnf[:, :],
                start=(i == 0), stop=(i == DT - 1)),
                deps=[s2, wts[i]] + ([afft] if blk > 0 and i == 0 else []))
        d1 = p.emit("vector", lambda v: v.tensor_scalar(
            den_sb[0:8, :], pden[0:8, :], wn8[:, 0:1], None, ALU.mult),
            deps=[bc3, tw1])
        dent = d1
        d2 = p.emit("vector", lambda v: v.tensor_single_scalar(
            den_sb[0:8, :], den_sb[0:8, :], 1e-9, ALU.add), deps=[d1])
        d3 = p.emit("vector", lambda v: v.reciprocal(den_sb[0:8, :], den_sb[0:8, :]),
                    deps=[d2])
        a1 = p.emit("vector", lambda v: v.tensor_mul(
            aff_sb[0:8, :], pnum[0:8, :], den_sb[0:8, :]), deps=[numt, d3])
        afft = a1
        p.emit("sync", lambda s, ts_=ts_: s.dma_start(
            out=aff_d[:, ts_], in_=aff_sb[0:8, :]), deps=[a1], dma=True)
        prev_end = [p.gate("gpsimd"), p.gate("sync")]
        a2 = p.emit("vector", lambda v: v.tensor_scalar(
            den_sb[0:8, :], aff_sb[0:8, :], 0.0, 0.0, ALU.add, ALU.add,
            accum_out=asum8[:, 0:1]), deps=[a1, d3])
        a3 = p.emit("vector", lambda v: v.tensor_add(
            acc8[:, :], acc8[:, :], asum8[:, 0:1]), deps=[a2, tz8])
        asum_toks.append(a3)
    tt = p.emit("tensor", lambda t: t.matmul(
        ptot[0:1, 0:1], acc8[:, 0:1], ones8f[:, 0:1], start=True, stop=True),
        deps=[asum_toks[-1], t8f])
    tv = p.emit("vector", lambda v: v.tensor_copy(one1[0:1, 0:1], ptot[0:1, 0:1]),
                deps=[tt])
    p.emit("gpsimd", lambda g: g.dma_start(out=asum_d[0:1, 0:1], in_=one1[0:1, 0:1]),
           deps=[tv], dma=True)

    toks = p.barrier_tokens()
    p.emit("sync", lambda s: s.nop(), deps=toks)
    with nc.Block() as block:
        p.finalize(block)
    return nc


def build_moe_main(dbg=False):
    E = 8
    FH = 1024
    Tb = 512
    NB = T // Tb
    TT_ = T // PT
    SCAP = 2048.0
    MINC = 4.0

    nc = bass.Bass("TRN2", target_bir_lowering=False)
    x_in = nc.dram_tensor("x_in", [D, T], F32, kind="ExternalInput")
    xn_din = nc.dram_tensor("xn_din", [D, T], BF16, kind="ExternalInput")
    aff_din = nc.dram_tensor("aff_din", [E, T], F32, kind="ExternalInput")
    parts = nc.dram_tensor("parts", [E, E], F32, kind="ExternalInput")
    thr_in = nc.dram_tensor("thr_in", [E, 1], F32, kind="ExternalInput")
    W1 = nc.dram_tensor("W1", [E, D, 4096], F32, kind="ExternalInput")
    W2 = nc.dram_tensor("W2", [E, 4096, D], F32, kind="ExternalInput")
    b1c = nc.dram_tensor("b1c", [PT, E * 32], F32, kind="ExternalInput")
    b2r = nc.dram_tensor("b2r", [E, D], F32, kind="ExternalInput")
    iota8_in = nc.dram_tensor("iota8_in", [PT, E], F32, kind="ExternalInput")
    iotaB_in = nc.dram_tensor("iotaB_in", [PT, E], F32, kind="ExternalInput")
    ident_in = nc.dram_tensor("ident_in", [PT, PT], F32, kind="ExternalInput")
    x_out = nc.dram_tensor("x_out", [D, T], F32, kind="ExternalOutput")
    if dbg:
        dbg_rows = nc.dram_tensor("dbg_rows", [8, 8], F32, kind="ExternalOutput")
        dbg_mtok = nc.dram_tensor("dbg_mtok", [PT, 256], F32, kind="ExternalOutput")
        dbg_acc = nc.dram_tensor("dbg_acc", [PT, D], F32, kind="ExternalOutput")
        dbg_csum = nc.dram_tensor("dbg_csum", [1, D], F32, kind="ExternalOutput")

    from contextlib import ExitStack
    es = nc.ctx
    sb = lambda n, s, d=F32: es.enter_context(nc.sbuf_tensor(n, s, d))
    psf = lambda n, s, d=F32: es.enter_context(nc.psum_tensor(n, s, d))

    xn16 = [sb(f"xn_{i}", [PT, T], BF16) for i in range(DT)]
    acc16 = [sb(f"acc_{tt}", [PT, D], BF16) for tt in range(TT_)]
    m_tok = sb("m_tok", [PT, TT_ * E])
    b1s = sb("b1s", [PT, E * 32])
    g1all = sb("g1all", [PT, E * 32], BF16)
    csum = sb("csum", [1, D])[0:1, :]
    crow = sb("crow", [1, D])[0:1, :]
    negc16 = sb("negc16", [1, D], BF16)[0:1, :]
    ones_row = sb("ones_rowM", [1, PT], BF16)
    ident16 = sb("identM", [PT, PT])
    rows8 = {}
    for nm in ["cap8", "lo8", "hi8", "mid8", "cnt8", "ge8", "t8a", "t8b",
               "rem8", "gt8"]:
        rows8[nm] = sb(nm, [8, 1])
    parts_sb = sb("parts_sb", [E, E])
    thr8 = sb("thr8", [E, 1])
    tot8 = sb("tot8", [E, 1])
    cri8 = sb("cri8", [E, 1])

    psA = psf("psA", [PT, Tb])
    psB = psf("psB", [PT, Tb])
    psC = psf("psC", [PT, Tb])
    psD = psf("psD", [PT, Tb])
    psE = psf("psE", [PT, Tb])
    psF = psf("psF", [PT, Tb])
    psR = psf("psR", [1, 512])
    psT = psf("psT", [PT, PT])

    p = Prog(nc)

    for i in range(DT):
        p.emit("sync", lambda s, i=i: s.dma_start(
            out=xn16[i][:, :], in_=xn_din[i * PT:(i + 1) * PT, :]), dma=True)
    p.emit("sync", lambda s: s.dma_start(out=b1s[:, :], in_=b1c[:, :]), dma=True)
    p.emit("sync", lambda s: s.dma_start(out=ident16[:, :], in_=ident_in[:, :]),
           dma=True)
    p.emit("sync", lambda s: s.dma_start(out=parts_sb[:, :], in_=parts[:, :]),
           dma=True)
    p.emit("sync", lambda s: s.dma_start(out=thr8[:, :], in_=thr_in[:, :]),
           dma=True)
    g0 = p.gate("sync")
    xnt = [g0] * DT
    tb1 = tid = tpt = tth = g0
    tg1 = p.emit("scalar", lambda s: s.activation(g1all[:, :], b1s[:, :], AF.Gelu),
                 deps=[tb1])
    tro = p.emit("vector", lambda v: v.memset(ones_row[:, :], 1.0))
    tcz = p.emit("vector", lambda v: v.memset(csum, 0.0))

    rs = ExitStack()
    aff_sb = rs.enter_context(nc.sbuf_tensor("aff_sb", [E, T], F32))
    eqm = rs.enter_context(nc.sbuf_tensor("eqm", [E, T], F32))
    pfx = rs.enter_context(nc.sbuf_tensor("pfx", [E, T], F32))
    gtm = rs.enter_context(nc.sbuf_tensor("gtm", [E, T], F32))
    affT = rs.enter_context(nc.sbuf_tensor("affT", [PT, TT_ * E], F32))
    ecrT = rs.enter_context(nc.sbuf_tensor("ecrT", [PT, TT_ * E], F32))
    io8 = rs.enter_context(nc.sbuf_tensor("io8", [PT, E], F32))
    ioB = rs.enter_context(nc.sbuf_tensor("ioB", [PT, E], F32))
    tcv = rs.enter_context(nc.sbuf_tensor("tcv", [PT, E], F32))
    mx1 = rs.enter_context(nc.sbuf_tensor("mx1", [PT, 1], F32))

    p.emit("sync", lambda s: s.dma_start(out=aff_sb[:, :], in_=aff_din[:, :]),
           dma=True)
    p.emit("sync", lambda s: s.dma_start(out=io8[:, :], in_=iota8_in[:, :]),
           dma=True)
    p.emit("sync", lambda s: s.dma_start(out=ioB[:, :], in_=iotaB_in[:, :]),
           dma=True)
    g1r = p.gate("sync")
    ta = ti1 = ti2 = g1r
    s1 = p.emit("vector", lambda v: v.tensor_reduce(
        tot8[:, 0:1], parts_sb[:, :], AX.X, ALU.add), deps=[tpt])
    s2 = p.emit("vector", lambda v: v.tensor_single_scalar(
        tot8[:, 0:1], tot8[:, 0:1], 1.0 / (NCORES * T * E), ALU.mult), deps=[s1])
    s3 = p.emit("vector", lambda v: v.tensor_sub(
        tot8[:, 0:1], tot8[:, 0:1], thr8[:, 0:1]), deps=[s2, tth])
    s4 = p.emit("scalar", lambda s: s.activation(
        tot8[:, 0:1], tot8[:, 0:1], AF.Sigmoid), deps=[s3])
    s5 = p.emit("vector", lambda v: v.tensor_single_scalar(
        tot8[:, 0:1], tot8[:, 0:1], SCAP, ALU.mult), deps=[s4])
    s6 = p.emit("vector", lambda v: v.tensor_copy(
        thr8[:, 0:1].bitcast(mybir.dt.int32), tot8[:, 0:1]), deps=[s5, s3])
    s7 = p.emit("vector", lambda v: v.tensor_copy(
        cri8[:, 0:1], thr8[:, 0:1].bitcast(mybir.dt.int32)), deps=[s6])
    s8 = p.emit("vector", lambda v: v.tensor_tensor(
        thr8[:, 0:1], cri8[:, 0:1], tot8[:, 0:1], ALU.is_gt), deps=[s7, s5])
    s9 = p.emit("vector", lambda v: v.tensor_sub(
        rows8["cap8"][:, 0:1], cri8[:, 0:1], thr8[:, 0:1]), deps=[s8])
    s10 = p.emit("vector", lambda v: v.tensor_single_scalar(
        rows8["cap8"][:, 0:1], rows8["cap8"][:, 0:1], MINC, ALU.max), deps=[s9])
    cp8 = p.emit("vector", lambda v: v.tensor_single_scalar(
        rows8["cap8"][:, 0:1], rows8["cap8"][:, 0:1], SCAP, ALU.min), deps=[s10])
    l0 = p.emit("vector", lambda v: v.tensor_reduce(
        rows8["lo8"][:, 0:1], aff_sb[:, :], AX.X, ALU.min), deps=[ta])
    l1 = p.emit("vector", lambda v: v.tensor_single_scalar(
        rows8["lo8"][:, 0:1], rows8["lo8"][:, 0:1], 1e-3, ALU.subtract), deps=[l0])
    h1 = p.emit("vector", lambda v: v.tensor_reduce(
        rows8["hi8"][:, 0:1], aff_sb[:, :], AX.X, ALU.max), deps=[ta, l0])
    last = h1
    for _ in range(34):
        b1_ = p.emit("vector", lambda v: v.tensor_add(
            rows8["mid8"][:, 0:1], rows8["lo8"][:, 0:1], rows8["hi8"][:, 0:1]),
            deps=[last])
        b2_ = p.emit("vector", lambda v: v.tensor_single_scalar(
            rows8["mid8"][:, 0:1], rows8["mid8"][:, 0:1], 0.5, ALU.mult), deps=[b1_])
        b3_ = p.emit("vector", lambda v: v.tensor_scalar(
            eqm[:, :], aff_sb[:, :], rows8["mid8"][:, 0:1], 0.0,
            ALU.is_gt, ALU.add, accum_out=rows8["cnt8"][:, 0:1]), deps=[b2_, ta])
        b4_ = p.emit("vector", lambda v: v.tensor_tensor(
            rows8["ge8"][:, 0:1], rows8["cnt8"][:, 0:1], rows8["cap8"][:, 0:1],
            ALU.is_ge), deps=[b3_, cp8])
        b5_ = p.emit("vector", lambda v: v.tensor_sub(
            rows8["t8a"][:, 0:1], rows8["mid8"][:, 0:1], rows8["lo8"][:, 0:1]),
            deps=[b4_])
        b6_ = p.emit("vector", lambda v: v.tensor_mul(
            rows8["t8a"][:, 0:1], rows8["t8a"][:, 0:1], rows8["ge8"][:, 0:1]),
            deps=[b5_])
        b7_ = p.emit("vector", lambda v: v.tensor_add(
            rows8["lo8"][:, 0:1], rows8["lo8"][:, 0:1], rows8["t8a"][:, 0:1]),
            deps=[b6_])
        b8_ = p.emit("vector", lambda v: v.tensor_sub(
            rows8["t8b"][:, 0:1], rows8["hi8"][:, 0:1], rows8["mid8"][:, 0:1]),
            deps=[b4_])
        b9_ = p.emit("vector", lambda v: v.tensor_mul(
            rows8["t8b"][:, 0:1], rows8["t8b"][:, 0:1], rows8["ge8"][:, 0:1]),
            deps=[b8_])
        b10_ = p.emit("vector", lambda v: v.tensor_add(
            rows8["hi8"][:, 0:1], rows8["mid8"][:, 0:1], rows8["t8b"][:, 0:1]),
            deps=[b9_, b7_])
        last = b10_
    g1_ = p.emit("vector", lambda v: v.tensor_scalar(
        gtm[:, :], aff_sb[:, :], rows8["hi8"][:, 0:1], 0.0,
        ALU.is_gt, ALU.add, accum_out=rows8["gt8"][:, 0:1]), deps=[last])
    g2_ = p.emit("vector", lambda v: v.tensor_sub(
        rows8["rem8"][:, 0:1], rows8["cap8"][:, 0:1], rows8["gt8"][:, 0:1]),
        deps=[g1_])
    g3_ = p.emit("vector", lambda v: v.tensor_scalar(
        eqm[:, :], aff_sb[:, :], rows8["hi8"][:, 0:1], None,
        ALU.is_equal), deps=[g1_])
    g4_ = p.emit("vector", lambda v: v.tensor_tensor_scan(
        pfx[:, :], eqm[:, :], eqm[:, :], 0.0, ALU.add, ALU.bypass), deps=[g3_])
    g5_ = p.emit("vector", lambda v: v.tensor_scalar(
        pfx[:, :], pfx[:, :], rows8["rem8"][:, 0:1], None, ALU.is_le), deps=[g4_, g2_])
    g6_ = p.emit("vector", lambda v: v.tensor_mul(pfx[:, :], pfx[:, :], eqm[:, :]),
                 deps=[g5_])
    g7_ = p.emit("vector", lambda v: v.tensor_add(gtm[:, :], gtm[:, :], pfx[:, :]),
                 deps=[g6_])
    mtoks = []
    for tt in range(TT_):
        t1_ = p.emit("tensor", lambda t, tt=tt: t.transpose(
            psT[:, 0:8], aff_sb[:, tt * PT:(tt + 1) * PT], ident16[0:8, 0:8]),
            deps=[ta, tid] + ([mtoks[-1]] if mtoks else []))
        t2_ = p.emit("vector", lambda v, tt=tt: v.tensor_copy(
            affT[:, tt * E:(tt + 1) * E], psT[:, 0:8]), deps=[t1_])
        t3_ = p.emit("tensor", lambda t, tt=tt: t.transpose(
            psT[:, 8:16], gtm[:, tt * PT:(tt + 1) * PT], ident16[0:8, 0:8]),
            deps=[g7_, t2_])
        t4_ = p.emit("vector", lambda v, tt=tt: v.tensor_copy(
            ecrT[:, tt * E:(tt + 1) * E], psT[:, 8:16]), deps=[t3_])
        t5_ = p.emit("vector", lambda v, tt=tt: v.tensor_reduce(
            mx1[:, 0:1], affT[:, tt * E:(tt + 1) * E], AX.X, ALU.max), deps=[t2_])
        t6_ = p.emit("vector", lambda v, tt=tt: v.tensor_scalar(
            tcv[:, :], affT[:, tt * E:(tt + 1) * E], mx1[:, 0:1], None,
            ALU.is_ge), deps=[t5_])
        t7_ = p.emit("vector", lambda v: v.tensor_mul(tcv[:, :], tcv[:, :], ioB[:, :]),
                     deps=[t6_, ti2])
        t8_ = p.emit("vector", lambda v: v.tensor_single_scalar(
            tcv[:, :], tcv[:, :], 999.0, ALU.add), deps=[t7_])
        t9_ = p.emit("vector", lambda v: v.tensor_reduce(
            mx1[:, 0:1], tcv[:, :], AX.X, ALU.min), deps=[t8_])
        t10_ = p.emit("vector", lambda v, tt=tt: v.tensor_scalar(
            tcv[:, :], io8[:, :], mx1[:, 0:1], None, ALU.is_equal),
            deps=[t9_, ti1])
        t11_ = p.emit("vector", lambda v, tt=tt: v.tensor_mul(
            m_tok[:, tt * E:(tt + 1) * E], tcv[:, :], ecrT[:, tt * E:(tt + 1) * E]),
            deps=[t10_, t4_])
        mtoks.append(t11_)
    route_done = mtoks[-1]
    if dbg:
        for j, nm in enumerate(["cap8", "hi8", "gt8", "rem8", "cnt8", "lo8"]):
            p.emit("gpsimd", lambda g, j=j, nm=nm: g.dma_start(
                out=dbg_rows[:, j:j + 1], in_=rows8[nm][:, 0:1],
                allow_slow_non_contiguous=True),
                deps=[route_done], dma=True)
        p.emit("gpsimd", lambda g: g.dma_start(
            out=dbg_mtok[:, :], in_=m_tok[:, :]), deps=[route_done], dma=True)
    rs.close()

    es2 = ExitStack()
    w1b = [es2.enter_context(nc.sbuf_tensor(f"w1b_{k}", [PT, FH], BF16)) for k in range(DT)]
    w2b = [es2.enter_context(nc.sbuf_tensor(f"w2b_{k}", [PT, D], BF16)) for k in range(8)]
    H16 = [es2.enter_context(nc.sbuf_tensor(f"H_{j}", [PT, Tb], BF16)) for j in range(8)]
    acc_init = []
    for tt in range(TT_):
        acc_init.append(p.emit("vector", lambda v, tt=tt: v.memset(
            acc16[tt][:, :], 0.0)))
    bankA = {"psA": None, "psB": None}
    bankG = {}
    w1_free, w2_free = [route_done], [route_done]
    csum_tok = tcz
    negc_tok = None
    for e in range(E):
        for fh in range(4):
            for k in range(DT):
                p.emit("gpsimd", lambda g, k=k, e=e, fh=fh: g.dma_start(
                    out=w1b[k][:, :], in_=W1[e, k * PT:(k + 1) * PT,
                                             fh * FH:(fh + 1) * FH]),
                    deps=w1_free, dma=True)
            for k in range(8):
                p.emit("gpsimd", lambda g, k=k, e=e, fh=fh: g.dma_start(
                    out=w2b[k][:, :], in_=W2[e, fh * FH + k * PT: fh * FH + (k + 1) * PT, :]),
                    deps=w2_free, dma=True)
            gwt = p.gate("gpsimd")
            wt1 = [gwt] * DT
            wt2 = [gwt] * 8
            w1_free, w2_free = [], []
            cv = None
            for dblk in range(2):
                mt = None
                for k in range(8):
                    mt = p.emit("tensor", lambda t, e=e, fh=fh, k=k, dblk=dblk: t.matmul(
                        psR[0:1, :],
                        g1all[:, e * 32 + fh * 8 + k:e * 32 + fh * 8 + k + 1],
                        w2b[k][:, dblk * 512:(dblk + 1) * 512],
                        start=(k == 0), stop=(k == 7)),
                        deps=[wt2[k], tg1] + ([csum_tok if dblk == 0 else cv]
                                              if k == 0 else []))
                cv = p.emit("vector", lambda v, dblk=dblk: v.tensor_copy(
                    crow[0:1, dblk * 512:(dblk + 1) * 512], psR[0:1, :]), deps=[mt])
            csum_tok = p.emit("vector", lambda v: v.tensor_add(csum, csum, crow),
                              deps=[cv, csum_tok])
            ng = p.emit("vector", lambda v: v.tensor_single_scalar(
                negc16, crow, -1.0, ALU.mult), deps=[cv] + ([negc_tok] if negc_tok else []))
            negc_tok = ng
            mm2_done = None
            for tb in range(NB):
                tsl = slice(tb * Tb, (tb + 1) * Tb)
                htoks = []
                for j in range(8):
                    bank = psA if j % 2 == 0 else psB
                    bname = "psA" if j % 2 == 0 else "psB"
                    mt = None
                    for k in range(DT):
                        mt = p.emit("tensor", lambda t, j=j, k=k, e=e, fh=fh, bank=bank, tsl=tsl: t.matmul(
                            bank[:, :], w1b[k][:, j * PT:(j + 1) * PT],
                            xn16[k][:, tsl], start=(k == 0), stop=(k == DT - 1)),
                            deps=[wt1[k], xnt[k]] +
                            ([bankA[bname]] if bankA[bname] and k == 0 else []))
                    ht = p.emit("scalar", lambda s, j=j, e=e, fh=fh, bank=bank: s.activation(
                        H16[j][:, :], bank[:, :], AF.Gelu,
                        bias=b1s[:, e * 32 + fh * 8 + j:e * 32 + fh * 8 + j + 1]),
                        deps=[mt, tb1] + ([mm2_done] if mm2_done else []))
                    bankA[bname] = ht
                    htoks.append(ht)
                w1_free.extend(htoks[-2:])
                for tt in range(4):
                    gtt = tb * 4 + tt
                    for dh in range(2):
                        gname = f"b{(tt * 2 + dh) % 4}"
                        bank = [psC, psD, psE, psF][(tt * 2 + dh) % 4]
                        mt = None
                        for k in range(8):
                            mt = p.emit("tensor", lambda t, k=k, tt=tt, dh=dh, bank=bank: t.matmul(
                                bank[:, :], H16[k][:, tt * PT:(tt + 1) * PT],
                                w2b[k][:, dh * 512:(dh + 1) * 512],
                                start=(k == 0), stop=False),
                                deps=[htoks[k], wt2[k]] +
                                ([bankG[gname]] if bankG.get(gname) and k == 0 else []))
                        mt = p.emit("tensor", lambda t, dh=dh, bank=bank: t.matmul(
                            bank[:, :], ones_row[0:1, :],
                            negc16[0:1, dh * 512:(dh + 1) * 512],
                            start=False, stop=True), deps=[ng, tro])
                        au = p.emit("vector", lambda v, gtt=gtt, dh=dh, e=e, bank=bank: v.scalar_tensor_tensor(
                            acc16[gtt][:, dh * 512:(dh + 1) * 512], bank[:, :],
                            m_tok[:, gtt * E + e:gtt * E + e + 1],
                            acc16[gtt][:, dh * 512:(dh + 1) * 512],
                            ALU.mult, ALU.add),
                            deps=[mt, route_done, acc_init[gtt]])
                        bankG[gname] = au
                        mm2_done = au
                        if tb == NB - 1:
                            w2_free.append(au)
    if dbg:
        p.emit("gpsimd", lambda g: g.dma_start(
            out=dbg_acc[:, :], in_=acc16[0][:, :]),
            deps=list(bankG.values()), dma=True)
        p.emit("gpsimd", lambda g: g.dma_start(
            out=dbg_csum[0:1, :], in_=csum), deps=[csum_tok], dma=True)
    es2.close()
    expert_done = list(bankG.values()) + [csum_tok]
    b2r16 = sb("b2r16", [E, D], BF16)
    cst16 = sb("cst16", [1, D], BF16)[0:1, :]
    cstb = sb("cstb", [PT, D])
    xfin = [sb(f"xf_{i}", [PT, PT]) for i in range(DT)]
    tmpf = sb("tmpf", [PT, D])
    ones8c = sb("ones8c", [E, 1], BF16)
    p.emit("gpsimd", lambda g: g.dma_start(out=b2r16[:, :], in_=b2r[:, :]),
           deps=expert_done, dma=True)
    tb2 = p.gate("gpsimd")
    t8c = p.emit("vector", lambda v: v.memset(ones8c[:, :], 1.0))
    cv1 = None
    for dblk in range(2):
        mt = p.emit("tensor", lambda t, dblk=dblk: t.matmul(
            psR[0:1, :], ones8c[:, 0:1], b2r16[:, dblk * 512:(dblk + 1) * 512],
            start=True, stop=True),
            deps=[tb2, t8c, csum_tok] + ([cv1] if cv1 else []))
        cv1 = p.emit("vector", lambda v, dblk=dblk: v.tensor_add(
            crow[0:1, dblk * 512:(dblk + 1) * 512], csum[0:1, dblk * 512:(dblk + 1) * 512],
            psR[0:1, :]), deps=[mt, csum_tok])
    cv2 = p.emit("vector", lambda v: v.tensor_copy(cst16, crow), deps=[cv1])
    cb_toks = []
    for c_ in range(DT):
        bk = p.emit("tensor", lambda t, c_=c_: t.matmul(
            psT[:, 0:PT], ones_row[0:1, :], cst16[0:1, c_ * PT:(c_ + 1) * PT],
            start=True, stop=True),
            deps=[cv2, tro] + ([cb_toks[-1]] if cb_toks else []))
        ck = p.emit("vector", lambda v, c_=c_: v.tensor_copy(
            cstb[:, c_ * PT:(c_ + 1) * PT], psT[:, 0:PT]), deps=[bk])
        cb_toks.append(ck)
    cst_done = cb_toks[-1]
    pv_tok = None
    for tt in range(TT_):
        for i in range(DT):
            p.emit("sync", lambda s, i=i, tt=tt: s.dma_start(
                out=xfin[i][:, :], in_=x_in[i * PT:(i + 1) * PT, tt * PT:(tt + 1) * PT]),
                deps=([pv_tok] if pv_tok else expert_done), dma=True)
        gxt = p.gate("sync")
        xt = [gxt] * DT
        a0 = p.emit("vector", lambda v, tt=tt: v.tensor_add(
            tmpf[:, :], acc16[tt][:, :], cstb[:, :]),
            deps=[cst_done] + list(bankG.values()) +
            ([pv_tok] if pv_tok else []))
        otoks = []
        for i in range(DT):
            tr = p.emit("tensor", lambda t, i=i: t.transpose(
                psT[:, 0:PT], tmpf[:, i * PT:(i + 1) * PT], ident16[:, :]),
                deps=[a0] + ([otoks[-1][1]] if otoks else [cst_done]))
            o1 = p.emit("vector", lambda v, i=i: v.scalar_tensor_tensor(
                xfin[i][:, :], psT[:, 0:PT], 0.5, xfin[i][:, :], ALU.mult, ALU.add),
                deps=[tr, xt[i]])
            p.emit("gpsimd", lambda g, i=i, tt=tt: g.dma_start(
                out=x_out[i * PT:(i + 1) * PT, tt * PT:(tt + 1) * PT],
                in_=xfin[i][:, :]), deps=[o1], dma=True)
            otoks.append((tr, o1))
        pv_tok = p.gate("gpsimd")

    toks = p.barrier_tokens()
    p.emit("sync", lambda s: s.nop(), deps=toks)
    with nc.Block() as block:
        p.finalize(block)
    return nc


def _prep_moe_main_inputs(xcore, xn_core, aff_core, parts8, thr, W1, b1, W2, b2):
    E = 8
    b1c = np.ascontiguousarray(
        np.asarray(b1).reshape(E, 32, 128).transpose(2, 0, 1).reshape(128, E * 32))
    io8 = np.broadcast_to(np.arange(E, dtype=np.float32), (128, E)).copy()
    ioB = io8 - 999.0
    return {
        "x_in": np.ascontiguousarray(xcore),
        "xn_din": np.ascontiguousarray(xn_core),
        "aff_din": np.ascontiguousarray(aff_core),
        "parts": np.ascontiguousarray(
            np.broadcast_to(np.asarray(parts8, np.float32).reshape(1, 8), (8, 8))),
        "thr_in": np.ascontiguousarray(
            np.broadcast_to(np.asarray(thr, np.float32).reshape(1, 1), (8, 1))),
        "W1": np.ascontiguousarray(W1),
        "W2": np.ascontiguousarray(W2),
        "b1c": b1c,
        "b2r": np.ascontiguousarray(b2),
        "iota8_in": io8,
        "iotaB_in": ioB,
        "ident_in": np.eye(128, dtype=np.float32),
    }


def build_ssd(nch=None):
    DI = 2048
    IT = DI // PT
    HN = 1024
    HT = HN // PT
    NH = 16
    Tc = 512
    NCH = (T // Tc) if nch is None else nch
    QC = 32
    NSC = Tc // QC
    RTOT = 6208
    EPS2 = 1e-6
    DT_MIN, DT_MAX = 1e-4, 1.0

    nc = bass.Bass("TRN2", target_bir_lowering=False)
    x_in = nc.dram_tensor("x_in", [D, T], F32, kind="ExternalInput")
    ipT2 = nc.dram_tensor("ipT2", [D, RTOT], F32, kind="ExternalInput")
    convw_c = nc.dram_tensor("convw_c", [PT, IT * 4], F32, kind="ExternalInput")
    convb_c = nc.dram_tensor("convb_c", [PT, IT], F32, kind="ExternalInput")
    dtpT2 = nc.dram_tensor("dtpT2", [64, NH], F32, kind="ExternalInput")
    dtb_c = nc.dram_tensor("dtb_c", [NH, 1], F32, kind="ExternalInput")
    Alog_c = nc.dram_tensor("Alog_c", [PT, HT], F32, kind="ExternalInput")
    Dsk_c = nc.dram_tensor("Dsk_c", [PT, NH], F32, kind="ExternalInput")
    nw2_c = nc.dram_tensor("nw2_c", [PT, IT], F32, kind="ExternalInput")
    resT = nc.dram_tensor("resT", [D, DI], F32, kind="ExternalInput")
    opT2 = nc.dram_tensor("opT2", [DI, D], F32, kind="ExternalInput")
    indW_c = nc.dram_tensor("indW_c", [PT, HT * NH], F32, kind="ExternalInput")
    bsel_c = nc.dram_tensor("bsel_c", [NH, HT * PT], F32, kind="ExternalInput")
    hsel_c = nc.dram_tensor("hsel_c", [NH, NH * PT], F32, kind="ExternalInput")
    ident_c = nc.dram_tensor("ident_c", [PT, PT], F32, kind="ExternalInput")
    x_out = nc.dram_tensor("x_out", [D, T], F32, kind="ExternalOutput")

    es = nc.ctx
    sb = lambda n, s, d=F32: es.enter_context(nc.sbuf_tensor(n, s, d))
    psf = lambda n, s, d=F32: es.enter_context(nc.psum_tensor(n, s, d))
    APX = bass.AP

    convw = sb("convw", [PT, IT * 4])
    convb = sb("convb", [PT, IT])
    dtpTs = sb("dtpTs", [64, NH], BF16)
    dtb16 = sb("dtb16", [NH, 1])
    A_sb = sb("A_sb", [PT, HT])
    Dsk = sb("Dsk", [PT, NH])
    nw2 = sb("nw2", [PT, IT])
    indW = sb("indW", [PT, HT * NH], BF16)
    bsel32 = sb("bsel32", [NH, HT * PT])
    bsel16 = sb("bsel16", [NH, HT * PT], BF16)
    hsel = sb("hsel", [NH, NH * PT], BF16)
    id16 = sb("id16", [PT, PT], BF16)
    ones_col = sb("ones_colZ", [PT, 1], BF16)
    ones_row = sb("ones_rowZ", [1, PT], BF16)
    eps_col = sb("eps_colZ", [PT, 1])
    ipg = [[sb(f"ipg_{b}_{k}", [PT, 512], BF16) for k in range(DT)] for b in range(2)]
    x_sb = [sb(f"x_{i}", [PT, Tc]) for i in range(DT)]
    xn16 = [sb(f"xn_{i}", [PT, Tc], BF16) for i in range(DT)]
    xq16 = [sb(f"xq_{i}", [PT, Tc], BF16) for i in range(2)]
    u16 = [sb(f"u16_{i}", [PT, Tc], BF16) for i in range(IT)]
    z16 = [sb(f"z16_{i}", [PT, Tc], BF16) for i in range(IT)]
    usb = [sb(f"usb_{i}", [PT, Tc + 4]) for i in range(IT)]
    halb = sb("halbZ", [PT, 3 * IT])
    B16 = [sb(f"B16_{i}", [PT, Tc], BF16) for i in range(HT)]
    C16 = [sb(f"C16_{i}", [PT, Tc], BF16) for i in range(HT)]
    dth16 = sb("dth16", [64, Tc], BF16)
    dtrow = sb("dtrow", [NH, Tc])
    dt16r = sb("dt16r", [NH, Tc], BF16)
    S_row = sb("S_row", [NH, Tc])
    scar = sb("scar", [NH, 1])
    S_b = [sb(f"S_b_{i}", [PT, Tc + 1]) for i in range(HT)]
    CW = [sb(f"CW_{i}", [PT, Tc]) for i in range(HT)]
    BwG = [sb(f"BwG_{i}", [PT, Tc], BF16) for i in range(HT)]
    Gam = [sb(f"Gam_{i}", [PT, NSC]) for i in range(HT)]
    e16 = sb("e16", [PT, Tc], BF16)
    w16 = sb("w16", [NH, Tc], BF16)
    hst = [sb(f"hst_{i}", [PT, PT]) for i in range(HT)]
    uT = [sb(f"uT_{pc}", [64, PT], BF16) for pc in range(8)]
    BwT = [sb(f"BwT_{pc}", [64, 64], BF16) for pc in range(8)]
    sc_a = sb("sc_aZ", [PT, Tc])
    sc_b = sb("sc_bZ", [PT, Tc])
    mrow = sb("m_rowZ", [1, Tc])[0:1, :]
    vrow = sb("v_rowZ", [1, Tc])[0:1, :]
    rrow = sb("r_rowZ", [1, Tc])[0:1, :]
    m16 = sb("m16Z", [1, Tc], BF16)[0:1, :]
    r16 = sb("r16Z", [1, Tc], BF16)[0:1, :]
    yac = [usb[i][:, 0:Tc] for i in range(IT)]

    pk = [psf(f"pk_{i}", [PT, Tc]) for i in range(4)]
    pw = psf("pw", [NH, Tc])
    pyh = psf("pyh", [PT, Tc])
    ptr16 = psf("ptr16", [64, PT], BF16)
    pbh = psf("pbh", [64, PT])

    p = Prog(nc)
    for dst, src_ in [(convw, convw_c), (convb, convb_c), (dtb16, dtb_c),
                      (A_sb, Alog_c), (Dsk, Dsk_c), (nw2, nw2_c),
                      (bsel32, bsel_c)]:
        p.emit("sync", lambda s, dst=dst, src_=src_: s.dma_start(
            out=dst[:, :], in_=src_[:, :]), dma=True)
    sg0 = p.gate("sync")
    for dst, src_ in [(dtpTs, dtpT2), (indW, indW_c), (hsel, hsel_c),
                      (id16, ident_c)]:
        p.emit("gpsimd", lambda g, dst=dst, src_=src_: g.dma_start(
            out=dst[:, :], in_=src_[:, :]), dma=True)
    rg = p.gate("gpsimd")
    t1 = p.emit("scalar", lambda s: s.activation(A_sb[:, :], A_sb[:, :], AF.Exp),
                deps=[sg0])
    tA = p.emit("vector", lambda v: v.tensor_single_scalar(
        A_sb[:, :], A_sb[:, :], -1.0, ALU.mult), deps=[t1])
    tb16 = p.emit("vector", lambda v: v.tensor_copy(bsel16[:, :], bsel32[:, :]),
                  deps=[sg0])
    tco = p.emit("vector", lambda v: v.memset(ones_col[:, :], 1.0))
    tro = p.emit("vector", lambda v: v.memset(ones_row[:, :], 1.0))
    tep = p.emit("vector", lambda v: v.memset(eps_col[:, :], LN_EPS))
    thal = p.emit("vector", lambda v: v.memset(halb[:, :], 0.0))
    tscz = p.emit("vector", lambda v: v.memset(scar[:, :], 0.0))
    hst_tok = [p.emit("vector", lambda v, i=i: v.memset(hst[i][:, :], 0.0))
               for i in range(HT)]
    tsb0 = [p.emit("vector", lambda v, i=i: v.memset(S_b[i][:, 0:1], 0.0))
            for i in range(HT)]

    bank_free = {}
    ipg_free = [[], []]
    prev_end = []

    def mm_bank(idx):
        return pk[idx], f"pk_{idx}"

    for ch in range(NCH):
        ts_ = slice(ch * Tc, (ch + 1) * Tc)
        for i in range(DT):
            p.emit("sync", lambda s, i=i, ts_=ts_: s.dma_start(
                out=x_sb[i][:, :], in_=x_in[i * PT:(i + 1) * PT, ts_]),
                deps=prev_end, dma=True)
        gx = p.gate("sync")
        stat_toks = []
        for i in range(DT):
            c1 = p.emit("vector", lambda v, i=i: v.tensor_copy(
                xq16[0][:, :], x_sb[i][:, :]),
                deps=[gx] + ([stat_toks[-1]] if stat_toks else []))
            c2 = p.emit("scalar", lambda s, i=i: s.activation(
                xq16[1][:, :], x_sb[i][:, :], AF.Square),
                deps=[gx] + ([stat_toks[-1]] if stat_toks else []))
            m1 = p.emit("tensor", lambda t, i=i: t.matmul(
                pk[2][0:1, :], ones_col[:, 0:1], xq16[0][:, :],
                start=(i == 0), stop=(i == DT - 1)),
                deps=[c1, tco] + ([bank_free["pk_2"]] if i == 0 and "pk_2" in bank_free else []))
            m2 = p.emit("tensor", lambda t, i=i: t.matmul(
                pk[3][0:1, :], ones_col[:, 0:1], xq16[1][:, :],
                start=(i == 0), stop=(i == DT - 1)),
                deps=[c2] + ([bank_free["pk_3"]] if i == 0 and "pk_3" in bank_free else []))
            stat_toks.append(m2)
        st = stat_toks[-1]
        e2 = p.emit("vector", lambda v: v.tensor_single_scalar(
            mrow, pk[2][0:1, :], 1.0 / D, ALU.mult), deps=[st])
        bank_free["pk_2"] = e2
        e3 = p.emit("scalar", lambda s: s.activation(sc_a[0:1, 0:Tc], mrow, AF.Square),
                    deps=[e2])
        e4 = p.emit("vector", lambda v: v.tensor_single_scalar(
            vrow, pk[3][0:1, :], 1.0 / D, ALU.mult), deps=[st])
        bank_free["pk_3"] = e4
        e5 = p.emit("vector", lambda v: v.tensor_sub(vrow, vrow, sc_a[0:1, 0:Tc]),
                    deps=[e4, e3])
        e6 = p.emit("scalar", lambda s: s.activation(
            sc_a[0:1, 0:Tc], vrow, AF.Sqrt, bias=eps_col[0:1, 0:1]), deps=[e5, tep])
        e7 = p.emit("vector", lambda v: v.reciprocal(rrow, sc_a[0:1, 0:Tc]), deps=[e6])
        c8 = p.emit("vector", lambda v: v.tensor_copy(m16, mrow), deps=[e2])
        c9 = p.emit("vector", lambda v: v.tensor_copy(r16, rrow), deps=[e7])
        bc1 = p.emit("tensor", lambda t: t.matmul(
            pk[0][:, :], ones_row[0:1, :], m16, start=True, stop=True),
            deps=[c8, tro] + ([bank_free["pk_0"]] if "pk_0" in bank_free else []))
        bc2 = p.emit("tensor", lambda t: t.matmul(
            pk[1][:, :], ones_row[0:1, :], r16, start=True, stop=True),
            deps=[c9] + ([bank_free["pk_1"]] if "pk_1" in bank_free else []))
        xnt = []
        last_s2 = None
        for i in range(DT):
            s1 = p.emit("vector", lambda v, i=i: v.tensor_sub(
                sc_a[:, :], x_sb[i][:, :], pk[0][:, :]),
                deps=[bc1, gx] + ([last_s2] if last_s2 else []))
            s2 = p.emit("vector", lambda v, i=i: v.tensor_mul(
                xn16[i][:, :], sc_a[:, :], pk[1][:, :]), deps=[bc2, s1])
            last_s2 = s2
            xnt.append(s2)
        bank_free["pk_0"] = last_s2
        bank_free["pk_1"] = last_s2
        prj_ev = {}
        for fg in range(ceil_div(RTOT, 512)):
            buf = fg % 2
            c0 = fg * 512
            cw = min(512, RTOT - c0)
            for k in range(DT):
                p.emit("gpsimd", lambda g, k=k, c0=c0, cw=cw, buf=buf: g.dma_start(
                    out=ipg[buf][k][:, 0:cw],
                    in_=ipT2[k * PT:(k + 1) * PT, c0:c0 + cw]),
                    deps=ipg_free[buf], dma=True)
            gfg = p.gate("gpsimd")
            ipg_free[buf] = []
            for jj in range(ceil_div(cw, PT)):
                r0 = c0 + jj * PT
                mw = min(PT, cw - jj * PT)
                bank, bname = mm_bank(2 + jj % 2)
                mt = None
                for k in range(DT):
                    mt = p.emit("tensor", lambda t, k=k, jj=jj, buf=buf, bank=bank, mw=mw: t.matmul(
                        bank[0:mw, :], ipg[buf][k][:, jj * PT:jj * PT + mw],
                        xn16[k][:, :], start=(k == 0), stop=(k == DT - 1)),
                        deps=[gfg, xnt[k]] +
                        ([bank_free[bname]] if bname in bank_free and k == 0 else []))
                ipg_free[buf].append(mt)
                ti = r0 // PT
                if r0 < DI:
                    ev = p.emit("scalar", lambda s, ti=ti, bank=bank: s.activation(
                        z16[ti][:, :], bank[:, :], AF.Silu), deps=[mt])
                elif r0 < 2 * DI:
                    ev = p.emit("vector", lambda v, ti=ti, bank=bank: v.tensor_copy(
                        usb[ti - 16][:, 3:3 + Tc], bank[:, :]), deps=[mt])
                elif r0 < 2 * DI + HN:
                    ev = p.emit("vector", lambda v, ti=ti, bank=bank: v.tensor_copy(
                        B16[ti - 32][:, :], bank[:, :]), deps=[mt])
                elif r0 < 2 * DI + 2 * HN:
                    ev = p.emit("scalar", lambda s, ti=ti, bank=bank: s.copy(
                        C16[ti - 40][:, :], bank[:, :]), deps=[mt])
                else:
                    ev = p.emit("vector", lambda v, bank=bank, mw=mw: v.tensor_copy(
                        dth16[0:mw, :], bank[0:mw, :]), deps=[mt])
                bank_free[bname] = ev
                prj_ev[ti] = ev
        conv_toks = []
        for it in range(IT):
            h0 = p.emit("vector", lambda v, it=it: v.tensor_copy(
                usb[it][:, 0:3], halb[:, it * 3:(it + 1) * 3]),
                deps=[prj_ev[16 + it], thal] + ([conv_toks[-1][0]] if conv_toks else []))
            a1 = p.emit("vector", lambda v, it=it: v.tensor_scalar(
                sc_a[:, :], usb[it][:, 0:Tc], convw[:, it * 4:it * 4 + 1], None,
                ALU.mult), deps=[h0, sg0])
            for kk in range(1, 4):
                a1 = p.emit("vector", lambda v, it=it, kk=kk: v.scalar_tensor_tensor(
                    sc_a[:, :], usb[it][:, kk:kk + Tc],
                    convw[:, it * 4 + kk:it * 4 + kk + 1],
                    sc_a[:, :], ALU.mult, ALU.add), deps=[a1])
            hs = p.emit("vector", lambda v, it=it: v.tensor_copy(
                halb[:, it * 3:(it + 1) * 3], usb[it][:, Tc:Tc + 3]),
                deps=[prj_ev[16 + it], h0])
            uu = p.emit("vector", lambda v, it=it: v.tensor_scalar(
                u16[it][:, :], sc_a[:, :], convb[:, it:it + 1], None, ALU.add),
                deps=[a1, sg0])
            conv_toks.append((uu, hs))
        mt = p.emit("tensor", lambda t: t.matmul(
            pw[0:NH, :], dtpTs[:, :], dth16[:, :], start=True, stop=True),
            deps=[prj_ev[48], rg] + ([bank_free["pw"]] if "pw" in bank_free else []))
        d1 = p.emit("vector", lambda v: v.tensor_scalar(
            dtrow[:, :], pw[0:NH, :], dtb16[:, 0:1], None, ALU.add), deps=[mt, sg0])
        bank_free["pw"] = d1
        d2 = p.emit("scalar", lambda s: s.activation(S_row[:, :], dtrow[:, :], AF.Abs),
                    deps=[d1])
        d3 = p.emit("scalar", lambda s: s.activation(S_row[:, :], S_row[:, :], AF.Exp,
                                                     scale=-1.0), deps=[d2])
        d4 = p.emit("scalar", lambda s: s.activation(S_row[:, :], S_row[:, :], AF.Ln,
                                                     bias=1.0), deps=[d3])
        d5 = p.emit("vector", lambda v: v.tensor_single_scalar(
            dtrow[:, :], dtrow[:, :], 0.0, ALU.max), deps=[d4, d1])
        d6 = p.emit("vector", lambda v: v.tensor_add(
            dtrow[:, :], dtrow[:, :], S_row[:, :]), deps=[d5])
        d7 = p.emit("vector", lambda v: v.tensor_single_scalar(
            dtrow[:, :], dtrow[:, :], DT_MIN, ALU.max), deps=[d6])
        d8 = p.emit("vector", lambda v: v.tensor_single_scalar(
            dtrow[:, :], dtrow[:, :], DT_MAX, ALU.min), deps=[d7])
        d9 = p.emit("vector", lambda v: v.tensor_copy(dt16r[:, :], dtrow[:, :]),
                    deps=[d8])
        sc1 = p.emit("vector", lambda v: v.tensor_tensor_scan(
            S_row[:, :], dtrow[:, :], dtrow[:, :], scar[:, 0:1],
            ALU.add, ALU.bypass), deps=[d9, tscz])
        sc2 = p.emit("vector", lambda v: v.tensor_copy(
            scar[:, 0:1], S_row[:, Tc - 1:Tc]), deps=[sc1])
        wv_toks = []
        for i in range(HT):
            bank, bname = mm_bank(i % 2)
            b1_ = p.emit("tensor", lambda t, i=i, bank=bank: t.matmul(
                bank[:, :], bsel32[:, i * PT:(i + 1) * PT], S_row[:, :],
                start=True, stop=True),
                deps=[sc1, sg0] + ([bank_free[bname]] if bname in bank_free else []))
            ev1 = p.emit("vector", lambda v, i=i, bank=bank: v.tensor_copy(
                S_b[i][:, 1:Tc + 1], bank[:, :]), deps=[b1_, tsb0[i]])
            bank_free[bname] = ev1
            bank2, bname2 = mm_bank(2 + i % 2)
            b2_ = p.emit("tensor", lambda t, i=i, bank2=bank2: t.matmul(
                bank2[:, :], bsel16[:, i * PT:(i + 1) * PT], dt16r[:, :],
                start=True, stop=True),
                deps=[d9, tb16] + ([bank_free[bname2]] if bname2 in bank_free else []))
            bd = p.emit("vector", lambda v, i=i, bank2=bank2: v.tensor_mul(
                B16[i][:, :], B16[i][:, :], bank2[:, :]), deps=[b2_, prj_ev[32 + i]])
            bank_free[bname2] = bd
            q1 = p.emit("vector", lambda v, i=i: v.tensor_sub(
                sc_a[:, :], S_b[i][:, 1:Tc + 1],
                APX(S_b[i].tensor if hasattr(S_b[i], 'tensor') else S_b[i], 0,
                    [[Tc + 1, PT], [QC, NSC], [0, QC]])), deps=[ev1] + wv_toks[-1:])
            q2 = p.emit("scalar", lambda s, i=i: s.activation(
                e16[:, :], sc_a[:, :], AF.Exp, scale=A_sb[:, i:i + 1]),
                deps=[q1, tA])
            q3 = p.emit("vector", lambda v, i=i: v.tensor_mul(
                CW[i][:, :], C16[i][:, :], e16[:, :]), deps=[q2, prj_ev[40 + i]])
            q4 = p.emit("vector", lambda v, i=i: v.tensor_sub(
                sc_a[:, :],
                APX(S_b[i].tensor if hasattr(S_b[i], 'tensor') else S_b[i], QC,
                    [[Tc + 1, PT], [QC, NSC], [0, QC]]),
                S_b[i][:, 1:Tc + 1]), deps=[q3, ev1])
            q5 = p.emit("scalar", lambda s, i=i: s.activation(
                e16[:, :], sc_a[:, :], AF.Exp, scale=A_sb[:, i:i + 1]),
                deps=[q4])
            q6 = p.emit("vector", lambda v, i=i: v.tensor_mul(
                BwG[i][:, :], B16[i][:, :], e16[:, :]), deps=[q5, bd])
            q7 = p.emit("vector", lambda v, i=i: v.tensor_tensor(
                Gam[i][:, :],
                APX(S_b[i].tensor if hasattr(S_b[i], 'tensor') else S_b[i], QC,
                    [[Tc + 1, PT], [QC, NSC]]),
                APX(S_b[i].tensor if hasattr(S_b[i], 'tensor') else S_b[i], 0,
                    [[Tc + 1, PT], [QC, NSC]]), ALU.subtract), deps=[ev1, q6])
            q8 = p.emit("scalar", lambda s, i=i: s.activation(
                Gam[i][:, :], Gam[i][:, :], AF.Exp, scale=A_sb[:, i:i + 1]),
                deps=[q7])
            wv_toks.append(q8)
        wv_done = wv_toks[-1]
        yinit = []
        for it in range(IT):
            q0 = p.emit("vector", lambda v, it=it: v.tensor_copy(
                sc_b[:, :], u16[it][:, :]),
                deps=[conv_toks[it][0], conv_toks[it][1]] + ([yinit[-1]] if yinit else [wv_done]))
            q1_ = p.emit("vector", lambda v, it=it: v.tensor_scalar(
                yac[it], sc_b[:, :], Dsk[:, it:it + 1], None, ALU.mult),
                deps=[q0, sg0])
            yinit.append(q1_)
        ysum_toks = []
        for h in range(NH):
            til = h // 2
            ro = (h % 2) * 64
            trt = []
            for pc in range(8):
                tr1 = p.emit("tensor", lambda t, h=h, pc=pc: t.transpose(
                    ptr16[0:64, :], u16[h][:, pc * 64:(pc + 1) * 64], id16[:, :]),
                    deps=[conv_toks[h][0], rg] + ([trt[-1][1]] if trt else ([ysum_toks[-1]] if ysum_toks else [])))
                tv1 = p.emit("vector", lambda v, pc=pc: v.tensor_copy(
                    uT[pc][:, :], ptr16[0:64, :]), deps=[tr1])
                tr2 = p.emit("tensor", lambda t, h=h, til=til, ro=ro, pc=pc: t.transpose(
                    ptr16[0:64, 0:64], BwG[til][ro:ro + 64, pc * 64:(pc + 1) * 64],
                    id16[ro:ro + 64, ro:ro + 64]), deps=[wv_done, tv1])
                tv2 = p.emit("vector", lambda v, pc=pc: v.tensor_copy(
                    BwT[pc][:, :], ptr16[0:64, 0:64]), deps=[tr2])
                trt.append((tr1, tv2))
            hupd = hst_tok[til]
            yiq = []
            for sc in range(NSC):
                yi = p.emit("tensor", lambda t, h=h, til=til, ro=ro, sc=sc: t.matmul(
                    pyh[:, sc * QC:(sc + 1) * QC], hst[til][ro:ro + 64, :],
                    CW[til][ro:ro + 64, sc * QC:(sc + 1) * QC],
                    start=True, stop=True),
                    deps=[hupd, wv_done] + ([bank_free["pyh"]] if sc == 0 and "pyh" in bank_free else []))
                bh = p.emit("tensor", lambda t, sc=sc: t.matmul(
                    pbh[0:64, :], BwT[sc // 2][(sc % 2) * 32:(sc % 2) * 32 + 32, :],
                    uT[sc // 2][(sc % 2) * 32:(sc % 2) * 32 + 32, :],
                    start=True, stop=True),
                    deps=[trt[sc // 2][1]] + ([hupd] if sc > 0 else [hst_tok[til]])),
                hupd = p.emit("vector", lambda v, til=til, ro=ro, sc=sc: v.scalar_tensor_tensor(
                    hst[til][ro:ro + 64, :], hst[til][ro:ro + 64, :],
                    Gam[til][ro:ro + 64, sc:sc + 1], pbh[0:64, :],
                    ALU.mult, ALU.add), deps=[bh[0], yi, wv_done])
                yiq.append(yi)
            hst_tok[til] = hupd
            ys = p.emit("vector", lambda v, h=h: v.tensor_add(
                yac[h], yac[h], pyh[:, :]), deps=[yiq[-1], yinit[h], hupd])
            bank_free["pyh"] = ys
            ysum_toks.append(ys)
        lag_last = ysum_toks[-1]
        wb_tok = {}
        for j in range(QC):
            wdt = []
            for i in range(HT):
                sbm = lambda off, hnd: APX(
                    hnd.tensor if hasattr(hnd, 'tensor') else hnd, off,
                    [[Tc + 1, PT], [QC, NSC], [1, QC - j]])
                t1_ = p.emit("vector", lambda v, i=i, j=j: v.tensor_tensor(
                    APX(sc_a, j, [[Tc, PT], [QC, NSC], [1, QC - j]]),
                    APX(S_b[i], j + 1, [[Tc + 1, PT], [QC, NSC], [1, QC - j]]),
                    APX(S_b[i], 1, [[Tc + 1, PT], [QC, NSC], [1, QC - j]]),
                    ALU.subtract),
                    deps=[wv_done] + ([wdt[-1]] if wdt else [lag_last]))
                t2_ = p.emit("scalar", lambda s, i=i, j=j: s.activation(
                    APX(BwG[i], j, [[Tc, PT], [QC, NSC], [1, QC - j]]),
                    APX(sc_a, j, [[Tc, PT], [QC, NSC], [1, QC - j]]),
                    AF.Exp, scale=A_sb[:, i:i + 1]),
                    deps=[t1_, tA] + ([wb_tok["mtw"]] if "mtw" in wb_tok else []))
                t3_ = p.emit("vector", lambda v, i=i, j=j: v.tensor_tensor(
                    APX(BwG[i], j, [[Tc, PT], [QC, NSC], [1, QC - j]]),
                    APX(BwG[i], j, [[Tc, PT], [QC, NSC], [1, QC - j]]),
                    APX(C16[i], j, [[Tc, PT], [QC, NSC], [1, QC - j]]),
                    ALU.mult), deps=[t2_])
                t4_ = p.emit("vector", lambda v, i=i, j=j: v.tensor_tensor(
                    APX(BwG[i], j, [[Tc, PT], [QC, NSC], [1, QC - j]]),
                    APX(BwG[i], j, [[Tc, PT], [QC, NSC], [1, QC - j]]),
                    APX(B16[i], 0, [[Tc, PT], [QC, NSC], [1, QC - j]]),
                    ALU.mult), deps=[t3_])
                wb_tok[("G", i)] = t4_
                wdt.append(t4_)
            mtw = None
            for i in range(HT):
                mtw = p.emit("tensor", lambda t, i=i: t.matmul(
                    pw[0:NH, :], indW[:, i * NH:(i + 1) * NH], BwG[i][:, :],
                    start=(i == 0), stop=(i == HT - 1)),
                    deps=[wdt[i], rg] + ([bank_free["pw"]] if i == 0 and "pw" in bank_free else []))
            wc = p.emit("vector", lambda v: v.tensor_copy(w16[:, :], pw[0:NH, :]),
                        deps=[mtw])
            bank_free["pw"] = wc
            wb_tok["mtw"] = mtw
            yprev = None
            for h in range(NH):
                bank, bname = mm_bank(2 + h % 2)
                wb = p.emit("tensor", lambda t, h=h, bank=bank: t.matmul(
                    bank[:, :], hsel[:, h * PT:(h + 1) * PT], w16[:, :],
                    start=True, stop=True),
                    deps=[wc, rg] + ([bank_free[bname]] if bname in bank_free else []))
                y1 = p.emit("vector", lambda v, h=h, j=j, bank=bank: v.tensor_tensor(
                    APX(sc_b, j, [[Tc, PT], [QC, NSC], [1, QC - j]]),
                    APX(u16[h], 0, [[Tc, PT], [QC, NSC], [1, QC - j]]),
                    APX(bank, j, [[Tc, PT], [QC, NSC], [1, QC - j]]),
                    ALU.mult),
                    deps=[wb, conv_toks[h][0]] + ([yprev] if yprev else []))
                y2 = p.emit("vector", lambda v, h=h, j=j: v.tensor_tensor(
                    APX(usb[h], j, [[Tc + 4, PT], [QC, NSC], [1, QC - j]]),
                    APX(usb[h], j, [[Tc + 4, PT], [QC, NSC], [1, QC - j]]),
                    APX(sc_b, j, [[Tc, PT], [QC, NSC], [1, QC - j]]),
                    ALU.add), deps=[y1, ysum_toks[h]])
                bank_free[bname] = y1
                yprev = y2
            lag_last = yprev
        res_ev = {}
        for fg in range(4):
            buf = fg % 2
            for k in range(DT):
                p.emit("gpsimd", lambda g, k=k, fg=fg, buf=buf: g.dma_start(
                    out=ipg[buf][k][:, :], in_=resT[k * PT:(k + 1) * PT,
                                                    fg * 512:(fg + 1) * 512]),
                    deps=ipg_free[buf], dma=True)
            gfg = p.gate("gpsimd")
            ipg_free[buf] = []
            for jj in range(4):
                it = fg * 4 + jj
                bank, bname = mm_bank(jj % 2)
                mt = None
                for k in range(DT):
                    mt = p.emit("tensor", lambda t, k=k, jj=jj, buf=buf, bank=bank: t.matmul(
                        bank[:, :], ipg[buf][k][:, jj * PT:(jj + 1) * PT],
                        xn16[k][:, :], start=(k == 0), stop=(k == DT - 1)),
                        deps=[gfg, xnt[k]] +
                        ([bank_free[bname]] if bname in bank_free and k == 0 else []))
                ipg_free[buf].append(mt)
                g1 = p.emit("vector", lambda v, it=it: v.tensor_mul(
                    sc_a[:, :], yac[it], z16[it][:, :]),
                    deps=[lag_last, prj_ev[it]] + ([res_ev[it - 1]] if it > 0 else []))
                g2 = p.emit("vector", lambda v, it=it, bank=bank: v.tensor_add(
                    z16[it][:, :], sc_a[:, :], bank[:, :]), deps=[g1, mt])
                bank_free[bname] = g2
                res_ev[it] = g2
        stat2 = []
        for it in range(IT):
            cq = p.emit("scalar", lambda s, it=it: s.activation(
                xq16[1][:, :], z16[it][:, :], AF.Square),
                deps=[res_ev[it]] + ([stat2[-1]] if stat2 else []))
            mq = p.emit("tensor", lambda t, it=it: t.matmul(
                pw[0:1, :], ones_col[:, 0:1], xq16[1][:, :],
                start=(it == 0), stop=(it == IT - 1)),
                deps=[cq, tco] + ([bank_free["pw"]] if it == 0 and "pw" in bank_free else []))
            stat2.append(mq)
        r1 = p.emit("vector", lambda v: v.tensor_single_scalar(
            vrow, pw[0:1, :], 1.0 / DI, ALU.mult), deps=[stat2[-1]])
        bank_free["pw"] = r1
        r2 = p.emit("vector", lambda v: v.tensor_single_scalar(
            vrow, vrow, EPS2, ALU.add), deps=[r1])
        r3 = p.emit("scalar", lambda s: s.activation(sc_a[0:1, 0:Tc], vrow, AF.Sqrt),
                    deps=[r2])
        r4 = p.emit("vector", lambda v: v.reciprocal(rrow, sc_a[0:1, 0:Tc]), deps=[r3])
        r5 = p.emit("vector", lambda v: v.tensor_copy(r16, rrow), deps=[r4])
        bcr = p.emit("tensor", lambda t: t.matmul(
            pk[0][:, :], ones_row[0:1, :], r16, start=True, stop=True),
            deps=[r5, tro] + ([bank_free["pk_0"]] if "pk_0" in bank_free else []))
        gn_toks = []
        for it in range(IT):
            n1 = p.emit("vector", lambda v, it=it: v.tensor_mul(
                sc_a[:, :], z16[it][:, :], pk[0][:, :]),
                deps=[bcr, res_ev[it]] + ([gn_toks[-1]] if gn_toks else []))
            n2 = p.emit("vector", lambda v, it=it: v.tensor_scalar(
                z16[it][:, :], sc_a[:, :], nw2[:, it:it + 1], None, ALU.mult),
                deps=[n1, sg0])
            gn_toks.append(n2)
        bank_free["pk_0"] = gn_toks[-1]
        prev_end = []
        for dh in range(2):
            mtg = [None] * 4
            for kh in range(2):
                buf = kh % 2
                for k in range(DT):
                    p.emit("gpsimd", lambda g, k=k, kh=kh, dh=dh, buf=buf: g.dma_start(
                        out=ipg[buf][k][:, :],
                        in_=opT2[kh * 1024 + k * PT:kh * 1024 + (k + 1) * PT,
                                 dh * 512:(dh + 1) * 512]),
                        deps=ipg_free[buf], dma=True)
            gop = p.gate("gpsimd")
            ipg_free[0] = []
            ipg_free[1] = []
            for kh in range(2):
                buf = kh % 2
                for jj in range(4):
                    bank, bname = mm_bank(jj)
                    for k in range(DT):
                        kg = kh * 8 + k
                        mtg[jj] = p.emit("tensor", lambda t, k=k, kg=kg, jj=jj, buf=buf, bank=bank: t.matmul(
                            bank[:, :], ipg[buf][k][:, jj * PT:(jj + 1) * PT],
                            z16[kg][:, :], start=(kg == 0), stop=(kg == IT - 1)),
                            deps=[gop, gn_toks[kg]] +
                            ([bank_free[bname]] if bname in bank_free and kg == 0 else []))
                    if kh == 1:
                        ipg_free[0].append(mtg[jj])
                        ipg_free[1].append(mtg[jj])
            for jj in range(4):
                i = dh * 4 + jj
                bank, bname = mm_bank(jj)
                o1 = p.emit("vector", lambda v, i=i, bank=bank: v.scalar_tensor_tensor(
                    x_sb[i][:, :], bank[:, :], 1.0, x_sb[i][:, :], ALU.mult, ALU.add),
                    deps=[mtg[jj]])
                bank_free[bname] = o1
                p.emit("gpsimd", lambda s, i=i, ts_=ts_: s.dma_start(
                    out=x_out[i * PT:(i + 1) * PT, ts_], in_=x_sb[i][:, :]),
                    deps=[o1], dma=True)
        prev_end = [p.gate("gpsimd")]

    toks = p.barrier_tokens()
    p.emit("sync", lambda s: s.nop(), deps=toks)
    with nc.Block() as block:
        p.finalize(block)
    return nc


def _prep_ssd_inputs(xcore, in_proj_w, dt_proj_w, conv_w, conv_b, A_log, Dskip,
                     dt_bias, norm_weight, out_proj_w, res_proj_w):
    DI, NH, DS = 2048, 16, 64
    ip = np.asarray(in_proj_w)
    order = np.concatenate([
        np.arange(0, 2 * DI),
        np.arange(2 * DI + 64, 2 * DI + 64 + 1024),
        np.arange(2 * DI + 64 + 1024, 6208),
        np.arange(2 * DI, 2 * DI + 64),
    ])
    ipr = ip[order]
    indW = np.zeros((128, 8 * 16), np.float32)
    bsel = np.zeros((16, 8 * 128), np.float32)
    hsel = np.zeros((16, 16 * 128), np.float32)
    for ti in range(8):
        for pp in range(128):
            h = (ti * 128 + pp) // 64
            indW[pp, ti * 16 + h] = 1.0
            bsel[h, ti * 128 + pp] = 1.0
    for h in range(16):
        hsel[h, h * 128:(h + 1) * 128] = 1.0
    return {
        "x_in": np.ascontiguousarray(xcore),
        "ipT2": np.ascontiguousarray(ipr.T),
        "convw_c": np.ascontiguousarray(
            np.asarray(conv_w).reshape(16, 128, 4).transpose(1, 0, 2).reshape(128, 64)),
        "convb_c": np.ascontiguousarray(np.asarray(conv_b).reshape(16, 128).T),
        "dtpT2": np.ascontiguousarray(np.asarray(dt_proj_w).T),
        "dtb_c": np.asarray(dt_bias, np.float32).reshape(16, 1),
        "Alog_c": np.ascontiguousarray(
            np.repeat(np.asarray(A_log).reshape(16, 64), 2, axis=0)
            .reshape(8, 2, 64).reshape(8, 128).T) if False else
            np.ascontiguousarray(np.asarray(A_log).reshape(8, 128).T),
        "Dsk_c": np.ascontiguousarray(np.asarray(Dskip).reshape(16, 128).T),
        "nw2_c": np.ascontiguousarray(np.asarray(norm_weight).reshape(16, 128).T),
        "resT": np.ascontiguousarray(np.asarray(res_proj_w).T),
        "opT2": np.ascontiguousarray(np.asarray(out_proj_w).T),
        "indW_c": indW,
        "bsel_c": bsel,
        "hsel_c": hsel,
        "ident_c": np.eye(128, dtype=np.float32),
    }


def kernel(**d):
    C = NCORES
    xs = [np.ascontiguousarray(d["L0_attn_x"][c].T) for c in range(C)]

    ffn_w = dict(w1=np.ascontiguousarray(d["L0_ffn_w1"]),
                 b1c=np.ascontiguousarray(d["L0_ffn_b1"].reshape(32, 128).T),
                 w2=np.ascontiguousarray(d["L0_ffn_w2"]),
                 b2c=np.ascontiguousarray(d["L0_ffn_b2"].reshape(8, 128).T))
    mam_args = dict(
        norm_w=d["L0_attn_norm_w"], in_proj_w=d["L0_attn_in_proj_w"],
        conv_w=d["L0_attn_conv_w"], conv_b=d["L0_attn_conv_b"],
        x_proj_w=d["L0_attn_x_proj_w"], dt_proj_w=d["L0_attn_dt_proj_w"],
        dt_proj_b=d["L0_attn_dt_proj_b"], A_log=d["L0_attn_A_log"],
        D_param=d["L0_attn_D_param"], out_proj_w=d["L0_attn_out_proj_w"])
    ssd_args = dict(
        in_proj_w=d["L1_attn_in_proj_w"], dt_proj_w=d["L1_attn_dt_proj_w"],
        conv_w=d["L1_attn_conv_w"], conv_b=d["L1_attn_conv_b"],
        A_log=d["L1_attn_A_log"], Dskip=d["L1_attn_Dskip"],
        dt_bias=d["L1_attn_dt_bias"], norm_weight=d["L1_attn_norm_weight"],
        out_proj_w=d["L1_attn_out_proj_w"], res_proj_w=d["L1_attn_res_proj_w"])
    waffT = np.ascontiguousarray(d["L1_ffn_waff"].T)
    thr = d["L1_ffn_affinity_threshold"]

    def launch(name, maps):
        res = run_bass_kernel_spmd(_get_prog(name), maps, core_ids=list(range(C)))
        return res.results

    def run_ffn(xs):
        r = launch("ffn0", [{"x_in": xs[c], **ffn_w} for c in range(C)])
        return [np.ascontiguousarray(r[c]["x_out"]) for c in range(C)]

    def run_mamba(xs):
        r = launch("mamba1", [_prep_mamba1_inputs(xs[c], **mam_args)
                              for c in range(C)])
        return [np.ascontiguousarray(r[c]["x_out"]) for c in range(C)]

    def run_moe(xs):
        rs = launch("moe_stats", [{"x_in": xs[c], "waffT": waffT}
                                  for c in range(C)])
        parts = np.array([rs[c]["asum_d"][0, 0] for c in range(C)], np.float32)
        rm = launch("moe_main", [_prep_moe_main_inputs(
            xs[c], rs[c]["xn_d"], rs[c]["aff_d"], parts, thr,
            d["L1_ffn_W1"], d["L1_ffn_b1"], d["L1_ffn_W2"], d["L1_ffn_b2"])
            for c in range(C)])
        return [np.ascontiguousarray(rm[c]["x_out"]) for c in range(C)]

    def run_ssd(xs):
        r = launch("ssd", [_prep_ssd_inputs(xs[c], **ssd_args)
                           for c in range(C)])
        return [np.ascontiguousarray(r[c]["x_out"]) for c in range(C)]

    xs = run_ffn(xs)
    xs = run_mamba(xs)
    xs = run_ffn(xs)
    xs = run_moe(xs)
    xs = run_ssd(xs)
    xs = run_moe(xs)
    return np.stack([x.T for x in xs]).astype(np.float32)


# version 04
# speedup vs baseline: 1.0212x; 1.0212x over previous
import numpy as np
import ml_dtypes
import concourse.bass as bass
import concourse.mybir as mybir
from concourse.bass_utils import run_bass_kernel_spmd

F32 = mybir.dt.float32
BF16 = mybir.dt.bfloat16
AF = mybir.ActivationFunctionType
ALU = mybir.AluOpType
AX = mybir.AxisListType

NCORES = 8
D = 1024
T = 4096
PT = 128
DT = D // PT
LN_EPS = 1e-5

BF = ml_dtypes.bfloat16


class Prog:

    ENGINES = ("sync", "tensor", "vector", "scalar", "gpsimd", "sync2")

    def __init__(self, nc):
        self.nc = nc
        self.q = {e: [] for e in self.ENGINES}
        self.sems = {}
        self.cnt = {}
        self.last_wait = {}
        for e in self.ENGINES:
            h = nc.ctx.enter_context(nc.semaphore(f"sem_{e}"))
            self.sems[e] = h
            self.cnt[e] = 0

    def emit(self, eng, fn, deps=(), dma=False, inc=None):
        if inc is None:
            inc = 16 if dma else 1
        self.cnt[eng] += inc
        tok = (eng, self.cnt[eng])
        sem = self.sems[eng]
        deps = tuple(deps)

        def code(e, fn=fn, deps=deps, sem=sem, inc=inc, eng=eng):
            for (s, c) in deps:
                key = (eng, s)
                if self.last_wait.get(key, 0) < c:
                    e.wait_ge(self.sems[s], c)
                    self.last_wait[key] = c
            instr = fn(e)
            instr.then_inc(sem, inc)
        self.q[eng].append(code)
        return tok

    def gate(self, eng):
        tot = self.cnt[eng]
        return self.emit(eng, lambda e: e.nop(), deps=[(eng, tot)])

    def barrier_tokens(self):
        return [(e, self.cnt[e]) for e in self.ENGINES if self.cnt[e] > 0]

    def finalize(self, block):
        for e in self.ENGINES:
            ops = self.q[e]
            if not ops:
                continue

            def run(engine, ops=ops):
                for op in ops:
                    op(engine)
            attr = "gpsimd" if e == "sync2" else e
            getattr(block, attr)(run)


def ceil_div(a, b):
    return -(-a // b)


def build_ffn0(scale_b1=1.0):
    H = 4096
    HT = H // PT
    TB = 256
    NB = T // TB

    nc = bass.Bass("TRN2", target_bir_lowering=False)
    x_in = nc.dram_tensor("x_in", [D, T], F32, kind="ExternalInput")
    w1 = nc.dram_tensor("w1", [D, H], F32, kind="ExternalInput")
    b1c = nc.dram_tensor("b1c", [PT, HT], F32, kind="ExternalInput")
    w2 = nc.dram_tensor("w2", [H, D], F32, kind="ExternalInput")
    b2c = nc.dram_tensor("b2c", [PT, DT], F32, kind="ExternalInput")
    x_out = nc.dram_tensor("x_out", [D, T], F32, kind="ExternalOutput")

    from contextlib import ExitStack
    es = nc.ctx
    sb = lambda n, s, d=F32: es.enter_context(nc.sbuf_tensor(n, s, d))
    ps = lambda n, s, d=F32: es.enter_context(nc.psum_tensor(n, s, d))

    w1s = [sb(f"w1_{i}", [PT, H], BF16) for i in range(DT)]
    w2s = [sb(f"w2_{i}", [PT, D], BF16) for i in range(HT)]
    b1s = sb("b1s", [PT, HT])
    b2s = sb("b2s", [PT, DT])
    b2h = sb("b2h", [PT, DT])
    ones_col = sb("ones_col", [PT, 1], BF16)
    eps_col = sb("eps_col", [PT, 1])
    ones_row = sb("ones_row", [1, PT], BF16)

    x_sb = [sb(f"x_{i}", [PT, TB]) for i in range(DT)]
    xb16 = [sb(f"xb16_{i}", [PT, TB], BF16) for i in range(DT)]
    sq16 = [sb(f"sq16_{i}", [PT, TB], BF16) for i in range(DT)]
    xn = [sb(f"xn_{i}", [PT, TB], BF16) for i in range(DT)]
    h16 = [sb(f"h16_{i}", [PT, TB], BF16) for i in range(HT)]
    ssum = sb("ssum", [1, TB])
    ssq = sb("ssq", [1, TB])
    mrow = sb("mrow", [1, TB])[0:1, :]
    rrow = sb("rrow", [1, TB])[0:1, :]
    m16 = sb("m16", [1, TB], BF16)[0:1, :]
    r16 = sb("r16", [1, TB], BF16)[0:1, :]
    mb = sb("mb", [PT, TB])
    rb = sb("rb", [PT, TB])
    tmp = sb("tmpx", [PT, TB])
    xo = [sb(f"xo_{i}", [PT, TB]) for i in range(DT)]

    pstat_s = ps("pstat_s", [1, TB])
    pstat_q = ps("pstat_q", [1, TB])
    pbc = ps("pbc", [PT, TB])
    pbc2 = ps("pbc2", [PT, TB])
    pmm = [ps(f"pmm_{i}", [PT, TB]) for i in range(4)]

    p = Prog(nc)

    wtoks = []
    for i in range(DT):
        wtoks.append(p.emit("gpsimd", lambda g, i=i: g.dma_start(
            out=w1s[i][:, :], in_=w1[i * PT:(i + 1) * PT, :]), dma=True))
    for i in range(HT):
        wtoks.append(p.emit("gpsimd", lambda g, i=i: g.dma_start(
            out=w2s[i][:, :], in_=w2[i * PT:(i + 1) * PT, :]), dma=True))
    tb1 = p.emit("sync", lambda s: s.dma_start(out=b1s[:, :], in_=b1c[:, :]), dma=True)
    tb2 = p.emit("sync", lambda s: s.dma_start(out=b2s[:, :], in_=b2c[:, :]), dma=True)
    wgate = p.gate("gpsimd")
    sg0 = p.gate("sync")
    tb1, tb2 = sg0, sg0
    tb2h = p.emit("vector", lambda v: v.tensor_single_scalar(
        b2h[:, :], b2s[:, :], 0.5, ALU.mult), deps=[tb2])
    tco = p.emit("vector", lambda v: v.memset(ones_col[:, :], 1.0))
    tep = p.emit("vector", lambda v: v.memset(eps_col[:, :], LN_EPS))
    tro = p.emit("vector", lambda v: v.memset(ones_row[:, :], 1.0))

    w1tok = wgate
    w2tok = wgate

    prev_end = []
    for blk in range(NB):
        ts_ = slice(blk * TB, (blk + 1) * TB)
        for i in range(DT):
            p.emit("sync", lambda s, i=i, ts_=ts_: s.dma_start(
                out=x_sb[i][:, :], in_=x_in[i * PT:(i + 1) * PT, ts_]),
                deps=prev_end, dma=True)
        gx = p.gate("sync")
        xt = [gx] * DT
        cp = []
        for i in range(DT):
            c1 = p.emit("vector", lambda v, i=i: v.tensor_copy(
                xb16[i][:, :], x_sb[i][:, :]), deps=[xt[i]])
            c2 = p.emit("scalar", lambda s, i=i: s.activation(
                sq16[i][:, :], x_sb[i][:, :], AF.Square), deps=[xt[i]])
            cp.append((c1, c2))
        st = None
        for i in range(DT):
            st = p.emit("tensor", lambda t, i=i: t.matmul(
                pstat_s[0:1, :], ones_col[:, :], xb16[i][:, :],
                start=(i == 0), stop=(i == DT - 1)),
                deps=[cp[i][0], tco])
        for i in range(DT):
            st = p.emit("tensor", lambda t, i=i: t.matmul(
                pstat_q[0:1, :], ones_col[:, :], sq16[i][:, :],
                start=(i == 0), stop=(i == DT - 1)),
                deps=[cp[i][1], tco])
        e2 = p.emit("vector", lambda v: v.tensor_single_scalar(
            mrow, pstat_s[0:1, :], 1.0 / D, ALU.mult), deps=[st])
        e3 = p.emit("scalar", lambda s: s.activation(
            tmp[0:1, 0:TB], mrow, AF.Square), deps=[e2])
        e4 = p.emit("vector", lambda v: v.tensor_single_scalar(
            rrow, pstat_q[0:1, :], 1.0 / D, ALU.mult), deps=[st])
        e5 = p.emit("vector", lambda v: v.tensor_sub(
            rrow, rrow, tmp[0:1, 0:TB]), deps=[e4, e3])
        e6 = p.emit("scalar", lambda s: s.activation(
            tmp[0:1, 0:TB], rrow, AF.Sqrt, bias=eps_col[0:1, 0:1]), deps=[e5, tep])
        e7 = p.emit("vector", lambda v: v.reciprocal(rrow, tmp[0:1, 0:TB]),
                    deps=[e6])
        e8 = p.emit("vector", lambda v: v.tensor_copy(m16, mrow),
                    deps=[e2])
        e9 = p.emit("vector", lambda v: v.tensor_copy(r16, rrow),
                    deps=[e7])
        bc1 = p.emit("tensor", lambda t: t.matmul(
            pbc[:, :], ones_row[:, :], m16, start=True, stop=True),
            deps=[e8, tro])
        bc2 = p.emit("tensor", lambda t: t.matmul(
            pbc2[:, :], ones_row[:, :], r16, start=True, stop=True),
            deps=[e9, tro])
        bce1 = p.emit("vector", lambda v: v.tensor_copy(mb[:, :], pbc[:, :]),
                      deps=[bc1])
        bce2 = p.emit("vector", lambda v: v.tensor_copy(rb[:, :], pbc2[:, :]),
                      deps=[bc2])
        xnt = []
        for i in range(DT):
            s1 = p.emit("vector", lambda v, i=i: v.tensor_sub(
                tmp[:, :], x_sb[i][:, :], mb[:, :]), deps=[bce1, xt[i]])
            s2 = p.emit("vector", lambda v, i=i: v.tensor_mul(
                xn[i][:, :], tmp[:, :], rb[:, :]), deps=[bce2, s1])
            xnt.append(s2)
        htoks = []
        for j in range(HT):
            bank = pmm[j % 2]
            mt = None
            for k in range(DT):
                mt = p.emit("tensor", lambda t, j=j, k=k, bank=bank: t.matmul(
                    bank[:, :], w1s[k][:, j * PT:(j + 1) * PT], xn[k][:, :],
                    start=(k == 0), stop=(k == DT - 1)),
                    deps=[xnt[k], w1tok] + ([htoks[j - 2]] if j >= 2 else []))
            ht = p.emit("scalar", lambda s, j=j, bank=bank: s.activation(
                h16[j][:, :], bank[:, :], AF.Gelu, bias=b1s[:, j:j + 1],
                scale=scale_b1), deps=[mt, tb1])
            htoks.append(ht)
        otoks = []
        for i in range(DT):
            bank = pmm[2 + i % 2]
            mt = None
            for k in range(HT):
                mt = p.emit("tensor", lambda t, i=i, k=k, bank=bank: t.matmul(
                    bank[:, :], w2s[k][:, i * PT:(i + 1) * PT], h16[k][:, :],
                    start=(k == 0), stop=(k == HT - 1)),
                    deps=[htoks[k], w2tok] + ([otoks[i - 2][0]] if i >= 2 else []))
            o1 = p.emit("vector", lambda v, i=i, bank=bank: v.scalar_tensor_tensor(
                xo[i][:, :], bank[:, :], 0.5, x_sb[i][:, :],
                ALU.mult, ALU.add), deps=[mt])
            o2 = p.emit("vector", lambda v, i=i: v.tensor_scalar(
                xo[i][:, :], xo[i][:, :], b2h[:, i:i + 1], None,
                ALU.add), deps=[o1, tb2h])
            otoks.append((o1, o2))
            p.emit("sync2", lambda s, i=i, ts_=ts_: s.dma_start(
                out=x_out[i * PT:(i + 1) * PT, ts_], in_=xo[i][:, :]),
                deps=[o2], dma=True)
            if i == DT - 1:
                prev_end = [p.gate("sync2")]

    toks = p.barrier_tokens()
    p.emit("sync", lambda s: s.nop(), deps=toks)

    with nc.Block() as block:
        p.finalize(block)
    return nc


def _b2_scale_note():
    pass


def _prep_ffn0_inputs(xcore, w1, b1, w2, b2):
    b1c = np.ascontiguousarray(b1.reshape(32, 128).T)
    b2c = np.ascontiguousarray(b2.reshape(8, 128).T)
    return {
        "x_in": np.ascontiguousarray(xcore),
        "w1": np.ascontiguousarray(w1),
        "b1c": b1c,
        "w2": np.ascontiguousarray(w2),
        "b2c": b2c,
    }


_PROGS = {}


def _get_prog(name):
    if name not in _PROGS:
        _PROGS[name] = {"ffn0": build_ffn0, "mamba1": build_mamba1,
                        "moe_stats": build_moe_stats, "moe_main": build_moe_main,
                        "ssd": build_ssd}[name]()
    return _PROGS[name]


def build_mamba1(dbg=False):
    DI = 2048
    IT = DI // PT
    S = 16
    Tc = 512
    NCH = T // Tc
    EPS2 = float(np.finfo(np.float32).eps)

    nc = bass.Bass("TRN2", target_bir_lowering=False)
    x_in = nc.dram_tensor("x_in", [D, T], F32, kind="ExternalInput")
    nw_c = nc.dram_tensor("nw_c", [PT, DT], F32, kind="ExternalInput")
    ipT = nc.dram_tensor("ipT", [D, 2 * DI], F32, kind="ExternalInput")
    convw_c = nc.dram_tensor("convw_c", [PT, IT * 4], F32, kind="ExternalInput")
    convb_c = nc.dram_tensor("convb_c", [PT, IT], F32, kind="ExternalInput")
    xpT_r = nc.dram_tensor("xpT_r", [DI, 128], F32, kind="ExternalInput")
    dtpT = nc.dram_tensor("dtpT", [64, DI], F32, kind="ExternalInput")
    dtb_c = nc.dram_tensor("dtb_c", [PT, IT], F32, kind="ExternalInput")
    Alog_c = nc.dram_tensor("Alog_c", [PT, IT * S], F32, kind="ExternalInput")
    D_c = nc.dram_tensor("D_c", [PT, IT], F32, kind="ExternalInput")
    opT = nc.dram_tensor("opT", [DI, D], F32, kind="ExternalInput")
    ind_in = nc.dram_tensor("ind_in", [S, S * PT], F32, kind="ExternalInput")
    x_out = nc.dram_tensor("x_out", [D, T], F32, kind="ExternalOutput")
    if dbg:
        dbg_xn = nc.dram_tensor("dbg_xn", [D, 512], F32, kind="ExternalOutput")
        dbg_u = nc.dram_tensor("dbg_u", [2048, 512], F32, kind="ExternalOutput")
        dbg_delta = nc.dram_tensor("dbg_delta", [PT, 16 * 512], F32, kind="ExternalOutput")
        dbg_B = nc.dram_tensor("dbg_B", [16, 512], F32, kind="ExternalOutput")
        dbg_C = nc.dram_tensor("dbg_C", [16, 512], F32, kind="ExternalOutput")
        dbg_y = nc.dram_tensor("dbg_y", [2048, 512], F32, kind="ExternalOutput")
        dbg_hcar = nc.dram_tensor("dbg_hcar", [PT, 256], F32, kind="ExternalOutput")
        dbg_u1 = nc.dram_tensor("dbg_u1", [2048, 512], F32, kind="ExternalOutput")
        dbg_d1 = nc.dram_tensor("dbg_d1", [PT, 16 * 512], F32, kind="ExternalOutput")

    es = nc.ctx
    sb = lambda n, s, d=F32: es.enter_context(nc.sbuf_tensor(n, s, d))
    psf = lambda n, s, d=F32: es.enter_context(nc.psum_tensor(n, s, d))

    opTs = [sb(f"opT_{k}", [PT, D], BF16) for k in range(IT)]
    xpTs = [sb(f"xpT_{k}", [PT, 128], BF16) for k in range(IT)]
    dtpTs = sb("dtpTs", [64, DI], BF16)
    A_sb = sb("A_sb", [PT, IT * S])
    nw = sb("nw", [PT, DT])
    convw = sb("convw", [PT, IT * 4])
    convb = sb("convb", [PT, IT])
    dtb = sb("dtb", [PT, IT])
    Dp = sb("Dp", [PT, IT])
    ind16 = sb("ind16", [S, S * PT], BF16)
    ones_row = sb("ones_row", [1, PT], BF16)
    ones_col = sb("ones_colm", [PT, 1], BF16)
    epsc = sb("epsc", [PT, 1])
    eps2c = sb("eps2c", [PT, 1])
    ipg = [[sb(f"ipg_{b}_{k}", [PT, 512], BF16) for k in range(DT)] for b in range(2)]
    x_sb = [sb(f"x_{i}", [PT, Tc]) for i in range(DT)]
    xn16 = [sb(f"xn_{i}", [PT, Tc], BF16) for i in range(DT)]
    xq16 = [xn16[0], xn16[1]]
    xsb = [sb(f"xs_{i}", [PT, Tc + 4]) for i in range(IT)]
    halb = sb("halb", [PT, 3 * IT])
    z16 = [sb(f"z16_{i}", [PT, Tc], BF16) for i in range(IT)]
    u16 = [sb(f"u16_{i}", [PT, Tc], BF16) for i in range(IT)]
    delta = sb("delta", [PT, IT * Tc])
    dtraw16 = sb("dtraw16", [64, Tc], BF16)
    B16 = sb("B16", [S, Tc], BF16)
    C16 = sb("C16", [S, Tc], BF16)
    sc_a = sb("sc_a", [PT, Tc])
    sc_b = sb("sc_b", [PT, Tc])
    sc_c = sb("sc_c", [PT, Tc])
    sc_d = sb("sc_d", [PT, Tc])
    sc_a2 = sb("sc_a2", [PT, Tc])
    hcar = sb("hcar", [PT, IT * S])
    mrow = sb("m_row", [1, Tc])[0:1, :]
    vrow = sb("v_row", [1, Tc])[0:1, :]
    rrow = sb("r_row", [1, Tc])[0:1, :]
    prow = sb("p_row", [1, Tc])[0:1, :]
    m16 = sb("m16r", [1, Tc], BF16)[0:1, :]
    p16 = sb("p16r", [1, Tc], BF16)[0:1, :]
    r16m = p16

    pstat_s = psf("pstat_s", [1, Tc])
    pstat_q = psf("pstat_q", [1, Tc])
    pbc = psf("pbc", [PT, Tc])
    pbc2 = psf("pbc2", [PT, Tc])
    pk = [psf(f"pk_{i}", [PT, Tc]) for i in range(4)]

    p = Prog(nc)

    rt = []
    for k in range(IT):
        rt.append(p.emit("gpsimd", lambda g, k=k: g.dma_start(
            out=opTs[k][:, :], in_=opT[k * PT:(k + 1) * PT, :]), dma=True))
        rt.append(p.emit("gpsimd", lambda g, k=k: g.dma_start(
            out=xpTs[k][:, :], in_=xpT_r[k * PT:(k + 1) * PT, :]), dma=True))
    rt.append(p.emit("gpsimd", lambda g: g.dma_start(
        out=dtpTs[:, :], in_=dtpT[:, :]), dma=True))
    rt.append(p.emit("gpsimd", lambda g: g.dma_start(
        out=ind16[:, :], in_=ind_in[:, :]), dma=True))
    talog = p.emit("sync", lambda s: s.dma_start(out=A_sb[:, :], in_=Alog_c[:, :]), dma=True)
    for dst, src in [(nw, nw_c), (convw, convw_c), (convb, convb_c),
                     (dtb, dtb_c), (Dp, D_c)]:
        rt.append(p.emit("sync", lambda s, dst=dst, src=src: s.dma_start(
            out=dst[:, :], in_=src[:, :]), dma=True))
    rg = p.gate("gpsimd")
    sg0 = p.gate("sync")
    t1 = p.emit("scalar", lambda s: s.activation(A_sb[:, :], A_sb[:, :], AF.Exp),
                deps=[sg0])
    tA = p.emit("vector", lambda v: v.tensor_single_scalar(
        A_sb[:, :], A_sb[:, :], -1.0, ALU.mult), deps=[t1])
    tro = p.emit("vector", lambda v: v.memset(ones_row[:, :], 1.0))
    tco = p.emit("vector", lambda v: v.memset(ones_col[:, :], 1.0))
    tep = p.emit("vector", lambda v: v.memset(epsc[:, :], LN_EPS))
    tep2 = p.emit("vector", lambda v: v.memset(eps2c[:, :], EPS2))
    thc = p.emit("vector", lambda v: v.memset(hcar[:, :], 0.0))
    thal = p.emit("vector", lambda v: v.memset(halb[:, :], 0.0))
    rtok = rg

    bank_free = {}
    ipg_free = [[], []]
    prev_end = []

    for ch in range(NCH):
        ts_ = slice(ch * Tc, (ch + 1) * Tc)
        for i in range(DT):
            p.emit("sync", lambda s, i=i, ts_=ts_: s.dma_start(
                out=x_sb[i][:, :], in_=x_in[i * PT:(i + 1) * PT, ts_]),
                deps=prev_end, dma=True)
        gx = p.gate("sync")
        xt = [gx] * DT
        stat_toks = []
        for i in range(DT):
            c1 = p.emit("vector", lambda v, i=i: v.tensor_copy(
                xq16[0][:, :], x_sb[i][:, :]), deps=[xt[i]] + ([stat_toks[-1]] if stat_toks else []))
            c2 = p.emit("scalar", lambda s, i=i: s.activation(
                xq16[1][:, :], x_sb[i][:, :], AF.Square), deps=[xt[i]] + ([stat_toks[-1]] if stat_toks else []))
            m1 = p.emit("tensor", lambda t, i=i: t.matmul(
                pstat_s[0:1, :], ones_col[:, 0:1], xq16[0][:, :],
                start=(i == 0), stop=(i == DT - 1)),
                deps=[c1, tco] + ([bank_free["pstat_s"]] if i == 0 and "pstat_s" in bank_free else []))
            m2 = p.emit("tensor", lambda t, i=i: t.matmul(
                pstat_q[0:1, :], ones_col[:, 0:1], xq16[1][:, :],
                start=(i == 0), stop=(i == DT - 1)),
                deps=[c2] + ([bank_free["pstat_q"]] if i == 0 and "pstat_q" in bank_free else []))
            stat_toks.append(m2)
        st = stat_toks[-1]
        e2 = p.emit("vector", lambda v: v.tensor_single_scalar(
            mrow, pstat_s[0:1, :], 1.0 / D, ALU.mult), deps=[st])
        bank_free["pstat_s"] = e2
        e3 = p.emit("scalar", lambda s: s.activation(sc_a[0:1, 0:Tc], mrow, AF.Square),
                    deps=[e2])
        e4 = p.emit("vector", lambda v: v.tensor_single_scalar(
            vrow, pstat_q[0:1, :], 1.0 / D, ALU.mult), deps=[st])
        bank_free["pstat_q"] = e4
        e5 = p.emit("vector", lambda v: v.tensor_sub(vrow, vrow, sc_a[0:1, 0:Tc]),
                    deps=[e4, e3])
        e6 = p.emit("scalar", lambda s: s.activation(
            sc_a[0:1, 0:Tc], vrow, AF.Sqrt, bias=epsc[0:1, 0:1]), deps=[e5, tep])
        e7 = p.emit("vector", lambda v: v.reciprocal(rrow, sc_a[0:1, 0:Tc]), deps=[e6])
        e8 = p.emit("scalar", lambda s: s.activation(sc_a[0:1, 0:Tc], rrow, AF.Square),
                    deps=[e7])
        e9 = p.emit("vector", lambda v: v.tensor_mul(prow, vrow, sc_a[0:1, 0:Tc]),
                    deps=[e8, e5])
        e10 = p.emit("scalar", lambda s: s.activation(
            sc_a[0:1, 0:Tc], prow, AF.Sqrt, bias=eps2c[0:1, 0:1]), deps=[e9, tep2])
        e11 = p.emit("vector", lambda v: v.reciprocal(prow, sc_a[0:1, 0:Tc]), deps=[e10])
        e12 = p.emit("vector", lambda v: v.tensor_mul(prow, prow, rrow), deps=[e11, e7])
        c8 = p.emit("vector", lambda v: v.tensor_copy(m16, mrow), deps=[e2])
        c9 = p.emit("vector", lambda v: v.tensor_copy(p16, prow), deps=[e12])
        bc1 = p.emit("tensor", lambda t: t.matmul(
            pbc[:, :], ones_row[0:1, :], m16, start=True, stop=True),
            deps=[c8, tro] + ([bank_free["pbc"]] if "pbc" in bank_free else []))
        bc2 = p.emit("tensor", lambda t: t.matmul(
            pbc2[:, :], ones_row[0:1, :], p16, start=True, stop=True),
            deps=[c9] + ([bank_free["pbc2"]] if "pbc2" in bank_free else []))
        xnt = []
        last_s2 = None
        for i in range(DT):
            s1 = p.emit("vector", lambda v, i=i: v.tensor_sub(
                sc_a[:, :], x_sb[i][:, :], pbc[:, :]),
                deps=[bc1, xt[i]] + ([last_s2] if last_s2 else []))
            s2 = p.emit("vector", lambda v, i=i: v.scalar_tensor_tensor(
                xn16[i][:, :], sc_a[:, :], nw[:, i:i + 1], pbc2[:, :],
                ALU.mult, ALU.mult), deps=[bc2, s1])
            last_s2 = s2
            xnt.append(s2)
        bank_free["pbc"] = xnt[-1]
        bank_free["pbc2"] = xnt[-1]
        if dbg and ch == 0:
            for i in range(DT):
                p.emit("gpsimd", lambda s, i=i: s.dma_start(
                    out=dbg_xn[i * PT:(i + 1) * PT, :], in_=xn16[i][:, :]),
                    deps=[xnt[i]], dma=True)
        xz_ev = []
        for fg in range(8):
            buf = fg % 2
            for k in range(DT):
                p.emit("gpsimd", lambda g, k=k, fg=fg, buf=buf: g.dma_start(
                    out=ipg[buf][k][:, :],
                    in_=ipT[k * PT:(k + 1) * PT, fg * 512:(fg + 1) * 512]),
                    deps=ipg_free[buf], dma=True)
            gfg = p.gate("gpsimd")
            lt = [gfg] * DT
            ipg_free[buf] = []
            for jj in range(4):
                j = fg * 4 + jj
                bank = pk[jj % 2]
                bname = f"pk_{jj % 2}"
                mt = None
                for k in range(DT):
                    mt = p.emit("tensor", lambda t, k=k, jj=jj, buf=buf, bank=bank: t.matmul(
                        bank[:, :], ipg[buf][k][:, jj * PT:(jj + 1) * PT], xn16[k][:, :],
                        start=(k == 0), stop=(k == DT - 1)),
                        deps=[lt[k], xnt[k]] + ([bank_free[bname]] if bname in bank_free and k == 0 else []))
                ipg_free[buf].append(mt)
                if j < IT:
                    ev = p.emit("vector", lambda v, j=j, bank=bank: v.tensor_copy(
                        xsb[j][:, 3:3 + Tc], bank[:, :]), deps=[mt])
                else:
                    ev = p.emit("scalar", lambda s, j=j, bank=bank: s.activation(
                        z16[j - IT][:, :], bank[:, :], AF.Silu), deps=[mt])
                bank_free[bname] = ev
                xz_ev.append(ev)
        conv_toks = []
        for it in range(IT):
            h0 = p.emit("vector", lambda v, it=it: v.tensor_copy(
                xsb[it][:, 0:3], halb[:, it * 3:(it + 1) * 3]),
                deps=[xz_ev[it], thal] + ([conv_toks[-1][0]] if conv_toks else []))
            a1 = p.emit("vector", lambda v, it=it: v.tensor_scalar(
                sc_a[:, :], xsb[it][:, 0:Tc], convw[:, it * 4:it * 4 + 1], None,
                ALU.mult), deps=[h0])
            for kk in range(1, 4):
                a1 = p.emit("vector", lambda v, it=it, kk=kk: v.scalar_tensor_tensor(
                    sc_a[:, :], xsb[it][:, kk:kk + Tc], convw[:, it * 4 + kk:it * 4 + kk + 1],
                    sc_a[:, :], ALU.mult, ALU.add), deps=[a1])
            uu = p.emit("scalar", lambda s, it=it: s.activation(
                u16[it][:, :], sc_a[:, :], AF.Silu, bias=convb[:, it:it + 1]),
                deps=[a1])
            hs = p.emit("vector", lambda v, it=it: v.tensor_copy(
                halb[:, it * 3:(it + 1) * 3], xsb[it][:, Tc:Tc + 3]), deps=[xz_ev[it], h0])
            conv_toks.append((uu, hs))
        if dbg and ch == 0:
            for it in range(IT):
                p.emit("gpsimd", lambda s, it=it: s.dma_start(
                    out=dbg_u[it * PT:(it + 1) * PT, :], in_=u16[it][:, :]),
                    deps=[conv_toks[it][0]], dma=True)
        if dbg and ch == 1:
            for it in range(IT):
                p.emit("gpsimd", lambda s, it=it: s.dma_start(
                    out=dbg_u1[it * PT:(it + 1) * PT, :], in_=u16[it][:, :]),
                    deps=[conv_toks[it][0]], dma=True)
        mt = None
        for k in range(IT):
            mt = p.emit("tensor", lambda t, k=k: t.matmul(
                pk[2][:, :], xpTs[k][:, :], u16[k][:, :],
                start=(k == 0), stop=(k == IT - 1)),
                deps=[conv_toks[k][0], rtok] + ([bank_free["pk_2"]] if "pk_2" in bank_free and k == 0 else []))
        evC = p.emit("vector", lambda v: v.tensor_copy(C16[:, :], pk[2][0:S, :]), deps=[mt])
        evB = p.emit("vector", lambda v: v.tensor_copy(B16[:, :], pk[2][32:32 + S, :]), deps=[mt])
        evd = p.emit("vector", lambda v: v.tensor_copy(dtraw16[:, :], pk[2][64:128, :]), deps=[mt])
        bank_free["pk_2"] = evd
        dphase = []
        for it in range(IT):
            bank = pk[3 - (it % 2)]
            bname = f"pk_{3 - (it % 2)}"
            mt = p.emit("tensor", lambda t, it=it, bank=bank: t.matmul(
                bank[:, :], dtpTs[:, it * PT:(it + 1) * PT], dtraw16[:, :],
                start=True, stop=True),
                deps=[evd, rtok] + ([bank_free[bname]] if bname in bank_free else []))
            r1 = p.emit("vector", lambda v, it=it, bank=bank: v.tensor_scalar(
                sc_a[:, :], bank[:, :], dtb[:, it:it + 1], None, ALU.add),
                deps=[mt] + ([dphase[-1]] if dphase else []))
            bank_free[bname] = r1
            r2 = p.emit("scalar", lambda s: s.activation(sc_b[:, :], sc_a[:, :], AF.Abs),
                        deps=[r1])
            r3 = p.emit("scalar", lambda s: s.activation(sc_b[:, :], sc_b[:, :], AF.Exp,
                                                         scale=-1.0), deps=[r2])
            r4 = p.emit("scalar", lambda s: s.activation(sc_b[:, :], sc_b[:, :], AF.Ln,
                                                         bias=1.0), deps=[r3])
            r5 = p.emit("vector", lambda v: v.tensor_single_scalar(
                sc_a[:, :], sc_a[:, :], 0.0, ALU.max), deps=[r4, r1])
            r6 = p.emit("vector", lambda v, it=it: v.tensor_add(
                delta[:, it * Tc:(it + 1) * Tc], sc_a[:, :], sc_b[:, :]), deps=[r5])
            dphase.append(r6)
        if dbg and ch == 0:
            p.emit("gpsimd", lambda s: s.dma_start(
                out=dbg_B[:, :], in_=B16[:, :]), deps=[evB], dma=True)
            p.emit("gpsimd", lambda s: s.dma_start(
                out=dbg_C[:, :], in_=C16[:, :]), deps=[evC], dma=True)
            p.emit("gpsimd", lambda s: s.dma_start(
                out=dbg_delta[:, :], in_=delta[:, :]), deps=[dphase[-1]], dma=True)
        op_in_toks = []
        prev_scan = dphase[-1]
        bb_tok = {0: None, 1: None}
        for it in range(IT):
            dsl = delta[:, it * Tc:(it + 1) * Tc]
            dut = p.emit("vector", lambda v, it=it, dsl=dsl: v.tensor_mul(
                sc_d[:, :], dsl, u16[it][:, :]), deps=[dphase[it], conv_toks[it][0], prev_scan])
            yac = xsb[it]
            ytok = None
            sdep = {}
            for s_ in range(S):
                par = s_ % 2
                pb = pk[par]
                pbn = f"pk_{par}"
                pc = pk[2 + par]
                pcn = f"pk_{2 + par}"
                sa = sc_a if par == 0 else sc_a2
                sbb_ = sc_b
                scc = sc_c
                col = it * S + s_
                tb = p.emit("tensor", lambda t, s_=s_, pb=pb: t.matmul(
                    pb[:, :], ind16[:, s_ * PT:(s_ + 1) * PT], B16[:, :],
                    start=True, stop=True),
                    deps=[evB, rtok] + ([bank_free[pbn]] if pbn in bank_free else []))
                tc = p.emit("tensor", lambda t, s_=s_, pc=pc: t.matmul(
                    pc[:, :], ind16[:, s_ * PT:(s_ + 1) * PT], C16[:, :],
                    start=True, stop=True),
                    deps=[evC] + ([bank_free[pcn]] if pcn in bank_free else []))
                da = p.emit("scalar", lambda s, col=col, dsl=dsl, sa=sa: s.activation(
                    sa[:, :], dsl, AF.Exp, scale=A_sb[:, col:col + 1]),
                    deps=[tA, dut] + ([sdep[("h", par)]] if ("h", par) in sdep else []))
                db = p.emit("vector", lambda v, pb=pb, sbb_=sbb_: v.tensor_mul(
                    sbb_[:, :], sc_d[:, :], pb[:, :]),
                    deps=[tb, dut] + ([sdep[("y", par)]] if ("y", par) in sdep else []))
                bank_free[pbn] = db
                hk = p.emit("vector", lambda v, col=col, sa=sa, sbb_=sbb_, scc=scc: v.tensor_tensor_scan(
                    scc[:, :], sa[:, :], sbb_[:, :], hcar[:, col:col + 1],
                    ALU.mult, ALU.add), deps=[da, db, thc] +
                    ([sdep[("yt", par)]] if ("yt", par) in sdep else []))
                hc = p.emit("vector", lambda v, col=col, scc=scc: v.tensor_copy(
                    hcar[:, col:col + 1], scc[:, 511:512]), deps=[hk])
                if s_ == 0:
                    ytok = p.emit("vector", lambda v, yac=yac, pc=pc, scc=scc: v.tensor_mul(
                        yac[:, 0:Tc], scc[:, :], pc[:, :]),
                        deps=[hk, tc, conv_toks[it][1], hc])
                    sdep[("yt", par)] = ytok
                else:
                    t1_ = p.emit("vector", lambda v, pc=pc, scc=scc, sbb_=sbb_: v.tensor_mul(
                        sbb_[:, :], scc[:, :], pc[:, :]), deps=[hk, tc, hc])
                    ytok = p.emit("vector", lambda v, yac=yac, sbb_=sbb_: v.tensor_add(
                        yac[:, 0:Tc], yac[:, 0:Tc], sbb_[:, :]), deps=[t1_, ytok])
                    sdep[("yt", par)] = t1_
                sdep[("h", par)] = ytok
                sdep[("y", par)] = ytok
                bank_free[pcn] = ytok
            q0 = p.emit("vector", lambda v, it=it: v.tensor_copy(
                sc_b[:, :], u16[it][:, :]), deps=[ytok, conv_toks[it][0]])
            q1 = p.emit("vector", lambda v, it=it: v.tensor_scalar(
                sc_a[:, :], sc_b[:, :], Dp[:, it:it + 1], None, ALU.mult),
                deps=[q0])
            q2 = p.emit("vector", lambda v, yac=yac: v.tensor_add(
                yac[:, 0:Tc], yac[:, 0:Tc], sc_a[:, :]), deps=[q1, ytok])
            q3 = p.emit("vector", lambda v, it=it, yac=yac: v.tensor_mul(
                u16[it][:, :], yac[:, 0:Tc], z16[it][:, :]), deps=[q2])
            if dbg and ch == 0:
                p.emit("gpsimd", lambda s, it=it: s.dma_start(
                    out=dbg_y[it * PT:(it + 1) * PT, :], in_=u16[it][:, :]),
                    deps=[q3], dma=True)
            op_in_toks.append(q3)
            prev_scan = q3
        if dbg and ch == 1:
            p.emit("gpsimd", lambda s: s.dma_start(
                out=dbg_hcar[:, :], in_=hcar[:, :]), deps=[op_in_toks[-1]], dma=True)
            p.emit("gpsimd", lambda s: s.dma_start(
                out=dbg_d1[:, :], in_=delta[:, :]), deps=[dphase[-1]], dma=True)
        rb2 = p.emit("vector", lambda v: v.tensor_copy(r16m, rrow), deps=[prev_scan])
        rbc1 = p.emit("tensor", lambda t: t.matmul(
            pbc[:, :], ones_row[0:1, :], m16, start=True, stop=True),
            deps=[rb2, bank_free["pbc"]] if "pbc" in bank_free else [rb2])
        rbc2 = p.emit("tensor", lambda t: t.matmul(
            pbc2[:, :], ones_row[0:1, :], r16m, start=True, stop=True),
            deps=[rb2, bank_free["pbc2"]] if "pbc2" in bank_free else [rb2])
        for i in range(DT):
            bank = pk[i % 2]
            bname = f"pk_{i % 2}"
            mt = None
            for k in range(IT):
                mt = p.emit("tensor", lambda t, i=i, k=k, bank=bank: t.matmul(
                    bank[:, :], opTs[k][:, i * PT:(i + 1) * PT], u16[k][:, :],
                    start=(k == 0), stop=(k == IT - 1)),
                    deps=[op_in_toks[k], rtok] + ([bank_free[bname]] if bname in bank_free and k == 0 else []))
            l1 = p.emit("vector", lambda v, i=i: v.tensor_sub(
                sc_a[:, :], x_sb[i][:, :], pbc[:, :]),
                deps=[rbc1, prev_scan] + ([bank_free["lnr"]] if "lnr" in bank_free else []))
            l2 = p.emit("vector", lambda v, i=i: v.tensor_mul(
                sc_a[:, :], sc_a[:, :], pbc2[:, :]), deps=[rbc2, l1])
            o1 = p.emit("vector", lambda v, i=i, bank=bank: v.scalar_tensor_tensor(
                x_sb[i][:, :], bank[:, :], 1.0, x_sb[i][:, :], ALU.mult, ALU.add),
                deps=[mt, l2])
            o2 = p.emit("vector", lambda v, i=i: v.tensor_add(
                x_sb[i][:, :], x_sb[i][:, :], sc_a[:, :]), deps=[o1])
            bank_free[bname] = o1
            bank_free["lnr"] = o2
            p.emit("gpsimd", lambda s, i=i, ts_=ts_: s.dma_start(
                out=x_out[i * PT:(i + 1) * PT, ts_], in_=x_sb[i][:, :]),
                deps=[o2], dma=True)
            if i == DT - 1:
                prev_end = [p.gate("gpsimd")]
        bank_free["pbc"] = bank_free["lnr"]
        bank_free["pbc2"] = bank_free["lnr"]

    toks = p.barrier_tokens()
    p.emit("sync", lambda s: s.nop(), deps=toks)
    with nc.Block() as block:
        p.finalize(block)
    return nc


def _prep_mamba1_inputs(xcore, norm_w, in_proj_w, conv_w, conv_b, x_proj_w,
                        dt_proj_w, dt_proj_b, A_log, D_param, out_proj_w):
    DI, S, DTR = 2048, 16, 64
    xp = np.asarray(x_proj_w)
    xpT_r = np.zeros((DI, 128), np.float32)
    xpT_r[:, 0:16] = xp[80:96].T
    xpT_r[:, 32:48] = xp[64:80].T
    xpT_r[:, 64:128] = xp[0:64].T
    ind = np.zeros((S, S * 128), np.float32)
    for s in range(S):
        ind[s, s * 128:(s + 1) * 128] = 1.0
    return {
        "x_in": np.ascontiguousarray(xcore),
        "nw_c": np.ascontiguousarray(np.asarray(norm_w).reshape(8, 128).T),
        "ipT": np.ascontiguousarray(np.asarray(in_proj_w).T),
        "convw_c": np.ascontiguousarray(
            np.asarray(conv_w).reshape(16, 128, 4).transpose(1, 0, 2).reshape(128, 64)),
        "convb_c": np.ascontiguousarray(np.asarray(conv_b).reshape(16, 128).T),
        "xpT_r": xpT_r,
        "dtpT": np.ascontiguousarray(np.asarray(dt_proj_w).T),
        "dtb_c": np.ascontiguousarray(np.asarray(dt_proj_b).reshape(16, 128).T),
        "Alog_c": np.ascontiguousarray(
            np.asarray(A_log).reshape(16, 128, 16).transpose(1, 0, 2).reshape(128, 256)),
        "D_c": np.ascontiguousarray(np.asarray(D_param).reshape(16, 128).T),
        "opT": np.ascontiguousarray(np.asarray(out_proj_w).T),
        "ind_in": ind,
    }


def build_moe_stats():
    TB = 512
    NB = T // TB

    nc = bass.Bass("TRN2", target_bir_lowering=False)
    x_in = nc.dram_tensor("x_in", [D, T], F32, kind="ExternalInput")
    waffT = nc.dram_tensor("waffT", [D, 8], F32, kind="ExternalInput")
    xn_d = nc.dram_tensor("xn_d", [D, T], BF16, kind="ExternalOutput")
    aff_d = nc.dram_tensor("aff_d", [8, T], F32, kind="ExternalOutput")
    asum_d = nc.dram_tensor("asum_d", [1, 1], F32, kind="ExternalOutput")

    es = nc.ctx
    sb = lambda n, s, d=F32: es.enter_context(nc.sbuf_tensor(n, s, d))
    psf = lambda n, s, d=F32: es.enter_context(nc.psum_tensor(n, s, d))

    waf = [sb(f"waf_{k}", [PT, 8]) for k in range(DT)]
    wsq = sb("wsq", [PT, 8])
    wn8 = sb("wn8", [8, 1])
    ones_col = sb("ones_colS", [PT, 1])
    ones_row = sb("ones_rowS", [1, PT])
    eps_col = sb("eps_colS", [PT, 1])
    x_sb = [sb(f"x_{i}", [PT, TB]) for i in range(DT)]
    xsq = sb("xsq", [PT, TB])
    xnf = sb("xnf", [PT, TB])
    xn16 = [sb(f"xn_{i}", [PT, TB], BF16) for i in range(DT)]
    mrow = sb("m_row", [1, TB])[0:1, :]
    vrow = sb("v_row", [1, TB])[0:1, :]
    rrow = sb("r_row", [1, TB])[0:1, :]
    xnr = sb("xnr_row", [1, TB])[0:1, :]
    srow = sb("s_row", [1, TB])[0:1, :]
    aff_sb = sb("aff_sb", [8, TB])
    den_sb = sb("den_sb", [8, TB])
    asum8 = sb("asum8", [8, 1])
    acc8 = sb("acc8", [8, 1])
    ones8f = sb("ones8f", [8, 1])
    one1 = sb("one1", [1, 1])

    pstat_s = psf("pstat_s", [1, TB])
    pstat_q = psf("pstat_q", [1, TB])
    pbc = psf("pbc", [PT, TB])
    pbc2 = psf("pbc2", [PT, TB])
    pnum = psf("pnum", [8, TB])
    pden = psf("pden", [8, TB])
    pwn = psf("pwn", [8, 1])
    ptot = psf("ptot", [1, 1])

    p = Prog(nc)
    for k in range(DT):
        p.emit("sync", lambda s, k=k: s.dma_start(
            out=waf[k][:, :], in_=waffT[k * PT:(k + 1) * PT, :]), dma=True)
    gw = p.gate("sync")
    wts = [gw] * DT
    tco = p.emit("vector", lambda v: v.memset(ones_col[:, :], 1.0))
    tro = p.emit("vector", lambda v: v.memset(ones_row[:, :], 1.0))
    t8f = p.emit("vector", lambda v: v.memset(ones8f[:, :], 1.0))
    tep = p.emit("vector", lambda v: v.memset(eps_col[:, :], LN_EPS))
    tz8 = p.emit("vector", lambda v: v.memset(acc8[:, :], 0.0))
    mt = None
    for k in range(DT):
        tq = p.emit("scalar", lambda s, k=k: s.activation(
            wsq[:, :], waf[k][:, :], AF.Square), deps=[wts[k]] + ([mt] if mt else []))
        mt = p.emit("tensor", lambda t, k=k: t.matmul(
            pwn[:, 0:1], wsq[:, :], ones_col[:, 0:1],
            start=(k == 0), stop=(k == DT - 1)), deps=[tq, tco])
    tw1 = p.emit("scalar", lambda s: s.activation(wn8[:, :], pwn[:, 0:1], AF.Sqrt),
                 deps=[mt])

    prev_end = []
    asum_toks = []
    prev_stat_s = prev_stat_q = None
    xn_toks = []
    afft = dent = None
    for blk in range(NB):
        ts_ = slice(blk * TB, (blk + 1) * TB)
        for i in range(DT):
            p.emit("sync", lambda s, i=i, ts_=ts_: s.dma_start(
                out=x_sb[i][:, :], in_=x_in[i * PT:(i + 1) * PT, ts_]),
                deps=prev_end, dma=True)
        gx = p.gate("sync")
        xt = [gx] * DT
        prev_end = []
        stat_toks = []
        for i in range(DT):
            c2 = p.emit("scalar", lambda s, i=i: s.activation(
                xsq[:, :], x_sb[i][:, :], AF.Square),
                deps=[xt[i]] + ([stat_toks[-1]] if stat_toks else []))
            m1 = p.emit("tensor", lambda t, i=i: t.matmul(
                pstat_s[0:1, :], x_sb[i][:, 0:1].bitcast(F32) if False else ones_col[:, 0:1],
                x_sb[i][:, :], start=(i == 0), stop=(i == DT - 1)),
                deps=[xt[i], tco] + ([prev_stat_s] if i == 0 and blk > 0 else []))
            m2 = p.emit("tensor", lambda t, i=i: t.matmul(
                pstat_q[0:1, :], ones_col[:, 0:1], xsq[:, :],
                start=(i == 0), stop=(i == DT - 1)),
                deps=[c2] + ([prev_stat_q] if i == 0 and blk > 0 else []))
            stat_toks.append(m2)
        st = stat_toks[-1]
        e2 = p.emit("vector", lambda v: v.tensor_single_scalar(
            mrow, pstat_s[0:1, :], 1.0 / D, ALU.mult), deps=[st])
        prev_stat_s = e2
        e3 = p.emit("scalar", lambda s: s.activation(srow, mrow, AF.Square), deps=[e2])
        e4 = p.emit("vector", lambda v: v.tensor_single_scalar(
            vrow, pstat_q[0:1, :], 1.0 / D, ALU.mult), deps=[st])
        prev_stat_q = e4
        e5 = p.emit("vector", lambda v: v.tensor_sub(vrow, vrow, srow), deps=[e4, e3])
        e6 = p.emit("scalar", lambda s: s.activation(
            srow, vrow, AF.Sqrt, bias=eps_col[0:1, 0:1]), deps=[e5, tep])
        e7 = p.emit("vector", lambda v: v.reciprocal(rrow, srow), deps=[e6])
        e8 = p.emit("scalar", lambda s: s.activation(srow, rrow, AF.Square), deps=[e7])
        e9 = p.emit("vector", lambda v: v.tensor_mul(xnr, vrow, srow), deps=[e8, e5])
        e10 = p.emit("vector", lambda v: v.tensor_single_scalar(
            xnr, xnr, float(D), ALU.mult), deps=[e9])
        e11 = p.emit("scalar", lambda s: s.activation(xnr, xnr, AF.Sqrt), deps=[e10])
        bc1 = p.emit("tensor", lambda t: t.matmul(
            pbc[:, :], ones_row[0:1, :], mrow, start=True, stop=True),
            deps=[e2, tro] + ([xn_toks[-1]] if blk > 0 else []))
        bc2 = p.emit("tensor", lambda t: t.matmul(
            pbc2[:, :], ones_row[0:1, :], rrow, start=True, stop=True),
            deps=[e7] + ([xn_toks[-1]] if blk > 0 else []))
        bc3 = p.emit("tensor", lambda t: t.matmul(
            pden[0:8, :], ones_row[0:1, 0:8], xnr, start=True, stop=True),
            deps=[e11] + ([dent] if blk > 0 else []))
        xn_toks = []
        numt = None
        for i in range(DT):
            s1 = p.emit("vector", lambda v, i=i: v.tensor_sub(
                x_sb[i][:, :], x_sb[i][:, :], pbc[:, :]), deps=[bc1, xt[i]])
            s2 = p.emit("vector", lambda v, i=i: v.tensor_mul(
                xnf[:, :], x_sb[i][:, :], pbc2[:, :]),
                deps=[bc2, s1] + ([numt] if numt else []))
            s3 = p.emit("vector", lambda v, i=i: v.tensor_copy(
                xn16[i][:, :], xnf[:, :]), deps=[s2])
            xn_toks.append(s3)
            p.emit("gpsimd", lambda g, i=i, ts_=ts_: g.dma_start(
                out=xn_d[i * PT:(i + 1) * PT, ts_], in_=xn16[i][:, :]),
                deps=[s3], dma=True)
            numt = p.emit("tensor", lambda t, i=i: t.matmul(
                pnum[0:8, :], waf[i][:, :], xnf[:, :],
                start=(i == 0), stop=(i == DT - 1)),
                deps=[s2, wts[i]] + ([afft] if blk > 0 and i == 0 else []))
        d1 = p.emit("vector", lambda v: v.tensor_scalar(
            den_sb[0:8, :], pden[0:8, :], wn8[:, 0:1], None, ALU.mult),
            deps=[bc3, tw1])
        dent = d1
        d2 = p.emit("vector", lambda v: v.tensor_single_scalar(
            den_sb[0:8, :], den_sb[0:8, :], 1e-9, ALU.add), deps=[d1])
        d3 = p.emit("vector", lambda v: v.reciprocal(den_sb[0:8, :], den_sb[0:8, :]),
                    deps=[d2])
        a1 = p.emit("vector", lambda v: v.tensor_mul(
            aff_sb[0:8, :], pnum[0:8, :], den_sb[0:8, :]), deps=[numt, d3])
        afft = a1
        p.emit("sync", lambda s, ts_=ts_: s.dma_start(
            out=aff_d[:, ts_], in_=aff_sb[0:8, :]), deps=[a1], dma=True)
        prev_end = [p.gate("gpsimd"), p.gate("sync")]
        a2 = p.emit("vector", lambda v: v.tensor_scalar(
            den_sb[0:8, :], aff_sb[0:8, :], 0.0, 0.0, ALU.add, ALU.add,
            accum_out=asum8[:, 0:1]), deps=[a1, d3])
        a3 = p.emit("vector", lambda v: v.tensor_add(
            acc8[:, :], acc8[:, :], asum8[:, 0:1]), deps=[a2, tz8])
        asum_toks.append(a3)
    tt = p.emit("tensor", lambda t: t.matmul(
        ptot[0:1, 0:1], acc8[:, 0:1], ones8f[:, 0:1], start=True, stop=True),
        deps=[asum_toks[-1], t8f])
    tv = p.emit("vector", lambda v: v.tensor_copy(one1[0:1, 0:1], ptot[0:1, 0:1]),
                deps=[tt])
    p.emit("gpsimd", lambda g: g.dma_start(out=asum_d[0:1, 0:1], in_=one1[0:1, 0:1]),
           deps=[tv], dma=True)

    toks = p.barrier_tokens()
    p.emit("sync", lambda s: s.nop(), deps=toks)
    with nc.Block() as block:
        p.finalize(block)
    return nc


def build_moe_main(dbg=False):
    E = 8
    FH = 1024
    Tb = 512
    NB = T // Tb
    TT_ = T // PT
    SCAP = 2048.0
    MINC = 4.0

    nc = bass.Bass("TRN2", target_bir_lowering=False)
    x_in = nc.dram_tensor("x_in", [D, T], F32, kind="ExternalInput")
    xn_din = nc.dram_tensor("xn_din", [D, T], BF16, kind="ExternalInput")
    aff_din = nc.dram_tensor("aff_din", [E, T], F32, kind="ExternalInput")
    parts = nc.dram_tensor("parts", [E, E], F32, kind="ExternalInput")
    thr_in = nc.dram_tensor("thr_in", [E, 1], F32, kind="ExternalInput")
    W1 = nc.dram_tensor("W1", [E, D, 4096], F32, kind="ExternalInput")
    W2 = nc.dram_tensor("W2", [E, 4096, D], F32, kind="ExternalInput")
    b1c = nc.dram_tensor("b1c", [PT, E * 32], F32, kind="ExternalInput")
    b2r = nc.dram_tensor("b2r", [E, D], F32, kind="ExternalInput")
    iota8_in = nc.dram_tensor("iota8_in", [PT, E], F32, kind="ExternalInput")
    iotaB_in = nc.dram_tensor("iotaB_in", [PT, E], F32, kind="ExternalInput")
    ident_in = nc.dram_tensor("ident_in", [PT, PT], F32, kind="ExternalInput")
    x_out = nc.dram_tensor("x_out", [D, T], F32, kind="ExternalOutput")
    if dbg:
        dbg_rows = nc.dram_tensor("dbg_rows", [8, 8], F32, kind="ExternalOutput")
        dbg_mtok = nc.dram_tensor("dbg_mtok", [PT, 256], F32, kind="ExternalOutput")
        dbg_acc = nc.dram_tensor("dbg_acc", [PT, D], F32, kind="ExternalOutput")
        dbg_csum = nc.dram_tensor("dbg_csum", [1, D], F32, kind="ExternalOutput")

    from contextlib import ExitStack
    es = nc.ctx
    sb = lambda n, s, d=F32: es.enter_context(nc.sbuf_tensor(n, s, d))
    psf = lambda n, s, d=F32: es.enter_context(nc.psum_tensor(n, s, d))

    xn16 = [sb(f"xn_{i}", [PT, T], BF16) for i in range(DT)]
    acc16 = [sb(f"acc_{tt}", [PT, D], BF16) for tt in range(TT_)]
    m_tok = sb("m_tok", [PT, TT_ * E])
    b1s = sb("b1s", [PT, E * 32])
    g1all = sb("g1all", [PT, E * 32], BF16)
    csum = sb("csum", [1, D])[0:1, :]
    crow = sb("crow", [1, D])[0:1, :]
    negc16 = sb("negc16", [1, D], BF16)[0:1, :]
    ones_row = sb("ones_rowM", [1, PT], BF16)
    ident16 = sb("identM", [PT, PT])
    rows8 = {}
    for nm in ["cap8", "lo8", "hi8", "mid8", "cnt8", "ge8", "t8a", "t8b",
               "rem8", "gt8"]:
        rows8[nm] = sb(nm, [8, 1])
    parts_sb = sb("parts_sb", [E, E])
    thr8 = sb("thr8", [E, 1])
    tot8 = sb("tot8", [E, 1])
    cri8 = sb("cri8", [E, 1])

    psA = psf("psA", [PT, Tb])
    psB = psf("psB", [PT, Tb])
    psC = psf("psC", [PT, Tb])
    psD = psf("psD", [PT, Tb])
    psE = psf("psE", [PT, Tb])
    psF = psf("psF", [PT, Tb])
    psR = psf("psR", [1, 512])
    psT = psf("psT", [PT, PT])

    p = Prog(nc)

    for i in range(DT):
        p.emit("sync", lambda s, i=i: s.dma_start(
            out=xn16[i][:, :], in_=xn_din[i * PT:(i + 1) * PT, :]), dma=True)
    p.emit("sync", lambda s: s.dma_start(out=b1s[:, :], in_=b1c[:, :]), dma=True)
    p.emit("sync", lambda s: s.dma_start(out=ident16[:, :], in_=ident_in[:, :]),
           dma=True)
    p.emit("sync", lambda s: s.dma_start(out=parts_sb[:, :], in_=parts[:, :]),
           dma=True)
    p.emit("sync", lambda s: s.dma_start(out=thr8[:, :], in_=thr_in[:, :]),
           dma=True)
    g0 = p.gate("sync")
    xnt = [g0] * DT
    tb1 = tid = tpt = tth = g0
    tg1 = p.emit("scalar", lambda s: s.activation(g1all[:, :], b1s[:, :], AF.Gelu),
                 deps=[tb1])
    tro = p.emit("vector", lambda v: v.memset(ones_row[:, :], 1.0))
    tcz = p.emit("vector", lambda v: v.memset(csum, 0.0))

    rs = ExitStack()
    aff_sb = rs.enter_context(nc.sbuf_tensor("aff_sb", [E, T], F32))
    eqm = rs.enter_context(nc.sbuf_tensor("eqm", [E, T], F32))
    pfx = rs.enter_context(nc.sbuf_tensor("pfx", [E, T], F32))
    gtm = rs.enter_context(nc.sbuf_tensor("gtm", [E, T], F32))
    affT = rs.enter_context(nc.sbuf_tensor("affT", [PT, TT_ * E], F32))
    ecrT = rs.enter_context(nc.sbuf_tensor("ecrT", [PT, TT_ * E], F32))
    io8 = rs.enter_context(nc.sbuf_tensor("io8", [PT, E], F32))
    ioB = rs.enter_context(nc.sbuf_tensor("ioB", [PT, E], F32))
    tcv = rs.enter_context(nc.sbuf_tensor("tcv", [PT, E], F32))
    mx1 = rs.enter_context(nc.sbuf_tensor("mx1", [PT, 1], F32))

    p.emit("sync", lambda s: s.dma_start(out=aff_sb[:, :], in_=aff_din[:, :]),
           dma=True)
    p.emit("sync", lambda s: s.dma_start(out=io8[:, :], in_=iota8_in[:, :]),
           dma=True)
    p.emit("sync", lambda s: s.dma_start(out=ioB[:, :], in_=iotaB_in[:, :]),
           dma=True)
    g1r = p.gate("sync")
    ta = ti1 = ti2 = g1r
    s1 = p.emit("vector", lambda v: v.tensor_reduce(
        tot8[:, 0:1], parts_sb[:, :], AX.X, ALU.add), deps=[tpt])
    s2 = p.emit("vector", lambda v: v.tensor_single_scalar(
        tot8[:, 0:1], tot8[:, 0:1], 1.0 / (NCORES * T * E), ALU.mult), deps=[s1])
    s3 = p.emit("vector", lambda v: v.tensor_sub(
        tot8[:, 0:1], tot8[:, 0:1], thr8[:, 0:1]), deps=[s2, tth])
    s4 = p.emit("scalar", lambda s: s.activation(
        tot8[:, 0:1], tot8[:, 0:1], AF.Sigmoid), deps=[s3])
    s5 = p.emit("vector", lambda v: v.tensor_single_scalar(
        tot8[:, 0:1], tot8[:, 0:1], SCAP, ALU.mult), deps=[s4])
    s6 = p.emit("vector", lambda v: v.tensor_copy(
        thr8[:, 0:1].bitcast(mybir.dt.int32), tot8[:, 0:1]), deps=[s5, s3])
    s7 = p.emit("vector", lambda v: v.tensor_copy(
        cri8[:, 0:1], thr8[:, 0:1].bitcast(mybir.dt.int32)), deps=[s6])
    s8 = p.emit("vector", lambda v: v.tensor_tensor(
        thr8[:, 0:1], cri8[:, 0:1], tot8[:, 0:1], ALU.is_gt), deps=[s7, s5])
    s9 = p.emit("vector", lambda v: v.tensor_sub(
        rows8["cap8"][:, 0:1], cri8[:, 0:1], thr8[:, 0:1]), deps=[s8])
    s10 = p.emit("vector", lambda v: v.tensor_single_scalar(
        rows8["cap8"][:, 0:1], rows8["cap8"][:, 0:1], MINC, ALU.max), deps=[s9])
    cp8 = p.emit("vector", lambda v: v.tensor_single_scalar(
        rows8["cap8"][:, 0:1], rows8["cap8"][:, 0:1], SCAP, ALU.min), deps=[s10])
    l0 = p.emit("vector", lambda v: v.tensor_reduce(
        rows8["lo8"][:, 0:1], aff_sb[:, :], AX.X, ALU.min), deps=[ta])
    l1 = p.emit("vector", lambda v: v.tensor_single_scalar(
        rows8["lo8"][:, 0:1], rows8["lo8"][:, 0:1], 1e-3, ALU.subtract), deps=[l0])
    h1 = p.emit("vector", lambda v: v.tensor_reduce(
        rows8["hi8"][:, 0:1], aff_sb[:, :], AX.X, ALU.max), deps=[ta, l0])
    last = h1
    for _ in range(34):
        b1_ = p.emit("vector", lambda v: v.tensor_add(
            rows8["mid8"][:, 0:1], rows8["lo8"][:, 0:1], rows8["hi8"][:, 0:1]),
            deps=[last])
        b2_ = p.emit("vector", lambda v: v.tensor_single_scalar(
            rows8["mid8"][:, 0:1], rows8["mid8"][:, 0:1], 0.5, ALU.mult), deps=[b1_])
        b3_ = p.emit("vector", lambda v: v.tensor_scalar(
            eqm[:, :], aff_sb[:, :], rows8["mid8"][:, 0:1], 0.0,
            ALU.is_gt, ALU.add, accum_out=rows8["cnt8"][:, 0:1]), deps=[b2_, ta])
        b4_ = p.emit("vector", lambda v: v.tensor_tensor(
            rows8["ge8"][:, 0:1], rows8["cnt8"][:, 0:1], rows8["cap8"][:, 0:1],
            ALU.is_ge), deps=[b3_, cp8])
        b5_ = p.emit("vector", lambda v: v.tensor_sub(
            rows8["t8a"][:, 0:1], rows8["mid8"][:, 0:1], rows8["lo8"][:, 0:1]),
            deps=[b4_])
        b6_ = p.emit("vector", lambda v: v.tensor_mul(
            rows8["t8a"][:, 0:1], rows8["t8a"][:, 0:1], rows8["ge8"][:, 0:1]),
            deps=[b5_])
        b7_ = p.emit("vector", lambda v: v.tensor_add(
            rows8["lo8"][:, 0:1], rows8["lo8"][:, 0:1], rows8["t8a"][:, 0:1]),
            deps=[b6_])
        b8_ = p.emit("vector", lambda v: v.tensor_sub(
            rows8["t8b"][:, 0:1], rows8["hi8"][:, 0:1], rows8["mid8"][:, 0:1]),
            deps=[b4_])
        b9_ = p.emit("vector", lambda v: v.tensor_mul(
            rows8["t8b"][:, 0:1], rows8["t8b"][:, 0:1], rows8["ge8"][:, 0:1]),
            deps=[b8_])
        b10_ = p.emit("vector", lambda v: v.tensor_add(
            rows8["hi8"][:, 0:1], rows8["mid8"][:, 0:1], rows8["t8b"][:, 0:1]),
            deps=[b9_, b7_])
        last = b10_
    g1_ = p.emit("vector", lambda v: v.tensor_scalar(
        gtm[:, :], aff_sb[:, :], rows8["hi8"][:, 0:1], 0.0,
        ALU.is_gt, ALU.add, accum_out=rows8["gt8"][:, 0:1]), deps=[last])
    g2_ = p.emit("vector", lambda v: v.tensor_sub(
        rows8["rem8"][:, 0:1], rows8["cap8"][:, 0:1], rows8["gt8"][:, 0:1]),
        deps=[g1_])
    g3_ = p.emit("vector", lambda v: v.tensor_scalar(
        eqm[:, :], aff_sb[:, :], rows8["hi8"][:, 0:1], None,
        ALU.is_equal), deps=[g1_])
    g4_ = p.emit("vector", lambda v: v.tensor_tensor_scan(
        pfx[:, :], eqm[:, :], eqm[:, :], 0.0, ALU.add, ALU.bypass), deps=[g3_])
    g5_ = p.emit("vector", lambda v: v.tensor_scalar(
        pfx[:, :], pfx[:, :], rows8["rem8"][:, 0:1], None, ALU.is_le), deps=[g4_, g2_])
    g6_ = p.emit("vector", lambda v: v.tensor_mul(pfx[:, :], pfx[:, :], eqm[:, :]),
                 deps=[g5_])
    g7_ = p.emit("vector", lambda v: v.tensor_add(gtm[:, :], gtm[:, :], pfx[:, :]),
                 deps=[g6_])
    mtoks = []
    for tt in range(TT_):
        t1_ = p.emit("tensor", lambda t, tt=tt: t.transpose(
            psT[:, 0:8], aff_sb[:, tt * PT:(tt + 1) * PT], ident16[0:8, 0:8]),
            deps=[ta, tid] + ([mtoks[-1]] if mtoks else []))
        t2_ = p.emit("vector", lambda v, tt=tt: v.tensor_copy(
            affT[:, tt * E:(tt + 1) * E], psT[:, 0:8]), deps=[t1_])
        t3_ = p.emit("tensor", lambda t, tt=tt: t.transpose(
            psT[:, 8:16], gtm[:, tt * PT:(tt + 1) * PT], ident16[0:8, 0:8]),
            deps=[g7_, t2_])
        t4_ = p.emit("vector", lambda v, tt=tt: v.tensor_copy(
            ecrT[:, tt * E:(tt + 1) * E], psT[:, 8:16]), deps=[t3_])
        t5_ = p.emit("vector", lambda v, tt=tt: v.tensor_reduce(
            mx1[:, 0:1], affT[:, tt * E:(tt + 1) * E], AX.X, ALU.max), deps=[t2_])
        t6_ = p.emit("vector", lambda v, tt=tt: v.tensor_scalar(
            tcv[:, :], affT[:, tt * E:(tt + 1) * E], mx1[:, 0:1], None,
            ALU.is_ge), deps=[t5_])
        t7_ = p.emit("vector", lambda v: v.tensor_mul(tcv[:, :], tcv[:, :], ioB[:, :]),
                     deps=[t6_, ti2])
        t8_ = p.emit("vector", lambda v: v.tensor_single_scalar(
            tcv[:, :], tcv[:, :], 999.0, ALU.add), deps=[t7_])
        t9_ = p.emit("vector", lambda v: v.tensor_reduce(
            mx1[:, 0:1], tcv[:, :], AX.X, ALU.min), deps=[t8_])
        t10_ = p.emit("vector", lambda v, tt=tt: v.tensor_scalar(
            tcv[:, :], io8[:, :], mx1[:, 0:1], None, ALU.is_equal),
            deps=[t9_, ti1])
        t11_ = p.emit("vector", lambda v, tt=tt: v.tensor_mul(
            m_tok[:, tt * E:(tt + 1) * E], tcv[:, :], ecrT[:, tt * E:(tt + 1) * E]),
            deps=[t10_, t4_])
        mtoks.append(t11_)
    route_done = mtoks[-1]
    if dbg:
        for j, nm in enumerate(["cap8", "hi8", "gt8", "rem8", "cnt8", "lo8"]):
            p.emit("gpsimd", lambda g, j=j, nm=nm: g.dma_start(
                out=dbg_rows[:, j:j + 1], in_=rows8[nm][:, 0:1],
                allow_slow_non_contiguous=True),
                deps=[route_done], dma=True)
        p.emit("gpsimd", lambda g: g.dma_start(
            out=dbg_mtok[:, :], in_=m_tok[:, :]), deps=[route_done], dma=True)
    rs.close()

    es2 = ExitStack()
    w1b = [es2.enter_context(nc.sbuf_tensor(f"w1b_{k}", [PT, FH], BF16)) for k in range(DT)]
    w2b = [es2.enter_context(nc.sbuf_tensor(f"w2b_{k}", [PT, D], BF16)) for k in range(8)]
    H16 = [es2.enter_context(nc.sbuf_tensor(f"H_{j}", [PT, Tb], BF16)) for j in range(8)]
    acc_init = []
    for tt in range(TT_):
        acc_init.append(p.emit("vector", lambda v, tt=tt: v.memset(
            acc16[tt][:, :], 0.0)))
    bankA = {"psA": None, "psB": None}
    bankG = {}
    w1_free, w2_free = [route_done], [route_done]
    csum_tok = tcz
    negc_tok = None
    for e in range(E):
        for fh in range(4):
            for k in range(DT):
                p.emit("gpsimd", lambda g, k=k, e=e, fh=fh: g.dma_start(
                    out=w1b[k][:, :], in_=W1[e, k * PT:(k + 1) * PT,
                                             fh * FH:(fh + 1) * FH]),
                    deps=w1_free, dma=True)
            for k in range(8):
                p.emit("gpsimd", lambda g, k=k, e=e, fh=fh: g.dma_start(
                    out=w2b[k][:, :], in_=W2[e, fh * FH + k * PT: fh * FH + (k + 1) * PT, :]),
                    deps=w2_free, dma=True)
            gwt = p.gate("gpsimd")
            wt1 = [gwt] * DT
            wt2 = [gwt] * 8
            w1_free, w2_free = [], []
            cv = None
            for dblk in range(2):
                mt = None
                for k in range(8):
                    mt = p.emit("tensor", lambda t, e=e, fh=fh, k=k, dblk=dblk: t.matmul(
                        psR[0:1, :],
                        g1all[:, e * 32 + fh * 8 + k:e * 32 + fh * 8 + k + 1],
                        w2b[k][:, dblk * 512:(dblk + 1) * 512],
                        start=(k == 0), stop=(k == 7)),
                        deps=[wt2[k], tg1] + ([csum_tok if dblk == 0 else cv]
                                              if k == 0 else []))
                cv = p.emit("vector", lambda v, dblk=dblk: v.tensor_copy(
                    crow[0:1, dblk * 512:(dblk + 1) * 512], psR[0:1, :]), deps=[mt])
            csum_tok = p.emit("vector", lambda v: v.tensor_add(csum, csum, crow),
                              deps=[cv, csum_tok])
            ng = p.emit("vector", lambda v: v.tensor_single_scalar(
                negc16, crow, -1.0, ALU.mult), deps=[cv] + ([negc_tok] if negc_tok else []))
            negc_tok = ng
            mm2_done = None
            for tb in range(NB):
                tsl = slice(tb * Tb, (tb + 1) * Tb)
                htoks = []
                for j in range(8):
                    bank = psA if j % 2 == 0 else psB
                    bname = "psA" if j % 2 == 0 else "psB"
                    mt = None
                    for k in range(DT):
                        mt = p.emit("tensor", lambda t, j=j, k=k, e=e, fh=fh, bank=bank, tsl=tsl: t.matmul(
                            bank[:, :], w1b[k][:, j * PT:(j + 1) * PT],
                            xn16[k][:, tsl], start=(k == 0), stop=(k == DT - 1)),
                            deps=[wt1[k], xnt[k]] +
                            ([bankA[bname]] if bankA[bname] and k == 0 else []))
                    ht = p.emit("scalar", lambda s, j=j, e=e, fh=fh, bank=bank: s.activation(
                        H16[j][:, :], bank[:, :], AF.Gelu,
                        bias=b1s[:, e * 32 + fh * 8 + j:e * 32 + fh * 8 + j + 1]),
                        deps=[mt, tb1] + ([mm2_done] if mm2_done else []))
                    bankA[bname] = ht
                    htoks.append(ht)
                w1_free.extend(htoks[-2:])
                for tt in range(4):
                    gtt = tb * 4 + tt
                    for dh in range(2):
                        gname = f"b{(tt * 2 + dh) % 4}"
                        bank = [psC, psD, psE, psF][(tt * 2 + dh) % 4]
                        mt = None
                        for k in range(8):
                            mt = p.emit("tensor", lambda t, k=k, tt=tt, dh=dh, bank=bank: t.matmul(
                                bank[:, :], H16[k][:, tt * PT:(tt + 1) * PT],
                                w2b[k][:, dh * 512:(dh + 1) * 512],
                                start=(k == 0), stop=False),
                                deps=[htoks[k], wt2[k]] +
                                ([bankG[gname]] if bankG.get(gname) and k == 0 else []))
                        mt = p.emit("tensor", lambda t, dh=dh, bank=bank: t.matmul(
                            bank[:, :], ones_row[0:1, :],
                            negc16[0:1, dh * 512:(dh + 1) * 512],
                            start=False, stop=True), deps=[ng, tro])
                        au = p.emit("vector", lambda v, gtt=gtt, dh=dh, e=e, bank=bank: v.scalar_tensor_tensor(
                            acc16[gtt][:, dh * 512:(dh + 1) * 512], bank[:, :],
                            m_tok[:, gtt * E + e:gtt * E + e + 1],
                            acc16[gtt][:, dh * 512:(dh + 1) * 512],
                            ALU.mult, ALU.add),
                            deps=[mt, route_done, acc_init[gtt]])
                        bankG[gname] = au
                        mm2_done = au
                        if tb == NB - 1:
                            w2_free.append(au)
    if dbg:
        p.emit("gpsimd", lambda g: g.dma_start(
            out=dbg_acc[:, :], in_=acc16[0][:, :]),
            deps=list(bankG.values()), dma=True)
        p.emit("gpsimd", lambda g: g.dma_start(
            out=dbg_csum[0:1, :], in_=csum), deps=[csum_tok], dma=True)
    es2.close()
    expert_done = list(bankG.values()) + [csum_tok]
    b2r16 = sb("b2r16", [E, D], BF16)
    cst16 = sb("cst16", [1, D], BF16)[0:1, :]
    cstb = sb("cstb", [PT, D])
    xfin = [sb(f"xf_{i}", [PT, PT]) for i in range(DT)]
    tmpf = sb("tmpf", [PT, D])
    ones8c = sb("ones8c", [E, 1], BF16)
    p.emit("gpsimd", lambda g: g.dma_start(out=b2r16[:, :], in_=b2r[:, :]),
           deps=expert_done, dma=True)
    tb2 = p.gate("gpsimd")
    t8c = p.emit("vector", lambda v: v.memset(ones8c[:, :], 1.0))
    cv1 = None
    for dblk in range(2):
        mt = p.emit("tensor", lambda t, dblk=dblk: t.matmul(
            psR[0:1, :], ones8c[:, 0:1], b2r16[:, dblk * 512:(dblk + 1) * 512],
            start=True, stop=True),
            deps=[tb2, t8c, csum_tok] + ([cv1] if cv1 else []))
        cv1 = p.emit("vector", lambda v, dblk=dblk: v.tensor_add(
            crow[0:1, dblk * 512:(dblk + 1) * 512], csum[0:1, dblk * 512:(dblk + 1) * 512],
            psR[0:1, :]), deps=[mt, csum_tok])
    cv2 = p.emit("vector", lambda v: v.tensor_copy(cst16, crow), deps=[cv1])
    cb_toks = []
    for c_ in range(DT):
        bk = p.emit("tensor", lambda t, c_=c_: t.matmul(
            psT[:, 0:PT], ones_row[0:1, :], cst16[0:1, c_ * PT:(c_ + 1) * PT],
            start=True, stop=True),
            deps=[cv2, tro] + ([cb_toks[-1]] if cb_toks else []))
        ck = p.emit("vector", lambda v, c_=c_: v.tensor_copy(
            cstb[:, c_ * PT:(c_ + 1) * PT], psT[:, 0:PT]), deps=[bk])
        cb_toks.append(ck)
    cst_done = cb_toks[-1]
    pv_tok = None
    for tt in range(TT_):
        for i in range(DT):
            p.emit("sync", lambda s, i=i, tt=tt: s.dma_start(
                out=xfin[i][:, :], in_=x_in[i * PT:(i + 1) * PT, tt * PT:(tt + 1) * PT]),
                deps=([pv_tok] if pv_tok else expert_done), dma=True)
        gxt = p.gate("sync")
        xt = [gxt] * DT
        a0 = p.emit("vector", lambda v, tt=tt: v.tensor_add(
            tmpf[:, :], acc16[tt][:, :], cstb[:, :]),
            deps=[cst_done] + list(bankG.values()) +
            ([pv_tok] if pv_tok else []))
        otoks = []
        for i in range(DT):
            tr = p.emit("tensor", lambda t, i=i: t.transpose(
                psT[:, 0:PT], tmpf[:, i * PT:(i + 1) * PT], ident16[:, :]),
                deps=[a0] + ([otoks[-1][1]] if otoks else [cst_done]))
            o1 = p.emit("vector", lambda v, i=i: v.scalar_tensor_tensor(
                xfin[i][:, :], psT[:, 0:PT], 0.5, xfin[i][:, :], ALU.mult, ALU.add),
                deps=[tr, xt[i]])
            p.emit("gpsimd", lambda g, i=i, tt=tt: g.dma_start(
                out=x_out[i * PT:(i + 1) * PT, tt * PT:(tt + 1) * PT],
                in_=xfin[i][:, :]), deps=[o1], dma=True)
            otoks.append((tr, o1))
        pv_tok = p.gate("gpsimd")

    toks = p.barrier_tokens()
    p.emit("sync", lambda s: s.nop(), deps=toks)
    with nc.Block() as block:
        p.finalize(block)
    return nc


def _prep_moe_main_inputs(xcore, xn_core, aff_core, parts8, thr, W1, b1, W2, b2):
    E = 8
    b1c = np.ascontiguousarray(
        np.asarray(b1).reshape(E, 32, 128).transpose(2, 0, 1).reshape(128, E * 32))
    io8 = np.broadcast_to(np.arange(E, dtype=np.float32), (128, E)).copy()
    ioB = io8 - 999.0
    return {
        "x_in": np.ascontiguousarray(xcore),
        "xn_din": np.ascontiguousarray(xn_core),
        "aff_din": np.ascontiguousarray(aff_core),
        "parts": np.ascontiguousarray(
            np.broadcast_to(np.asarray(parts8, np.float32).reshape(1, 8), (8, 8))),
        "thr_in": np.ascontiguousarray(
            np.broadcast_to(np.asarray(thr, np.float32).reshape(1, 1), (8, 1))),
        "W1": np.ascontiguousarray(W1),
        "W2": np.ascontiguousarray(W2),
        "b1c": b1c,
        "b2r": np.ascontiguousarray(b2),
        "iota8_in": io8,
        "iotaB_in": ioB,
        "ident_in": np.eye(128, dtype=np.float32),
    }


def build_ssd(nch=None):
    DI = 2048
    IT = DI // PT
    HN = 1024
    HT = HN // PT
    NH = 16
    Tc = 512
    NCH = (T // Tc) if nch is None else nch
    QC = 32
    NSC = Tc // QC
    RTOT = 6208
    EPS2 = 1e-6
    DT_MIN, DT_MAX = 1e-4, 1.0

    nc = bass.Bass("TRN2", target_bir_lowering=False)
    x_in = nc.dram_tensor("x_in", [D, T], F32, kind="ExternalInput")
    ipT2 = nc.dram_tensor("ipT2", [D, RTOT], F32, kind="ExternalInput")
    convw_c = nc.dram_tensor("convw_c", [PT, IT * 4], F32, kind="ExternalInput")
    convb_c = nc.dram_tensor("convb_c", [PT, IT], F32, kind="ExternalInput")
    dtpT2 = nc.dram_tensor("dtpT2", [64, NH], F32, kind="ExternalInput")
    dtb_c = nc.dram_tensor("dtb_c", [NH, 1], F32, kind="ExternalInput")
    Alog_c = nc.dram_tensor("Alog_c", [PT, HT], F32, kind="ExternalInput")
    Dsk_c = nc.dram_tensor("Dsk_c", [PT, NH], F32, kind="ExternalInput")
    nw2_c = nc.dram_tensor("nw2_c", [PT, IT], F32, kind="ExternalInput")
    resT = nc.dram_tensor("resT", [D, DI], F32, kind="ExternalInput")
    opT2 = nc.dram_tensor("opT2", [DI, D], F32, kind="ExternalInput")
    indW_c = nc.dram_tensor("indW_c", [PT, HT * NH], F32, kind="ExternalInput")
    bsel_c = nc.dram_tensor("bsel_c", [NH, HT * PT], F32, kind="ExternalInput")
    hsel_c = nc.dram_tensor("hsel_c", [NH, NH * PT], F32, kind="ExternalInput")
    ident_c = nc.dram_tensor("ident_c", [PT, PT], F32, kind="ExternalInput")
    x_out = nc.dram_tensor("x_out", [D, T], F32, kind="ExternalOutput")

    es = nc.ctx
    sb = lambda n, s, d=F32: es.enter_context(nc.sbuf_tensor(n, s, d))
    psf = lambda n, s, d=F32: es.enter_context(nc.psum_tensor(n, s, d))
    APX = bass.AP

    convw = sb("convw", [PT, IT * 4])
    convb = sb("convb", [PT, IT])
    dtpTs = sb("dtpTs", [64, NH], BF16)
    dtb16 = sb("dtb16", [NH, 1])
    A_sb = sb("A_sb", [PT, HT])
    Dsk = sb("Dsk", [PT, NH])
    nw2 = sb("nw2", [PT, IT])
    indW = sb("indW", [PT, HT * NH], BF16)
    bsel32 = sb("bsel32", [NH, HT * PT])
    bsel16 = sb("bsel16", [NH, HT * PT], BF16)
    hsel = sb("hsel", [NH, NH * PT], BF16)
    id16 = sb("id16", [PT, PT], BF16)
    ones_col = sb("ones_colZ", [PT, 1], BF16)
    ones_row = sb("ones_rowZ", [1, PT], BF16)
    eps_col = sb("eps_colZ", [PT, 1])
    ipg = [[sb(f"ipg_{b}_{k}", [PT, 512], BF16) for k in range(DT)] for b in range(2)]
    x_sb = [sb(f"x_{i}", [PT, Tc]) for i in range(DT)]
    xn16 = [sb(f"xn_{i}", [PT, Tc], BF16) for i in range(DT)]
    xq16 = [sb(f"xq_{i}", [PT, Tc], BF16) for i in range(2)]
    u16 = [sb(f"u16_{i}", [PT, Tc], BF16) for i in range(IT)]
    z16 = [sb(f"z16_{i}", [PT, Tc], BF16) for i in range(IT)]
    usb = [sb(f"usb_{i}", [PT, Tc + 4]) for i in range(IT)]
    halb = sb("halbZ", [PT, 3 * IT])
    B16 = [sb(f"B16_{i}", [PT, Tc], BF16) for i in range(HT)]
    C16 = [sb(f"C16_{i}", [PT, Tc], BF16) for i in range(HT)]
    dth16 = sb("dth16", [64, Tc], BF16)
    dtrow = sb("dtrow", [NH, Tc])
    dt16r = sb("dt16r", [NH, Tc], BF16)
    S_row = sb("S_row", [NH, Tc])
    scar = sb("scar", [NH, 1])
    S_b = [sb(f"S_b_{i}", [PT, Tc + 1]) for i in range(HT)]
    CW = [sb(f"CW_{i}", [PT, Tc]) for i in range(HT)]
    BwG = [sb(f"BwG_{i}", [PT, Tc], BF16) for i in range(HT)]
    Gam = [sb(f"Gam_{i}", [PT, NSC]) for i in range(HT)]
    e16 = sb("e16", [PT, Tc], BF16)
    w16 = sb("w16", [NH, Tc], BF16)
    hst = [sb(f"hst_{i}", [PT, PT]) for i in range(HT)]
    uT = [sb(f"uT_{pc}", [64, PT], BF16) for pc in range(8)]
    BwT = [sb(f"BwT_{pc}", [64, 64], BF16) for pc in range(8)]
    sc_a = sb("sc_aZ", [PT, Tc])
    sc_b = sb("sc_bZ", [PT, Tc])
    sc_a2 = sb("sc_a2Z", [PT, Tc])
    mrow = sb("m_rowZ", [1, Tc])[0:1, :]
    vrow = sb("v_rowZ", [1, Tc])[0:1, :]
    rrow = sb("r_rowZ", [1, Tc])[0:1, :]
    m16 = sb("m16Z", [1, Tc], BF16)[0:1, :]
    r16 = sb("r16Z", [1, Tc], BF16)[0:1, :]
    yac = [usb[i][:, 0:Tc] for i in range(IT)]

    pk = [psf(f"pk_{i}", [PT, Tc]) for i in range(4)]
    pw = psf("pw", [NH, Tc])
    pyh = psf("pyh", [PT, Tc])
    ptr16 = psf("ptr16", [64, PT], BF16)
    pbh = psf("pbh", [64, PT])

    p = Prog(nc)
    for dst, src_ in [(convw, convw_c), (convb, convb_c), (dtb16, dtb_c),
                      (A_sb, Alog_c), (Dsk, Dsk_c), (nw2, nw2_c),
                      (bsel32, bsel_c)]:
        p.emit("sync", lambda s, dst=dst, src_=src_: s.dma_start(
            out=dst[:, :], in_=src_[:, :]), dma=True)
    sg0 = p.gate("sync")
    for dst, src_ in [(dtpTs, dtpT2), (indW, indW_c), (hsel, hsel_c),
                      (id16, ident_c)]:
        p.emit("gpsimd", lambda g, dst=dst, src_=src_: g.dma_start(
            out=dst[:, :], in_=src_[:, :]), dma=True)
    rg = p.gate("gpsimd")
    t1 = p.emit("scalar", lambda s: s.activation(A_sb[:, :], A_sb[:, :], AF.Exp),
                deps=[sg0])
    tA = p.emit("vector", lambda v: v.tensor_single_scalar(
        A_sb[:, :], A_sb[:, :], -1.0, ALU.mult), deps=[t1])
    tb16 = p.emit("vector", lambda v: v.tensor_copy(bsel16[:, :], bsel32[:, :]),
                  deps=[sg0])
    tco = p.emit("vector", lambda v: v.memset(ones_col[:, :], 1.0))
    tro = p.emit("vector", lambda v: v.memset(ones_row[:, :], 1.0))
    tep = p.emit("vector", lambda v: v.memset(eps_col[:, :], LN_EPS))
    thal = p.emit("vector", lambda v: v.memset(halb[:, :], 0.0))
    tscz = p.emit("vector", lambda v: v.memset(scar[:, :], 0.0))
    hst_tok = [p.emit("vector", lambda v, i=i: v.memset(hst[i][:, :], 0.0))
               for i in range(HT)]
    tsb0 = [p.emit("vector", lambda v, i=i: v.memset(S_b[i][:, 0:1], 0.0))
            for i in range(HT)]

    bank_free = {}
    ipg_free = [[], []]
    prev_end = []

    def mm_bank(idx):
        return pk[idx], f"pk_{idx}"

    for ch in range(NCH):
        ts_ = slice(ch * Tc, (ch + 1) * Tc)
        for i in range(DT):
            p.emit("sync", lambda s, i=i, ts_=ts_: s.dma_start(
                out=x_sb[i][:, :], in_=x_in[i * PT:(i + 1) * PT, ts_]),
                deps=prev_end, dma=True)
        gx = p.gate("sync")
        stat_toks = []
        for i in range(DT):
            c1 = p.emit("vector", lambda v, i=i: v.tensor_copy(
                xq16[0][:, :], x_sb[i][:, :]),
                deps=[gx] + ([stat_toks[-1]] if stat_toks else []))
            c2 = p.emit("scalar", lambda s, i=i: s.activation(
                xq16[1][:, :], x_sb[i][:, :], AF.Square),
                deps=[gx] + ([stat_toks[-1]] if stat_toks else []))
            m1 = p.emit("tensor", lambda t, i=i: t.matmul(
                pk[2][0:1, :], ones_col[:, 0:1], xq16[0][:, :],
                start=(i == 0), stop=(i == DT - 1)),
                deps=[c1, tco] + ([bank_free["pk_2"]] if i == 0 and "pk_2" in bank_free else []))
            m2 = p.emit("tensor", lambda t, i=i: t.matmul(
                pk[3][0:1, :], ones_col[:, 0:1], xq16[1][:, :],
                start=(i == 0), stop=(i == DT - 1)),
                deps=[c2] + ([bank_free["pk_3"]] if i == 0 and "pk_3" in bank_free else []))
            stat_toks.append(m2)
        st = stat_toks[-1]
        e2 = p.emit("vector", lambda v: v.tensor_single_scalar(
            mrow, pk[2][0:1, :], 1.0 / D, ALU.mult), deps=[st])
        bank_free["pk_2"] = e2
        e3 = p.emit("scalar", lambda s: s.activation(sc_a[0:1, 0:Tc], mrow, AF.Square),
                    deps=[e2])
        e4 = p.emit("vector", lambda v: v.tensor_single_scalar(
            vrow, pk[3][0:1, :], 1.0 / D, ALU.mult), deps=[st])
        bank_free["pk_3"] = e4
        e5 = p.emit("vector", lambda v: v.tensor_sub(vrow, vrow, sc_a[0:1, 0:Tc]),
                    deps=[e4, e3])
        e6 = p.emit("scalar", lambda s: s.activation(
            sc_a[0:1, 0:Tc], vrow, AF.Sqrt, bias=eps_col[0:1, 0:1]), deps=[e5, tep])
        e7 = p.emit("vector", lambda v: v.reciprocal(rrow, sc_a[0:1, 0:Tc]), deps=[e6])
        c8 = p.emit("vector", lambda v: v.tensor_copy(m16, mrow), deps=[e2])
        c9 = p.emit("vector", lambda v: v.tensor_copy(r16, rrow), deps=[e7])
        bc1 = p.emit("tensor", lambda t: t.matmul(
            pk[0][:, :], ones_row[0:1, :], m16, start=True, stop=True),
            deps=[c8, tro] + ([bank_free["pk_0"]] if "pk_0" in bank_free else []))
        bc2 = p.emit("tensor", lambda t: t.matmul(
            pk[1][:, :], ones_row[0:1, :], r16, start=True, stop=True),
            deps=[c9] + ([bank_free["pk_1"]] if "pk_1" in bank_free else []))
        xnt = []
        last_s2 = None
        for i in range(DT):
            s1 = p.emit("vector", lambda v, i=i: v.tensor_sub(
                sc_a[:, :], x_sb[i][:, :], pk[0][:, :]),
                deps=[bc1, gx] + ([last_s2] if last_s2 else []))
            s2 = p.emit("vector", lambda v, i=i: v.tensor_mul(
                xn16[i][:, :], sc_a[:, :], pk[1][:, :]), deps=[bc2, s1])
            last_s2 = s2
            xnt.append(s2)
        bank_free["pk_0"] = last_s2
        bank_free["pk_1"] = last_s2
        prj_ev = {}
        for fg in range(ceil_div(RTOT, 512)):
            buf = fg % 2
            c0 = fg * 512
            cw = min(512, RTOT - c0)
            for k in range(DT):
                p.emit("gpsimd", lambda g, k=k, c0=c0, cw=cw, buf=buf: g.dma_start(
                    out=ipg[buf][k][:, 0:cw],
                    in_=ipT2[k * PT:(k + 1) * PT, c0:c0 + cw]),
                    deps=ipg_free[buf], dma=True)
            gfg = p.gate("gpsimd")
            ipg_free[buf] = []
            for jj in range(ceil_div(cw, PT)):
                r0 = c0 + jj * PT
                mw = min(PT, cw - jj * PT)
                bank, bname = mm_bank(2 + jj % 2)
                mt = None
                for k in range(DT):
                    mt = p.emit("tensor", lambda t, k=k, jj=jj, buf=buf, bank=bank, mw=mw: t.matmul(
                        bank[0:mw, :], ipg[buf][k][:, jj * PT:jj * PT + mw],
                        xn16[k][:, :], start=(k == 0), stop=(k == DT - 1)),
                        deps=[gfg, xnt[k]] +
                        ([bank_free[bname]] if bname in bank_free and k == 0 else []))
                ipg_free[buf].append(mt)
                ti = r0 // PT
                if r0 < DI:
                    ev = p.emit("scalar", lambda s, ti=ti, bank=bank: s.activation(
                        z16[ti][:, :], bank[:, :], AF.Silu), deps=[mt])
                elif r0 < 2 * DI:
                    ev = p.emit("vector", lambda v, ti=ti, bank=bank: v.tensor_copy(
                        usb[ti - 16][:, 3:3 + Tc], bank[:, :]), deps=[mt])
                elif r0 < 2 * DI + HN:
                    ev = p.emit("vector", lambda v, ti=ti, bank=bank: v.tensor_copy(
                        B16[ti - 32][:, :], bank[:, :]), deps=[mt])
                elif r0 < 2 * DI + 2 * HN:
                    ev = p.emit("scalar", lambda s, ti=ti, bank=bank: s.copy(
                        C16[ti - 40][:, :], bank[:, :]), deps=[mt])
                else:
                    ev = p.emit("vector", lambda v, bank=bank, mw=mw: v.tensor_copy(
                        dth16[0:mw, :], bank[0:mw, :]), deps=[mt])
                bank_free[bname] = ev
                prj_ev[ti] = ev
        conv_toks = []
        for it in range(IT):
            h0 = p.emit("vector", lambda v, it=it: v.tensor_copy(
                usb[it][:, 0:3], halb[:, it * 3:(it + 1) * 3]),
                deps=[prj_ev[16 + it], thal] + ([conv_toks[-1][0]] if conv_toks else []))
            a1 = p.emit("vector", lambda v, it=it: v.tensor_scalar(
                sc_a[:, :], usb[it][:, 0:Tc], convw[:, it * 4:it * 4 + 1], None,
                ALU.mult), deps=[h0, sg0])
            for kk in range(1, 4):
                a1 = p.emit("vector", lambda v, it=it, kk=kk: v.scalar_tensor_tensor(
                    sc_a[:, :], usb[it][:, kk:kk + Tc],
                    convw[:, it * 4 + kk:it * 4 + kk + 1],
                    sc_a[:, :], ALU.mult, ALU.add), deps=[a1])
            hs = p.emit("vector", lambda v, it=it: v.tensor_copy(
                halb[:, it * 3:(it + 1) * 3], usb[it][:, Tc:Tc + 3]),
                deps=[prj_ev[16 + it], h0])
            uu = p.emit("vector", lambda v, it=it: v.tensor_scalar(
                u16[it][:, :], sc_a[:, :], convb[:, it:it + 1], None, ALU.add),
                deps=[a1, sg0])
            conv_toks.append((uu, hs))
        mt = p.emit("tensor", lambda t: t.matmul(
            pw[0:NH, :], dtpTs[:, :], dth16[:, :], start=True, stop=True),
            deps=[prj_ev[48], rg] + ([bank_free["pw"]] if "pw" in bank_free else []))
        d1 = p.emit("vector", lambda v: v.tensor_scalar(
            dtrow[:, :], pw[0:NH, :], dtb16[:, 0:1], None, ALU.add), deps=[mt, sg0])
        bank_free["pw"] = d1
        d2 = p.emit("scalar", lambda s: s.activation(S_row[:, :], dtrow[:, :], AF.Abs),
                    deps=[d1])
        d3 = p.emit("scalar", lambda s: s.activation(S_row[:, :], S_row[:, :], AF.Exp,
                                                     scale=-1.0), deps=[d2])
        d4 = p.emit("scalar", lambda s: s.activation(S_row[:, :], S_row[:, :], AF.Ln,
                                                     bias=1.0), deps=[d3])
        d5 = p.emit("vector", lambda v: v.tensor_single_scalar(
            dtrow[:, :], dtrow[:, :], 0.0, ALU.max), deps=[d4, d1])
        d6 = p.emit("vector", lambda v: v.tensor_add(
            dtrow[:, :], dtrow[:, :], S_row[:, :]), deps=[d5])
        d7 = p.emit("vector", lambda v: v.tensor_single_scalar(
            dtrow[:, :], dtrow[:, :], DT_MIN, ALU.max), deps=[d6])
        d8 = p.emit("vector", lambda v: v.tensor_single_scalar(
            dtrow[:, :], dtrow[:, :], DT_MAX, ALU.min), deps=[d7])
        d9 = p.emit("vector", lambda v: v.tensor_copy(dt16r[:, :], dtrow[:, :]),
                    deps=[d8])
        sc1 = p.emit("vector", lambda v: v.tensor_tensor_scan(
            S_row[:, :], dtrow[:, :], dtrow[:, :], scar[:, 0:1],
            ALU.add, ALU.bypass), deps=[d9, tscz])
        sc2 = p.emit("vector", lambda v: v.tensor_copy(
            scar[:, 0:1], S_row[:, Tc - 1:Tc]), deps=[sc1])
        wv_toks = []
        for i in range(HT):
            bank, bname = mm_bank(i % 2)
            b1_ = p.emit("tensor", lambda t, i=i, bank=bank: t.matmul(
                bank[:, :], bsel32[:, i * PT:(i + 1) * PT], S_row[:, :],
                start=True, stop=True),
                deps=[sc1, sg0] + ([bank_free[bname]] if bname in bank_free else []))
            ev1 = p.emit("vector", lambda v, i=i, bank=bank: v.tensor_copy(
                S_b[i][:, 1:Tc + 1], bank[:, :]), deps=[b1_, tsb0[i]])
            bank_free[bname] = ev1
            bank2, bname2 = mm_bank(2 + i % 2)
            b2_ = p.emit("tensor", lambda t, i=i, bank2=bank2: t.matmul(
                bank2[:, :], bsel16[:, i * PT:(i + 1) * PT], dt16r[:, :],
                start=True, stop=True),
                deps=[d9, tb16] + ([bank_free[bname2]] if bname2 in bank_free else []))
            bd = p.emit("vector", lambda v, i=i, bank2=bank2: v.tensor_mul(
                B16[i][:, :], B16[i][:, :], bank2[:, :]), deps=[b2_, prj_ev[32 + i]])
            bank_free[bname2] = bd
            q1 = p.emit("vector", lambda v, i=i: v.tensor_sub(
                sc_a[:, :], S_b[i][:, 1:Tc + 1],
                APX(S_b[i].tensor if hasattr(S_b[i], 'tensor') else S_b[i], 0,
                    [[Tc + 1, PT], [QC, NSC], [0, QC]])), deps=[ev1] + wv_toks[-1:])
            q2 = p.emit("scalar", lambda s, i=i: s.activation(
                e16[:, :], sc_a[:, :], AF.Exp, scale=A_sb[:, i:i + 1]),
                deps=[q1, tA])
            q3 = p.emit("vector", lambda v, i=i: v.tensor_mul(
                CW[i][:, :], C16[i][:, :], e16[:, :]), deps=[q2, prj_ev[40 + i]])
            q4 = p.emit("vector", lambda v, i=i: v.tensor_sub(
                sc_a[:, :],
                APX(S_b[i].tensor if hasattr(S_b[i], 'tensor') else S_b[i], QC,
                    [[Tc + 1, PT], [QC, NSC], [0, QC]]),
                S_b[i][:, 1:Tc + 1]), deps=[q3, ev1])
            q5 = p.emit("scalar", lambda s, i=i: s.activation(
                e16[:, :], sc_a[:, :], AF.Exp, scale=A_sb[:, i:i + 1]),
                deps=[q4])
            q6 = p.emit("vector", lambda v, i=i: v.tensor_mul(
                BwG[i][:, :], B16[i][:, :], e16[:, :]), deps=[q5, bd])
            q7 = p.emit("vector", lambda v, i=i: v.tensor_tensor(
                Gam[i][:, :],
                APX(S_b[i].tensor if hasattr(S_b[i], 'tensor') else S_b[i], QC,
                    [[Tc + 1, PT], [QC, NSC]]),
                APX(S_b[i].tensor if hasattr(S_b[i], 'tensor') else S_b[i], 0,
                    [[Tc + 1, PT], [QC, NSC]]), ALU.subtract), deps=[ev1, q6])
            q8 = p.emit("scalar", lambda s, i=i: s.activation(
                Gam[i][:, :], Gam[i][:, :], AF.Exp, scale=A_sb[:, i:i + 1]),
                deps=[q7])
            wv_toks.append(q8)
        wv_done = wv_toks[-1]
        yinit = []
        for it in range(IT):
            q0 = p.emit("vector", lambda v, it=it: v.tensor_copy(
                sc_b[:, :], u16[it][:, :]),
                deps=[conv_toks[it][0], conv_toks[it][1]] + ([yinit[-1]] if yinit else [wv_done]))
            q1_ = p.emit("vector", lambda v, it=it: v.tensor_scalar(
                yac[it], sc_b[:, :], Dsk[:, it:it + 1], None, ALU.mult),
                deps=[q0, sg0])
            yinit.append(q1_)
        ysum_toks = []
        for h in range(NH):
            til = h // 2
            ro = (h % 2) * 64
            trt = []
            for pc in range(8):
                tr1 = p.emit("tensor", lambda t, h=h, pc=pc: t.transpose(
                    ptr16[0:64, :], u16[h][:, pc * 64:(pc + 1) * 64], id16[:, :]),
                    deps=[conv_toks[h][0], rg] + ([trt[-1][1]] if trt else ([ysum_toks[-1]] if ysum_toks else [])))
                tv1 = p.emit("vector", lambda v, pc=pc: v.tensor_copy(
                    uT[pc][:, :], ptr16[0:64, :]), deps=[tr1])
                tr2 = p.emit("tensor", lambda t, h=h, til=til, ro=ro, pc=pc: t.transpose(
                    ptr16[0:64, 0:64], BwG[til][ro:ro + 64, pc * 64:(pc + 1) * 64],
                    id16[ro:ro + 64, ro:ro + 64]), deps=[wv_done, tv1])
                tv2 = p.emit("vector", lambda v, pc=pc: v.tensor_copy(
                    BwT[pc][:, :], ptr16[0:64, 0:64]), deps=[tr2])
                trt.append((tr1, tv2))
            hupd = hst_tok[til]
            yiq = []
            for sc in range(NSC):
                yi = p.emit("tensor", lambda t, h=h, til=til, ro=ro, sc=sc: t.matmul(
                    pyh[:, sc * QC:(sc + 1) * QC], hst[til][ro:ro + 64, :],
                    CW[til][ro:ro + 64, sc * QC:(sc + 1) * QC],
                    start=True, stop=True),
                    deps=[hupd, wv_done] + ([bank_free["pyh"]] if sc == 0 and "pyh" in bank_free else []))
                bh = p.emit("tensor", lambda t, sc=sc: t.matmul(
                    pbh[0:64, :], BwT[sc // 2][(sc % 2) * 32:(sc % 2) * 32 + 32, :],
                    uT[sc // 2][(sc % 2) * 32:(sc % 2) * 32 + 32, :],
                    start=True, stop=True),
                    deps=[trt[sc // 2][1]] + ([hupd] if sc > 0 else [hst_tok[til]])),
                hupd = p.emit("vector", lambda v, til=til, ro=ro, sc=sc: v.scalar_tensor_tensor(
                    hst[til][ro:ro + 64, :], hst[til][ro:ro + 64, :],
                    Gam[til][ro:ro + 64, sc:sc + 1], pbh[0:64, :],
                    ALU.mult, ALU.add), deps=[bh[0], yi, wv_done])
                yiq.append(yi)
            hst_tok[til] = hupd
            ys = p.emit("vector", lambda v, h=h: v.tensor_add(
                yac[h], yac[h], pyh[:, :]), deps=[yiq[-1], yinit[h], hupd])
            bank_free["pyh"] = ys
            ysum_toks.append(ys)
        lag_last = ysum_toks[-1]
        wb_tok = {}
        for j in range(QC):
            wdt = []
            prev_t2 = {}
            for i in range(HT):
                par = i % 2
                sa = sc_a if par == 0 else sc_a2
                t1_ = p.emit("vector", lambda v, i=i, j=j, sa=sa: v.tensor_tensor(
                    APX(sa, j, [[Tc, PT], [QC, NSC], [1, QC - j]]),
                    APX(S_b[i], j + 1, [[Tc + 1, PT], [QC, NSC], [1, QC - j]]),
                    APX(S_b[i], 1, [[Tc + 1, PT], [QC, NSC], [1, QC - j]]),
                    ALU.subtract),
                    deps=[wv_done] + ([prev_t2[par]] if par in prev_t2 else [lag_last]))
                t2_ = p.emit("scalar", lambda s, i=i, j=j, sa=sa: s.activation(
                    APX(BwG[i], j, [[Tc, PT], [QC, NSC], [1, QC - j]]),
                    APX(sa, j, [[Tc, PT], [QC, NSC], [1, QC - j]]),
                    AF.Exp, scale=A_sb[:, i:i + 1]),
                    deps=[t1_, tA] + ([wb_tok["mtw"]] if "mtw" in wb_tok else []))
                prev_t2[par] = t2_
                t3_ = p.emit("gpsimd", lambda v, i=i, j=j: v.tensor_tensor(
                    APX(BwG[i], j, [[Tc, PT], [QC, NSC], [1, QC - j]]),
                    APX(BwG[i], j, [[Tc, PT], [QC, NSC], [1, QC - j]]),
                    APX(C16[i], j, [[Tc, PT], [QC, NSC], [1, QC - j]]),
                    ALU.mult), deps=[t2_])
                t4_ = p.emit("gpsimd", lambda v, i=i, j=j: v.tensor_tensor(
                    APX(BwG[i], j, [[Tc, PT], [QC, NSC], [1, QC - j]]),
                    APX(BwG[i], j, [[Tc, PT], [QC, NSC], [1, QC - j]]),
                    APX(B16[i], 0, [[Tc, PT], [QC, NSC], [1, QC - j]]),
                    ALU.mult), deps=[t3_])
                wb_tok[("G", i)] = t4_
                wdt.append(t4_)
            mtw = None
            for i in range(HT):
                mtw = p.emit("tensor", lambda t, i=i: t.matmul(
                    pw[0:NH, :], indW[:, i * NH:(i + 1) * NH], BwG[i][:, :],
                    start=(i == 0), stop=(i == HT - 1)),
                    deps=[wdt[i], rg] + ([bank_free["pw"]] if i == 0 and "pw" in bank_free else []))
            wc = p.emit("vector", lambda v: v.tensor_copy(w16[:, :], pw[0:NH, :]),
                        deps=[mtw])
            bank_free["pw"] = wc
            wb_tok["mtw"] = mtw
            yprev = None
            for h in range(NH):
                bank, bname = mm_bank(h % 4)
                wb = p.emit("tensor", lambda t, h=h, bank=bank: t.matmul(
                    bank[:, :], hsel[:, h * PT:(h + 1) * PT], w16[:, :],
                    start=True, stop=True),
                    deps=[wc, rg] + ([bank_free[bname]] if bname in bank_free else []))
                y1 = p.emit("vector", lambda v, h=h, j=j, bank=bank: v.tensor_tensor(
                    APX(sc_b, j, [[Tc, PT], [QC, NSC], [1, QC - j]]),
                    APX(u16[h], 0, [[Tc, PT], [QC, NSC], [1, QC - j]]),
                    APX(bank, j, [[Tc, PT], [QC, NSC], [1, QC - j]]),
                    ALU.mult),
                    deps=[wb, conv_toks[h][0]] + ([yprev] if yprev else []))
                y2 = p.emit("vector", lambda v, h=h, j=j: v.tensor_tensor(
                    APX(usb[h], j, [[Tc + 4, PT], [QC, NSC], [1, QC - j]]),
                    APX(usb[h], j, [[Tc + 4, PT], [QC, NSC], [1, QC - j]]),
                    APX(sc_b, j, [[Tc, PT], [QC, NSC], [1, QC - j]]),
                    ALU.add), deps=[y1, ysum_toks[h]])
                bank_free[bname] = y1
                yprev = y2
            lag_last = yprev
        res_ev = {}
        for fg in range(4):
            buf = fg % 2
            for k in range(DT):
                p.emit("gpsimd", lambda g, k=k, fg=fg, buf=buf: g.dma_start(
                    out=ipg[buf][k][:, :], in_=resT[k * PT:(k + 1) * PT,
                                                    fg * 512:(fg + 1) * 512]),
                    deps=ipg_free[buf], dma=True)
            gfg = p.gate("gpsimd")
            ipg_free[buf] = []
            for jj in range(4):
                it = fg * 4 + jj
                bank, bname = mm_bank(jj % 2)
                mt = None
                for k in range(DT):
                    mt = p.emit("tensor", lambda t, k=k, jj=jj, buf=buf, bank=bank: t.matmul(
                        bank[:, :], ipg[buf][k][:, jj * PT:(jj + 1) * PT],
                        xn16[k][:, :], start=(k == 0), stop=(k == DT - 1)),
                        deps=[gfg, xnt[k]] +
                        ([bank_free[bname]] if bname in bank_free and k == 0 else []))
                ipg_free[buf].append(mt)
                g1 = p.emit("vector", lambda v, it=it: v.tensor_mul(
                    sc_a[:, :], yac[it], z16[it][:, :]),
                    deps=[lag_last, prj_ev[it]] + ([res_ev[it - 1]] if it > 0 else []))
                g2 = p.emit("vector", lambda v, it=it, bank=bank: v.tensor_add(
                    z16[it][:, :], sc_a[:, :], bank[:, :]), deps=[g1, mt])
                bank_free[bname] = g2
                res_ev[it] = g2
        stat2 = []
        for it in range(IT):
            cq = p.emit("scalar", lambda s, it=it: s.activation(
                xq16[1][:, :], z16[it][:, :], AF.Square),
                deps=[res_ev[it]] + ([stat2[-1]] if stat2 else []))
            mq = p.emit("tensor", lambda t, it=it: t.matmul(
                pw[0:1, :], ones_col[:, 0:1], xq16[1][:, :],
                start=(it == 0), stop=(it == IT - 1)),
                deps=[cq, tco] + ([bank_free["pw"]] if it == 0 and "pw" in bank_free else []))
            stat2.append(mq)
        r1 = p.emit("vector", lambda v: v.tensor_single_scalar(
            vrow, pw[0:1, :], 1.0 / DI, ALU.mult), deps=[stat2[-1]])
        bank_free["pw"] = r1
        r2 = p.emit("vector", lambda v: v.tensor_single_scalar(
            vrow, vrow, EPS2, ALU.add), deps=[r1])
        r3 = p.emit("scalar", lambda s: s.activation(sc_a[0:1, 0:Tc], vrow, AF.Sqrt),
                    deps=[r2])
        r4 = p.emit("vector", lambda v: v.reciprocal(rrow, sc_a[0:1, 0:Tc]), deps=[r3])
        r5 = p.emit("vector", lambda v: v.tensor_copy(r16, rrow), deps=[r4])
        bcr = p.emit("tensor", lambda t: t.matmul(
            pk[0][:, :], ones_row[0:1, :], r16, start=True, stop=True),
            deps=[r5, tro] + ([bank_free["pk_0"]] if "pk_0" in bank_free else []))
        gn_toks = []
        for it in range(IT):
            n1 = p.emit("vector", lambda v, it=it: v.tensor_mul(
                sc_a[:, :], z16[it][:, :], pk[0][:, :]),
                deps=[bcr, res_ev[it]] + ([gn_toks[-1]] if gn_toks else []))
            n2 = p.emit("vector", lambda v, it=it: v.tensor_scalar(
                z16[it][:, :], sc_a[:, :], nw2[:, it:it + 1], None, ALU.mult),
                deps=[n1, sg0])
            gn_toks.append(n2)
        bank_free["pk_0"] = gn_toks[-1]
        prev_end = []
        for dh in range(2):
            mtg = [None] * 4
            for kh in range(2):
                buf = kh % 2
                for k in range(DT):
                    p.emit("gpsimd", lambda g, k=k, kh=kh, dh=dh, buf=buf: g.dma_start(
                        out=ipg[buf][k][:, :],
                        in_=opT2[kh * 1024 + k * PT:kh * 1024 + (k + 1) * PT,
                                 dh * 512:(dh + 1) * 512]),
                        deps=ipg_free[buf], dma=True)
            gop = p.gate("gpsimd")
            ipg_free[0] = []
            ipg_free[1] = []
            for kh in range(2):
                buf = kh % 2
                for jj in range(4):
                    bank, bname = mm_bank(jj)
                    for k in range(DT):
                        kg = kh * 8 + k
                        mtg[jj] = p.emit("tensor", lambda t, k=k, kg=kg, jj=jj, buf=buf, bank=bank: t.matmul(
                            bank[:, :], ipg[buf][k][:, jj * PT:(jj + 1) * PT],
                            z16[kg][:, :], start=(kg == 0), stop=(kg == IT - 1)),
                            deps=[gop, gn_toks[kg]] +
                            ([bank_free[bname]] if bname in bank_free and kg == 0 else []))
                    if kh == 1:
                        ipg_free[0].append(mtg[jj])
                        ipg_free[1].append(mtg[jj])
            for jj in range(4):
                i = dh * 4 + jj
                bank, bname = mm_bank(jj)
                o1 = p.emit("vector", lambda v, i=i, bank=bank: v.scalar_tensor_tensor(
                    x_sb[i][:, :], bank[:, :], 1.0, x_sb[i][:, :], ALU.mult, ALU.add),
                    deps=[mtg[jj]])
                bank_free[bname] = o1
                p.emit("gpsimd", lambda s, i=i, ts_=ts_: s.dma_start(
                    out=x_out[i * PT:(i + 1) * PT, ts_], in_=x_sb[i][:, :]),
                    deps=[o1], dma=True)
        prev_end = [p.gate("gpsimd")]

    toks = p.barrier_tokens()
    p.emit("sync", lambda s: s.nop(), deps=toks)
    with nc.Block() as block:
        p.finalize(block)
    return nc


def _prep_ssd_inputs(xcore, in_proj_w, dt_proj_w, conv_w, conv_b, A_log, Dskip,
                     dt_bias, norm_weight, out_proj_w, res_proj_w):
    DI, NH, DS = 2048, 16, 64
    ip = np.asarray(in_proj_w)
    order = np.concatenate([
        np.arange(0, 2 * DI),
        np.arange(2 * DI + 64, 2 * DI + 64 + 1024),
        np.arange(2 * DI + 64 + 1024, 6208),
        np.arange(2 * DI, 2 * DI + 64),
    ])
    ipr = ip[order]
    indW = np.zeros((128, 8 * 16), np.float32)
    bsel = np.zeros((16, 8 * 128), np.float32)
    hsel = np.zeros((16, 16 * 128), np.float32)
    for ti in range(8):
        for pp in range(128):
            h = (ti * 128 + pp) // 64
            indW[pp, ti * 16 + h] = 1.0
            bsel[h, ti * 128 + pp] = 1.0
    for h in range(16):
        hsel[h, h * 128:(h + 1) * 128] = 1.0
    return {
        "x_in": np.ascontiguousarray(xcore),
        "ipT2": np.ascontiguousarray(ipr.T),
        "convw_c": np.ascontiguousarray(
            np.asarray(conv_w).reshape(16, 128, 4).transpose(1, 0, 2).reshape(128, 64)),
        "convb_c": np.ascontiguousarray(np.asarray(conv_b).reshape(16, 128).T),
        "dtpT2": np.ascontiguousarray(np.asarray(dt_proj_w).T),
        "dtb_c": np.asarray(dt_bias, np.float32).reshape(16, 1),
        "Alog_c": np.ascontiguousarray(
            np.repeat(np.asarray(A_log).reshape(16, 64), 2, axis=0)
            .reshape(8, 2, 64).reshape(8, 128).T) if False else
            np.ascontiguousarray(np.asarray(A_log).reshape(8, 128).T),
        "Dsk_c": np.ascontiguousarray(np.asarray(Dskip).reshape(16, 128).T),
        "nw2_c": np.ascontiguousarray(np.asarray(norm_weight).reshape(16, 128).T),
        "resT": np.ascontiguousarray(np.asarray(res_proj_w).T),
        "opT2": np.ascontiguousarray(np.asarray(out_proj_w).T),
        "indW_c": indW,
        "bsel_c": bsel,
        "hsel_c": hsel,
        "ident_c": np.eye(128, dtype=np.float32),
    }


def kernel(**d):
    C = NCORES
    xs = [np.ascontiguousarray(d["L0_attn_x"][c].T) for c in range(C)]

    ffn_w = dict(w1=np.ascontiguousarray(d["L0_ffn_w1"]),
                 b1c=np.ascontiguousarray(d["L0_ffn_b1"].reshape(32, 128).T),
                 w2=np.ascontiguousarray(d["L0_ffn_w2"]),
                 b2c=np.ascontiguousarray(d["L0_ffn_b2"].reshape(8, 128).T))
    mam_args = dict(
        norm_w=d["L0_attn_norm_w"], in_proj_w=d["L0_attn_in_proj_w"],
        conv_w=d["L0_attn_conv_w"], conv_b=d["L0_attn_conv_b"],
        x_proj_w=d["L0_attn_x_proj_w"], dt_proj_w=d["L0_attn_dt_proj_w"],
        dt_proj_b=d["L0_attn_dt_proj_b"], A_log=d["L0_attn_A_log"],
        D_param=d["L0_attn_D_param"], out_proj_w=d["L0_attn_out_proj_w"])
    ssd_args = dict(
        in_proj_w=d["L1_attn_in_proj_w"], dt_proj_w=d["L1_attn_dt_proj_w"],
        conv_w=d["L1_attn_conv_w"], conv_b=d["L1_attn_conv_b"],
        A_log=d["L1_attn_A_log"], Dskip=d["L1_attn_Dskip"],
        dt_bias=d["L1_attn_dt_bias"], norm_weight=d["L1_attn_norm_weight"],
        out_proj_w=d["L1_attn_out_proj_w"], res_proj_w=d["L1_attn_res_proj_w"])
    waffT = np.ascontiguousarray(d["L1_ffn_waff"].T)
    thr = d["L1_ffn_affinity_threshold"]

    def launch(name, maps):
        res = run_bass_kernel_spmd(_get_prog(name), maps, core_ids=list(range(C)))
        return res.results

    def run_ffn(xs):
        r = launch("ffn0", [{"x_in": xs[c], **ffn_w} for c in range(C)])
        return [np.ascontiguousarray(r[c]["x_out"]) for c in range(C)]

    def run_mamba(xs):
        r = launch("mamba1", [_prep_mamba1_inputs(xs[c], **mam_args)
                              for c in range(C)])
        return [np.ascontiguousarray(r[c]["x_out"]) for c in range(C)]

    def run_moe(xs):
        rs = launch("moe_stats", [{"x_in": xs[c], "waffT": waffT}
                                  for c in range(C)])
        parts = np.array([rs[c]["asum_d"][0, 0] for c in range(C)], np.float32)
        rm = launch("moe_main", [_prep_moe_main_inputs(
            xs[c], rs[c]["xn_d"], rs[c]["aff_d"], parts, thr,
            d["L1_ffn_W1"], d["L1_ffn_b1"], d["L1_ffn_W2"], d["L1_ffn_b2"])
            for c in range(C)])
        return [np.ascontiguousarray(rm[c]["x_out"]) for c in range(C)]

    def run_ssd(xs):
        r = launch("ssd", [_prep_ssd_inputs(xs[c], **ssd_args)
                           for c in range(C)])
        return [np.ascontiguousarray(r[c]["x_out"]) for c in range(C)]

    xs = run_ffn(xs)
    xs = run_mamba(xs)
    xs = run_ffn(xs)
    xs = run_moe(xs)
    xs = run_ssd(xs)
    xs = run_moe(xs)
    return np.stack([x.T for x in xs]).astype(np.float32)


# version 05 (final)
# speedup vs baseline: 1.0297x; 1.0084x over previous
import numpy as np
import ml_dtypes
import concourse.bass as bass
import concourse.mybir as mybir
from concourse.bass_utils import run_bass_kernel_spmd

F32 = mybir.dt.float32
BF16 = mybir.dt.bfloat16
AF = mybir.ActivationFunctionType
ALU = mybir.AluOpType
AX = mybir.AxisListType

NCORES = 8
D = 1024
T = 4096
PT = 128
DT = D // PT
LN_EPS = 1e-5

BF = ml_dtypes.bfloat16


class Prog:

    ENGINES = ("sync", "tensor", "vector", "scalar", "gpsimd", "sync2")

    def __init__(self, nc):
        self.nc = nc
        self.q = {e: [] for e in self.ENGINES}
        self.sems = {}
        self.cnt = {}
        self.last_wait = {}
        for e in self.ENGINES:
            h = nc.ctx.enter_context(nc.semaphore(f"sem_{e}"))
            self.sems[e] = h
            self.cnt[e] = 0

    def emit(self, eng, fn, deps=(), dma=False, inc=None):
        if inc is None:
            inc = 16 if dma else 1
        self.cnt[eng] += inc
        tok = (eng, self.cnt[eng])
        sem = self.sems[eng]
        deps = tuple(deps)

        def code(e, fn=fn, deps=deps, sem=sem, inc=inc, eng=eng):
            for (s, c) in deps:
                key = (eng, s)
                if self.last_wait.get(key, 0) < c:
                    e.wait_ge(self.sems[s], c)
                    self.last_wait[key] = c
            instr = fn(e)
            instr.then_inc(sem, inc)
        self.q[eng].append(code)
        return tok

    def gate(self, eng):
        tot = self.cnt[eng]
        return self.emit(eng, lambda e: e.nop(), deps=[(eng, tot)])

    def barrier_tokens(self):
        return [(e, self.cnt[e]) for e in self.ENGINES if self.cnt[e] > 0]

    def finalize(self, block):
        for e in self.ENGINES:
            ops = self.q[e]
            if not ops:
                continue

            def run(engine, ops=ops):
                for op in ops:
                    op(engine)
            attr = "gpsimd" if e == "sync2" else e
            getattr(block, attr)(run)


def ceil_div(a, b):
    return -(-a // b)


def build_ffn0(scale_b1=1.0):
    H = 4096
    HT = H // PT
    TB = 256
    NB = T // TB

    nc = bass.Bass("TRN2", target_bir_lowering=False)
    x_in = nc.dram_tensor("x_in", [D, T], F32, kind="ExternalInput")
    w1 = nc.dram_tensor("w1", [D, H], F32, kind="ExternalInput")
    b1c = nc.dram_tensor("b1c", [PT, HT], F32, kind="ExternalInput")
    w2 = nc.dram_tensor("w2", [H, D], F32, kind="ExternalInput")
    b2c = nc.dram_tensor("b2c", [PT, DT], F32, kind="ExternalInput")
    x_out = nc.dram_tensor("x_out", [D, T], F32, kind="ExternalOutput")

    from contextlib import ExitStack
    es = nc.ctx
    sb = lambda n, s, d=F32: es.enter_context(nc.sbuf_tensor(n, s, d))
    ps = lambda n, s, d=F32: es.enter_context(nc.psum_tensor(n, s, d))

    w1s = [sb(f"w1_{i}", [PT, H], BF16) for i in range(DT)]
    w2s = [sb(f"w2_{i}", [PT, D], BF16) for i in range(HT)]
    b1s = sb("b1s", [PT, HT])
    b2s = sb("b2s", [PT, DT])
    b2h = sb("b2h", [PT, DT])
    ones_col = sb("ones_col", [PT, 1], BF16)
    eps_col = sb("eps_col", [PT, 1])
    ones_row = sb("ones_row", [1, PT], BF16)

    x_sb = [sb(f"x_{i}", [PT, TB]) for i in range(DT)]
    xb16 = [sb(f"xb16_{i}", [PT, TB], BF16) for i in range(DT)]
    sq16 = [sb(f"sq16_{i}", [PT, TB], BF16) for i in range(DT)]
    xn = [sb(f"xn_{i}", [PT, TB], BF16) for i in range(DT)]
    h16 = [sb(f"h16_{i}", [PT, TB], BF16) for i in range(HT)]
    ssum = sb("ssum", [1, TB])
    ssq = sb("ssq", [1, TB])
    mrow = sb("mrow", [1, TB])[0:1, :]
    rrow = sb("rrow", [1, TB])[0:1, :]
    m16 = sb("m16", [1, TB], BF16)[0:1, :]
    r16 = sb("r16", [1, TB], BF16)[0:1, :]
    mb = sb("mb", [PT, TB])
    rb = sb("rb", [PT, TB])
    tmp = sb("tmpx", [PT, TB])
    xo = [sb(f"xo_{i}", [PT, TB]) for i in range(DT)]

    pstat_s = ps("pstat_s", [1, TB])
    pstat_q = ps("pstat_q", [1, TB])
    pbc = ps("pbc", [PT, TB])
    pbc2 = ps("pbc2", [PT, TB])
    pmm = [ps(f"pmm_{i}", [PT, TB]) for i in range(4)]

    p = Prog(nc)

    wtoks = []
    for i in range(DT):
        wtoks.append(p.emit("gpsimd", lambda g, i=i: g.dma_start(
            out=w1s[i][:, :], in_=w1[i * PT:(i + 1) * PT, :]), dma=True))
    for i in range(HT):
        wtoks.append(p.emit("gpsimd", lambda g, i=i: g.dma_start(
            out=w2s[i][:, :], in_=w2[i * PT:(i + 1) * PT, :]), dma=True))
    tb1 = p.emit("sync", lambda s: s.dma_start(out=b1s[:, :], in_=b1c[:, :]), dma=True)
    tb2 = p.emit("sync", lambda s: s.dma_start(out=b2s[:, :], in_=b2c[:, :]), dma=True)
    wgate = p.gate("gpsimd")
    sg0 = p.gate("sync")
    tb1, tb2 = sg0, sg0
    tb2h = p.emit("vector", lambda v: v.tensor_single_scalar(
        b2h[:, :], b2s[:, :], 0.5, ALU.mult), deps=[tb2])
    tco = p.emit("vector", lambda v: v.memset(ones_col[:, :], 1.0))
    tep = p.emit("vector", lambda v: v.memset(eps_col[:, :], LN_EPS))
    tro = p.emit("vector", lambda v: v.memset(ones_row[:, :], 1.0))

    w1tok = wgate
    w2tok = wgate

    prev_end = []
    for blk in range(NB):
        ts_ = slice(blk * TB, (blk + 1) * TB)
        for i in range(DT):
            p.emit("sync", lambda s, i=i, ts_=ts_: s.dma_start(
                out=x_sb[i][:, :], in_=x_in[i * PT:(i + 1) * PT, ts_]),
                deps=prev_end, dma=True)
        gx = p.gate("sync")
        xt = [gx] * DT
        cp = []
        for i in range(DT):
            c1 = p.emit("vector", lambda v, i=i: v.tensor_copy(
                xb16[i][:, :], x_sb[i][:, :]), deps=[xt[i]])
            c2 = p.emit("scalar", lambda s, i=i: s.activation(
                sq16[i][:, :], x_sb[i][:, :], AF.Square), deps=[xt[i]])
            cp.append((c1, c2))
        st = None
        for i in range(DT):
            st = p.emit("tensor", lambda t, i=i: t.matmul(
                pstat_s[0:1, :], ones_col[:, :], xb16[i][:, :],
                start=(i == 0), stop=(i == DT - 1)),
                deps=[cp[i][0], tco])
        for i in range(DT):
            st = p.emit("tensor", lambda t, i=i: t.matmul(
                pstat_q[0:1, :], ones_col[:, :], sq16[i][:, :],
                start=(i == 0), stop=(i == DT - 1)),
                deps=[cp[i][1], tco])
        e2 = p.emit("vector", lambda v: v.tensor_single_scalar(
            mrow, pstat_s[0:1, :], 1.0 / D, ALU.mult), deps=[st])
        e3 = p.emit("scalar", lambda s: s.activation(
            tmp[0:1, 0:TB], mrow, AF.Square), deps=[e2])
        e4 = p.emit("vector", lambda v: v.tensor_single_scalar(
            rrow, pstat_q[0:1, :], 1.0 / D, ALU.mult), deps=[st])
        e5 = p.emit("vector", lambda v: v.tensor_sub(
            rrow, rrow, tmp[0:1, 0:TB]), deps=[e4, e3])
        e6 = p.emit("scalar", lambda s: s.activation(
            tmp[0:1, 0:TB], rrow, AF.Sqrt, bias=eps_col[0:1, 0:1]), deps=[e5, tep])
        e7 = p.emit("vector", lambda v: v.reciprocal(rrow, tmp[0:1, 0:TB]),
                    deps=[e6])
        e8 = p.emit("vector", lambda v: v.tensor_copy(m16, mrow),
                    deps=[e2])
        e9 = p.emit("vector", lambda v: v.tensor_copy(r16, rrow),
                    deps=[e7])
        bc1 = p.emit("tensor", lambda t: t.matmul(
            pbc[:, :], ones_row[:, :], m16, start=True, stop=True),
            deps=[e8, tro])
        bc2 = p.emit("tensor", lambda t: t.matmul(
            pbc2[:, :], ones_row[:, :], r16, start=True, stop=True),
            deps=[e9, tro])
        bce1 = p.emit("vector", lambda v: v.tensor_copy(mb[:, :], pbc[:, :]),
                      deps=[bc1])
        bce2 = p.emit("vector", lambda v: v.tensor_copy(rb[:, :], pbc2[:, :]),
                      deps=[bc2])
        xnt = []
        for i in range(DT):
            s1 = p.emit("vector", lambda v, i=i: v.tensor_sub(
                tmp[:, :], x_sb[i][:, :], mb[:, :]), deps=[bce1, xt[i]])
            s2 = p.emit("vector", lambda v, i=i: v.tensor_mul(
                xn[i][:, :], tmp[:, :], rb[:, :]), deps=[bce2, s1])
            xnt.append(s2)
        htoks = []
        for j in range(HT):
            bank = pmm[j % 2]
            mt = None
            for k in range(DT):
                mt = p.emit("tensor", lambda t, j=j, k=k, bank=bank: t.matmul(
                    bank[:, :], w1s[k][:, j * PT:(j + 1) * PT], xn[k][:, :],
                    start=(k == 0), stop=(k == DT - 1)),
                    deps=[xnt[k], w1tok] + ([htoks[j - 2]] if j >= 2 else []))
            ht = p.emit("scalar", lambda s, j=j, bank=bank: s.activation(
                h16[j][:, :], bank[:, :], AF.Gelu, bias=b1s[:, j:j + 1],
                scale=scale_b1), deps=[mt, tb1])
            htoks.append(ht)
        otoks = []
        for i in range(DT):
            bank = pmm[2 + i % 2]
            mt = None
            for k in range(HT):
                mt = p.emit("tensor", lambda t, i=i, k=k, bank=bank: t.matmul(
                    bank[:, :], w2s[k][:, i * PT:(i + 1) * PT], h16[k][:, :],
                    start=(k == 0), stop=(k == HT - 1)),
                    deps=[htoks[k], w2tok] + ([otoks[i - 2][0]] if i >= 2 else []))
            o1 = p.emit("vector", lambda v, i=i, bank=bank: v.scalar_tensor_tensor(
                xo[i][:, :], bank[:, :], 0.5, x_sb[i][:, :],
                ALU.mult, ALU.add), deps=[mt])
            o2 = p.emit("vector", lambda v, i=i: v.tensor_scalar(
                xo[i][:, :], xo[i][:, :], b2h[:, i:i + 1], None,
                ALU.add), deps=[o1, tb2h])
            otoks.append((o1, o2))
            p.emit("sync2", lambda s, i=i, ts_=ts_: s.dma_start(
                out=x_out[i * PT:(i + 1) * PT, ts_], in_=xo[i][:, :]),
                deps=[o2], dma=True)
            if i == DT - 1:
                prev_end = [p.gate("sync2")]

    toks = p.barrier_tokens()
    p.emit("sync", lambda s: s.nop(), deps=toks)

    with nc.Block() as block:
        p.finalize(block)
    return nc


def _b2_scale_note():
    pass


def _prep_ffn0_inputs(xcore, w1, b1, w2, b2):
    b1c = np.ascontiguousarray(b1.reshape(32, 128).T)
    b2c = np.ascontiguousarray(b2.reshape(8, 128).T)
    return {
        "x_in": np.ascontiguousarray(xcore),
        "w1": np.ascontiguousarray(w1),
        "b1c": b1c,
        "w2": np.ascontiguousarray(w2),
        "b2c": b2c,
    }


_PROGS = {}


def _get_prog(name):
    if name not in _PROGS:
        _PROGS[name] = {"ffn0": build_ffn0, "mamba1": build_mamba1,
                        "moe_stats": build_moe_stats, "moe_main": build_moe_main,
                        "ssd": build_ssd}[name]()
    return _PROGS[name]


def build_mamba1(dbg=False):
    DI = 2048
    IT = DI // PT
    S = 16
    Tc = 512
    NCH = T // Tc
    EPS2 = float(np.finfo(np.float32).eps)

    nc = bass.Bass("TRN2", target_bir_lowering=False)
    x_in = nc.dram_tensor("x_in", [D, T], F32, kind="ExternalInput")
    nw_c = nc.dram_tensor("nw_c", [PT, DT], F32, kind="ExternalInput")
    ipT = nc.dram_tensor("ipT", [D, 2 * DI], F32, kind="ExternalInput")
    convw_c = nc.dram_tensor("convw_c", [PT, IT * 4], F32, kind="ExternalInput")
    convb_c = nc.dram_tensor("convb_c", [PT, IT], F32, kind="ExternalInput")
    xpT_r = nc.dram_tensor("xpT_r", [DI, 128], F32, kind="ExternalInput")
    dtpT = nc.dram_tensor("dtpT", [64, DI], F32, kind="ExternalInput")
    dtb_c = nc.dram_tensor("dtb_c", [PT, IT], F32, kind="ExternalInput")
    Alog_c = nc.dram_tensor("Alog_c", [PT, IT * S], F32, kind="ExternalInput")
    D_c = nc.dram_tensor("D_c", [PT, IT], F32, kind="ExternalInput")
    opT = nc.dram_tensor("opT", [DI, D], F32, kind="ExternalInput")
    ind_in = nc.dram_tensor("ind_in", [S, S * PT], F32, kind="ExternalInput")
    x_out = nc.dram_tensor("x_out", [D, T], F32, kind="ExternalOutput")
    if dbg:
        dbg_xn = nc.dram_tensor("dbg_xn", [D, 512], F32, kind="ExternalOutput")
        dbg_u = nc.dram_tensor("dbg_u", [2048, 512], F32, kind="ExternalOutput")
        dbg_delta = nc.dram_tensor("dbg_delta", [PT, 16 * 512], F32, kind="ExternalOutput")
        dbg_B = nc.dram_tensor("dbg_B", [16, 512], F32, kind="ExternalOutput")
        dbg_C = nc.dram_tensor("dbg_C", [16, 512], F32, kind="ExternalOutput")
        dbg_y = nc.dram_tensor("dbg_y", [2048, 512], F32, kind="ExternalOutput")
        dbg_hcar = nc.dram_tensor("dbg_hcar", [PT, 256], F32, kind="ExternalOutput")
        dbg_u1 = nc.dram_tensor("dbg_u1", [2048, 512], F32, kind="ExternalOutput")
        dbg_d1 = nc.dram_tensor("dbg_d1", [PT, 16 * 512], F32, kind="ExternalOutput")

    es = nc.ctx
    sb = lambda n, s, d=F32: es.enter_context(nc.sbuf_tensor(n, s, d))
    psf = lambda n, s, d=F32: es.enter_context(nc.psum_tensor(n, s, d))

    opTs = [sb(f"opT_{k}", [PT, D], BF16) for k in range(IT)]
    xpTs = [sb(f"xpT_{k}", [PT, 128], BF16) for k in range(IT)]
    dtpTs = sb("dtpTs", [64, DI], BF16)
    A_sb = sb("A_sb", [PT, IT * S])
    nw = sb("nw", [PT, DT])
    convw = sb("convw", [PT, IT * 4])
    convb = sb("convb", [PT, IT])
    dtb = sb("dtb", [PT, IT])
    Dp = sb("Dp", [PT, IT])
    ind16 = sb("ind16", [S, S * PT], BF16)
    ones_row = sb("ones_row", [1, PT], BF16)
    ones_col = sb("ones_colm", [PT, 1], BF16)
    epsc = sb("epsc", [PT, 1])
    eps2c = sb("eps2c", [PT, 1])
    ipg = [[sb(f"ipg_{b}_{k}", [PT, 512], BF16) for k in range(DT)] for b in range(2)]
    x_sb = [sb(f"x_{i}", [PT, Tc]) for i in range(DT)]
    xn16 = [sb(f"xn_{i}", [PT, Tc], BF16) for i in range(DT)]
    xq16 = [xn16[0], xn16[1]]
    xsb = [sb(f"xs_{i}", [PT, Tc + 4]) for i in range(IT)]
    halb = sb("halb", [PT, 3 * IT])
    z16 = [sb(f"z16_{i}", [PT, Tc], BF16) for i in range(IT)]
    u16 = [sb(f"u16_{i}", [PT, Tc], BF16) for i in range(IT)]
    delta = sb("delta", [PT, IT * Tc])
    dtraw16 = sb("dtraw16", [64, Tc], BF16)
    B16 = sb("B16", [S, Tc], BF16)
    C16 = sb("C16", [S, Tc], BF16)
    sc_a = sb("sc_a", [PT, Tc])
    sc_b = sb("sc_b", [PT, Tc])
    sc_c = sb("sc_c", [PT, Tc])
    sc_d = sb("sc_d", [PT, Tc])
    sc_a2 = sb("sc_a2", [PT, Tc])
    hcar = sb("hcar", [PT, IT * S])
    mrow = sb("m_row", [1, Tc])[0:1, :]
    vrow = sb("v_row", [1, Tc])[0:1, :]
    rrow = sb("r_row", [1, Tc])[0:1, :]
    prow = sb("p_row", [1, Tc])[0:1, :]
    m16 = sb("m16r", [1, Tc], BF16)[0:1, :]
    p16 = sb("p16r", [1, Tc], BF16)[0:1, :]
    r16m = p16

    pstat_s = psf("pstat_s", [1, Tc])
    pstat_q = psf("pstat_q", [1, Tc])
    pbc = psf("pbc", [PT, Tc])
    pbc2 = psf("pbc2", [PT, Tc])
    pk = [psf(f"pk_{i}", [PT, Tc]) for i in range(4)]

    p = Prog(nc)

    rt = []
    for k in range(IT):
        rt.append(p.emit("gpsimd", lambda g, k=k: g.dma_start(
            out=opTs[k][:, :], in_=opT[k * PT:(k + 1) * PT, :]), dma=True))
        rt.append(p.emit("gpsimd", lambda g, k=k: g.dma_start(
            out=xpTs[k][:, :], in_=xpT_r[k * PT:(k + 1) * PT, :]), dma=True))
    rt.append(p.emit("gpsimd", lambda g: g.dma_start(
        out=dtpTs[:, :], in_=dtpT[:, :]), dma=True))
    rt.append(p.emit("gpsimd", lambda g: g.dma_start(
        out=ind16[:, :], in_=ind_in[:, :]), dma=True))
    talog = p.emit("sync", lambda s: s.dma_start(out=A_sb[:, :], in_=Alog_c[:, :]), dma=True)
    for dst, src in [(nw, nw_c), (convw, convw_c), (convb, convb_c),
                     (dtb, dtb_c), (Dp, D_c)]:
        rt.append(p.emit("sync", lambda s, dst=dst, src=src: s.dma_start(
            out=dst[:, :], in_=src[:, :]), dma=True))
    rg = p.gate("gpsimd")
    sg0 = p.gate("sync")
    t1 = p.emit("scalar", lambda s: s.activation(A_sb[:, :], A_sb[:, :], AF.Exp),
                deps=[sg0])
    tA = p.emit("vector", lambda v: v.tensor_single_scalar(
        A_sb[:, :], A_sb[:, :], -1.0, ALU.mult), deps=[t1])
    tro = p.emit("vector", lambda v: v.memset(ones_row[:, :], 1.0))
    tco = p.emit("vector", lambda v: v.memset(ones_col[:, :], 1.0))
    tep = p.emit("vector", lambda v: v.memset(epsc[:, :], LN_EPS))
    tep2 = p.emit("vector", lambda v: v.memset(eps2c[:, :], EPS2))
    thc = p.emit("vector", lambda v: v.memset(hcar[:, :], 0.0))
    thal = p.emit("vector", lambda v: v.memset(halb[:, :], 0.0))
    rtok = rg

    bank_free = {}
    ipg_free = [[], []]
    prev_end = []

    for ch in range(NCH):
        ts_ = slice(ch * Tc, (ch + 1) * Tc)
        for i in range(DT):
            p.emit("sync", lambda s, i=i, ts_=ts_: s.dma_start(
                out=x_sb[i][:, :], in_=x_in[i * PT:(i + 1) * PT, ts_]),
                deps=prev_end, dma=True)
        gx = p.gate("sync")
        xt = [gx] * DT
        stat_toks = []
        for i in range(DT):
            c1 = p.emit("vector", lambda v, i=i: v.tensor_copy(
                xq16[0][:, :], x_sb[i][:, :]), deps=[xt[i]] + ([stat_toks[-1]] if stat_toks else []))
            c2 = p.emit("scalar", lambda s, i=i: s.activation(
                xq16[1][:, :], x_sb[i][:, :], AF.Square), deps=[xt[i]] + ([stat_toks[-1]] if stat_toks else []))
            m1 = p.emit("tensor", lambda t, i=i: t.matmul(
                pstat_s[0:1, :], ones_col[:, 0:1], xq16[0][:, :],
                start=(i == 0), stop=(i == DT - 1)),
                deps=[c1, tco] + ([bank_free["pstat_s"]] if i == 0 and "pstat_s" in bank_free else []))
            m2 = p.emit("tensor", lambda t, i=i: t.matmul(
                pstat_q[0:1, :], ones_col[:, 0:1], xq16[1][:, :],
                start=(i == 0), stop=(i == DT - 1)),
                deps=[c2] + ([bank_free["pstat_q"]] if i == 0 and "pstat_q" in bank_free else []))
            stat_toks.append(m2)
        st = stat_toks[-1]
        e2 = p.emit("vector", lambda v: v.tensor_single_scalar(
            mrow, pstat_s[0:1, :], 1.0 / D, ALU.mult), deps=[st])
        bank_free["pstat_s"] = e2
        e3 = p.emit("scalar", lambda s: s.activation(sc_a[0:1, 0:Tc], mrow, AF.Square),
                    deps=[e2])
        e4 = p.emit("vector", lambda v: v.tensor_single_scalar(
            vrow, pstat_q[0:1, :], 1.0 / D, ALU.mult), deps=[st])
        bank_free["pstat_q"] = e4
        e5 = p.emit("vector", lambda v: v.tensor_sub(vrow, vrow, sc_a[0:1, 0:Tc]),
                    deps=[e4, e3])
        e6 = p.emit("scalar", lambda s: s.activation(
            sc_a[0:1, 0:Tc], vrow, AF.Sqrt, bias=epsc[0:1, 0:1]), deps=[e5, tep])
        e7 = p.emit("vector", lambda v: v.reciprocal(rrow, sc_a[0:1, 0:Tc]), deps=[e6])
        e8 = p.emit("scalar", lambda s: s.activation(sc_a[0:1, 0:Tc], rrow, AF.Square),
                    deps=[e7])
        e9 = p.emit("vector", lambda v: v.tensor_mul(prow, vrow, sc_a[0:1, 0:Tc]),
                    deps=[e8, e5])
        e10 = p.emit("scalar", lambda s: s.activation(
            sc_a[0:1, 0:Tc], prow, AF.Sqrt, bias=eps2c[0:1, 0:1]), deps=[e9, tep2])
        e11 = p.emit("vector", lambda v: v.reciprocal(prow, sc_a[0:1, 0:Tc]), deps=[e10])
        e12 = p.emit("vector", lambda v: v.tensor_mul(prow, prow, rrow), deps=[e11, e7])
        c8 = p.emit("vector", lambda v: v.tensor_copy(m16, mrow), deps=[e2])
        c9 = p.emit("vector", lambda v: v.tensor_copy(p16, prow), deps=[e12])
        bc1 = p.emit("tensor", lambda t: t.matmul(
            pbc[:, :], ones_row[0:1, :], m16, start=True, stop=True),
            deps=[c8, tro] + ([bank_free["pbc"]] if "pbc" in bank_free else []))
        bc2 = p.emit("tensor", lambda t: t.matmul(
            pbc2[:, :], ones_row[0:1, :], p16, start=True, stop=True),
            deps=[c9] + ([bank_free["pbc2"]] if "pbc2" in bank_free else []))
        xnt = []
        last_s2 = None
        for i in range(DT):
            s1 = p.emit("vector", lambda v, i=i: v.tensor_sub(
                sc_a[:, :], x_sb[i][:, :], pbc[:, :]),
                deps=[bc1, xt[i]] + ([last_s2] if last_s2 else []))
            s2 = p.emit("vector", lambda v, i=i: v.scalar_tensor_tensor(
                xn16[i][:, :], sc_a[:, :], nw[:, i:i + 1], pbc2[:, :],
                ALU.mult, ALU.mult), deps=[bc2, s1])
            last_s2 = s2
            xnt.append(s2)
        bank_free["pbc"] = xnt[-1]
        bank_free["pbc2"] = xnt[-1]
        if dbg and ch == 0:
            for i in range(DT):
                p.emit("gpsimd", lambda s, i=i: s.dma_start(
                    out=dbg_xn[i * PT:(i + 1) * PT, :], in_=xn16[i][:, :]),
                    deps=[xnt[i]], dma=True)
        xz_ev = []
        for fg in range(8):
            buf = fg % 2
            for k in range(DT):
                p.emit("gpsimd", lambda g, k=k, fg=fg, buf=buf: g.dma_start(
                    out=ipg[buf][k][:, :],
                    in_=ipT[k * PT:(k + 1) * PT, fg * 512:(fg + 1) * 512]),
                    deps=ipg_free[buf], dma=True)
            gfg = p.gate("gpsimd")
            lt = [gfg] * DT
            ipg_free[buf] = []
            for jj in range(4):
                j = fg * 4 + jj
                bank = pk[jj % 2]
                bname = f"pk_{jj % 2}"
                mt = None
                for k in range(DT):
                    mt = p.emit("tensor", lambda t, k=k, jj=jj, buf=buf, bank=bank: t.matmul(
                        bank[:, :], ipg[buf][k][:, jj * PT:(jj + 1) * PT], xn16[k][:, :],
                        start=(k == 0), stop=(k == DT - 1)),
                        deps=[lt[k], xnt[k]] + ([bank_free[bname]] if bname in bank_free and k == 0 else []))
                ipg_free[buf].append(mt)
                if j < IT:
                    ev = p.emit("vector", lambda v, j=j, bank=bank: v.tensor_copy(
                        xsb[j][:, 3:3 + Tc], bank[:, :]), deps=[mt])
                else:
                    ev = p.emit("scalar", lambda s, j=j, bank=bank: s.activation(
                        z16[j - IT][:, :], bank[:, :], AF.Silu), deps=[mt])
                bank_free[bname] = ev
                xz_ev.append(ev)
        conv_toks = []
        for it in range(IT):
            h0 = p.emit("vector", lambda v, it=it: v.tensor_copy(
                xsb[it][:, 0:3], halb[:, it * 3:(it + 1) * 3]),
                deps=[xz_ev[it], thal] + ([conv_toks[-1][0]] if conv_toks else []))
            a1 = p.emit("vector", lambda v, it=it: v.tensor_scalar(
                sc_a[:, :], xsb[it][:, 0:Tc], convw[:, it * 4:it * 4 + 1], None,
                ALU.mult), deps=[h0])
            for kk in range(1, 4):
                a1 = p.emit("vector", lambda v, it=it, kk=kk: v.scalar_tensor_tensor(
                    sc_a[:, :], xsb[it][:, kk:kk + Tc], convw[:, it * 4 + kk:it * 4 + kk + 1],
                    sc_a[:, :], ALU.mult, ALU.add), deps=[a1])
            uu = p.emit("scalar", lambda s, it=it: s.activation(
                u16[it][:, :], sc_a[:, :], AF.Silu, bias=convb[:, it:it + 1]),
                deps=[a1])
            hs = p.emit("vector", lambda v, it=it: v.tensor_copy(
                halb[:, it * 3:(it + 1) * 3], xsb[it][:, Tc:Tc + 3]), deps=[xz_ev[it], h0])
            conv_toks.append((uu, hs))
        if dbg and ch == 0:
            for it in range(IT):
                p.emit("gpsimd", lambda s, it=it: s.dma_start(
                    out=dbg_u[it * PT:(it + 1) * PT, :], in_=u16[it][:, :]),
                    deps=[conv_toks[it][0]], dma=True)
        if dbg and ch == 1:
            for it in range(IT):
                p.emit("gpsimd", lambda s, it=it: s.dma_start(
                    out=dbg_u1[it * PT:(it + 1) * PT, :], in_=u16[it][:, :]),
                    deps=[conv_toks[it][0]], dma=True)
        mt = None
        for k in range(IT):
            mt = p.emit("tensor", lambda t, k=k: t.matmul(
                pk[2][:, :], xpTs[k][:, :], u16[k][:, :],
                start=(k == 0), stop=(k == IT - 1)),
                deps=[conv_toks[k][0], rtok] + ([bank_free["pk_2"]] if "pk_2" in bank_free and k == 0 else []))
        evC = p.emit("vector", lambda v: v.tensor_copy(C16[:, :], pk[2][0:S, :]), deps=[mt])
        evB = p.emit("vector", lambda v: v.tensor_copy(B16[:, :], pk[2][32:32 + S, :]), deps=[mt])
        evd = p.emit("vector", lambda v: v.tensor_copy(dtraw16[:, :], pk[2][64:128, :]), deps=[mt])
        bank_free["pk_2"] = evd
        dphase = []
        for it in range(IT):
            bank = pk[3 - (it % 2)]
            bname = f"pk_{3 - (it % 2)}"
            mt = p.emit("tensor", lambda t, it=it, bank=bank: t.matmul(
                bank[:, :], dtpTs[:, it * PT:(it + 1) * PT], dtraw16[:, :],
                start=True, stop=True),
                deps=[evd, rtok] + ([bank_free[bname]] if bname in bank_free else []))
            r1 = p.emit("vector", lambda v, it=it, bank=bank: v.tensor_scalar(
                sc_a[:, :], bank[:, :], dtb[:, it:it + 1], None, ALU.add),
                deps=[mt] + ([dphase[-1]] if dphase else []))
            bank_free[bname] = r1
            r2 = p.emit("scalar", lambda s: s.activation(sc_b[:, :], sc_a[:, :], AF.Abs),
                        deps=[r1])
            r3 = p.emit("scalar", lambda s: s.activation(sc_b[:, :], sc_b[:, :], AF.Exp,
                                                         scale=-1.0), deps=[r2])
            r4 = p.emit("scalar", lambda s: s.activation(sc_b[:, :], sc_b[:, :], AF.Ln,
                                                         bias=1.0), deps=[r3])
            r5 = p.emit("vector", lambda v: v.tensor_single_scalar(
                sc_a[:, :], sc_a[:, :], 0.0, ALU.max), deps=[r4, r1])
            r6 = p.emit("vector", lambda v, it=it: v.tensor_add(
                delta[:, it * Tc:(it + 1) * Tc], sc_a[:, :], sc_b[:, :]), deps=[r5])
            dphase.append(r6)
        if dbg and ch == 0:
            p.emit("gpsimd", lambda s: s.dma_start(
                out=dbg_B[:, :], in_=B16[:, :]), deps=[evB], dma=True)
            p.emit("gpsimd", lambda s: s.dma_start(
                out=dbg_C[:, :], in_=C16[:, :]), deps=[evC], dma=True)
            p.emit("gpsimd", lambda s: s.dma_start(
                out=dbg_delta[:, :], in_=delta[:, :]), deps=[dphase[-1]], dma=True)
        op_in_toks = []
        prev_scan = dphase[-1]
        bb_tok = {0: None, 1: None}
        for it in range(IT):
            dsl = delta[:, it * Tc:(it + 1) * Tc]
            dut = p.emit("vector", lambda v, it=it, dsl=dsl: v.tensor_mul(
                sc_d[:, :], dsl, u16[it][:, :]), deps=[dphase[it], conv_toks[it][0], prev_scan])
            yac = xsb[it]
            ytok = None
            sdep = {}
            for s_ in range(S):
                par = s_ % 2
                pb = pk[par]
                pbn = f"pk_{par}"
                pc = pk[2 + par]
                pcn = f"pk_{2 + par}"
                sa = sc_a if par == 0 else sc_a2
                sbb_ = sc_b
                scc = sc_c
                col = it * S + s_
                tb = p.emit("tensor", lambda t, s_=s_, pb=pb: t.matmul(
                    pb[:, :], ind16[:, s_ * PT:(s_ + 1) * PT], B16[:, :],
                    start=True, stop=True),
                    deps=[evB, rtok] + ([bank_free[pbn]] if pbn in bank_free else []))
                tc = p.emit("tensor", lambda t, s_=s_, pc=pc: t.matmul(
                    pc[:, :], ind16[:, s_ * PT:(s_ + 1) * PT], C16[:, :],
                    start=True, stop=True),
                    deps=[evC] + ([bank_free[pcn]] if pcn in bank_free else []))
                da = p.emit("scalar", lambda s, col=col, dsl=dsl, sa=sa: s.activation(
                    sa[:, :], dsl, AF.Exp, scale=A_sb[:, col:col + 1]),
                    deps=[tA, dut] + ([sdep[("h", par)]] if ("h", par) in sdep else []))
                db = p.emit("vector", lambda v, pb=pb, sbb_=sbb_: v.tensor_mul(
                    sbb_[:, :], sc_d[:, :], pb[:, :]),
                    deps=[tb, dut] + ([sdep[("y", par)]] if ("y", par) in sdep else []))
                bank_free[pbn] = db
                hk = p.emit("vector", lambda v, col=col, sa=sa, sbb_=sbb_, scc=scc: v.tensor_tensor_scan(
                    scc[:, :], sa[:, :], sbb_[:, :], hcar[:, col:col + 1],
                    ALU.mult, ALU.add), deps=[da, db, thc] +
                    ([sdep[("yt", par)]] if ("yt", par) in sdep else []))
                hc = p.emit("vector", lambda v, col=col, scc=scc: v.tensor_copy(
                    hcar[:, col:col + 1], scc[:, 511:512]), deps=[hk])
                if s_ == 0:
                    ytok = p.emit("vector", lambda v, yac=yac, pc=pc, scc=scc: v.tensor_mul(
                        yac[:, 0:Tc], scc[:, :], pc[:, :]),
                        deps=[hk, tc, conv_toks[it][1], hc])
                    sdep[("yt", par)] = ytok
                else:
                    t1_ = p.emit("vector", lambda v, pc=pc, scc=scc, sbb_=sbb_: v.tensor_mul(
                        sbb_[:, :], scc[:, :], pc[:, :]), deps=[hk, tc, hc])
                    ytok = p.emit("vector", lambda v, yac=yac, sbb_=sbb_: v.tensor_add(
                        yac[:, 0:Tc], yac[:, 0:Tc], sbb_[:, :]), deps=[t1_, ytok])
                    sdep[("yt", par)] = t1_
                sdep[("h", par)] = ytok
                sdep[("y", par)] = ytok
                bank_free[pcn] = ytok
            q0 = p.emit("vector", lambda v, it=it: v.tensor_copy(
                sc_b[:, :], u16[it][:, :]), deps=[ytok, conv_toks[it][0]])
            q1 = p.emit("vector", lambda v, it=it: v.tensor_scalar(
                sc_a[:, :], sc_b[:, :], Dp[:, it:it + 1], None, ALU.mult),
                deps=[q0])
            q2 = p.emit("vector", lambda v, yac=yac: v.tensor_add(
                yac[:, 0:Tc], yac[:, 0:Tc], sc_a[:, :]), deps=[q1, ytok])
            q3 = p.emit("vector", lambda v, it=it, yac=yac: v.tensor_mul(
                u16[it][:, :], yac[:, 0:Tc], z16[it][:, :]), deps=[q2])
            if dbg and ch == 0:
                p.emit("gpsimd", lambda s, it=it: s.dma_start(
                    out=dbg_y[it * PT:(it + 1) * PT, :], in_=u16[it][:, :]),
                    deps=[q3], dma=True)
            op_in_toks.append(q3)
            prev_scan = q3
        if dbg and ch == 1:
            p.emit("gpsimd", lambda s: s.dma_start(
                out=dbg_hcar[:, :], in_=hcar[:, :]), deps=[op_in_toks[-1]], dma=True)
            p.emit("gpsimd", lambda s: s.dma_start(
                out=dbg_d1[:, :], in_=delta[:, :]), deps=[dphase[-1]], dma=True)
        rb2 = p.emit("vector", lambda v: v.tensor_copy(r16m, rrow), deps=[prev_scan])
        rbc1 = p.emit("tensor", lambda t: t.matmul(
            pbc[:, :], ones_row[0:1, :], m16, start=True, stop=True),
            deps=[rb2, bank_free["pbc"]] if "pbc" in bank_free else [rb2])
        rbc2 = p.emit("tensor", lambda t: t.matmul(
            pbc2[:, :], ones_row[0:1, :], r16m, start=True, stop=True),
            deps=[rb2, bank_free["pbc2"]] if "pbc2" in bank_free else [rb2])
        for i in range(DT):
            bank = pk[i % 2]
            bname = f"pk_{i % 2}"
            mt = None
            for k in range(IT):
                mt = p.emit("tensor", lambda t, i=i, k=k, bank=bank: t.matmul(
                    bank[:, :], opTs[k][:, i * PT:(i + 1) * PT], u16[k][:, :],
                    start=(k == 0), stop=(k == IT - 1)),
                    deps=[op_in_toks[k], rtok] + ([bank_free[bname]] if bname in bank_free and k == 0 else []))
            l1 = p.emit("vector", lambda v, i=i: v.tensor_sub(
                sc_a[:, :], x_sb[i][:, :], pbc[:, :]),
                deps=[rbc1, prev_scan] + ([bank_free["lnr"]] if "lnr" in bank_free else []))
            l2 = p.emit("vector", lambda v, i=i: v.tensor_mul(
                sc_a[:, :], sc_a[:, :], pbc2[:, :]), deps=[rbc2, l1])
            o1 = p.emit("vector", lambda v, i=i, bank=bank: v.scalar_tensor_tensor(
                x_sb[i][:, :], bank[:, :], 1.0, x_sb[i][:, :], ALU.mult, ALU.add),
                deps=[mt, l2])
            o2 = p.emit("vector", lambda v, i=i: v.tensor_add(
                x_sb[i][:, :], x_sb[i][:, :], sc_a[:, :]), deps=[o1])
            bank_free[bname] = o1
            bank_free["lnr"] = o2
            p.emit("gpsimd", lambda s, i=i, ts_=ts_: s.dma_start(
                out=x_out[i * PT:(i + 1) * PT, ts_], in_=x_sb[i][:, :]),
                deps=[o2], dma=True)
            if i == DT - 1:
                prev_end = [p.gate("gpsimd")]
        bank_free["pbc"] = bank_free["lnr"]
        bank_free["pbc2"] = bank_free["lnr"]

    toks = p.barrier_tokens()
    p.emit("sync", lambda s: s.nop(), deps=toks)
    with nc.Block() as block:
        p.finalize(block)
    return nc


def _prep_mamba1_inputs(xcore, norm_w, in_proj_w, conv_w, conv_b, x_proj_w,
                        dt_proj_w, dt_proj_b, A_log, D_param, out_proj_w):
    DI, S, DTR = 2048, 16, 64
    xp = np.asarray(x_proj_w)
    xpT_r = np.zeros((DI, 128), np.float32)
    xpT_r[:, 0:16] = xp[80:96].T
    xpT_r[:, 32:48] = xp[64:80].T
    xpT_r[:, 64:128] = xp[0:64].T
    ind = np.zeros((S, S * 128), np.float32)
    for s in range(S):
        ind[s, s * 128:(s + 1) * 128] = 1.0
    return {
        "x_in": np.ascontiguousarray(xcore),
        "nw_c": np.ascontiguousarray(np.asarray(norm_w).reshape(8, 128).T),
        "ipT": np.ascontiguousarray(np.asarray(in_proj_w).T),
        "convw_c": np.ascontiguousarray(
            np.asarray(conv_w).reshape(16, 128, 4).transpose(1, 0, 2).reshape(128, 64)),
        "convb_c": np.ascontiguousarray(np.asarray(conv_b).reshape(16, 128).T),
        "xpT_r": xpT_r,
        "dtpT": np.ascontiguousarray(np.asarray(dt_proj_w).T),
        "dtb_c": np.ascontiguousarray(np.asarray(dt_proj_b).reshape(16, 128).T),
        "Alog_c": np.ascontiguousarray(
            np.asarray(A_log).reshape(16, 128, 16).transpose(1, 0, 2).reshape(128, 256)),
        "D_c": np.ascontiguousarray(np.asarray(D_param).reshape(16, 128).T),
        "opT": np.ascontiguousarray(np.asarray(out_proj_w).T),
        "ind_in": ind,
    }


def build_moe_stats():
    TB = 512
    NB = T // TB

    nc = bass.Bass("TRN2", target_bir_lowering=False)
    x_in = nc.dram_tensor("x_in", [D, T], F32, kind="ExternalInput")
    waffT = nc.dram_tensor("waffT", [D, 8], F32, kind="ExternalInput")
    xn_d = nc.dram_tensor("xn_d", [D, T], BF16, kind="ExternalOutput")
    aff_d = nc.dram_tensor("aff_d", [8, T], F32, kind="ExternalOutput")
    asum_d = nc.dram_tensor("asum_d", [1, 1], F32, kind="ExternalOutput")

    es = nc.ctx
    sb = lambda n, s, d=F32: es.enter_context(nc.sbuf_tensor(n, s, d))
    psf = lambda n, s, d=F32: es.enter_context(nc.psum_tensor(n, s, d))

    waf = [sb(f"waf_{k}", [PT, 8]) for k in range(DT)]
    wsq = sb("wsq", [PT, 8])
    wn8 = sb("wn8", [8, 1])
    ones_col = sb("ones_colS", [PT, 1])
    ones_row = sb("ones_rowS", [1, PT])
    eps_col = sb("eps_colS", [PT, 1])
    x_sb = [sb(f"x_{i}", [PT, TB]) for i in range(DT)]
    xsq = sb("xsq", [PT, TB])
    xnf = sb("xnf", [PT, TB])
    xn16 = [sb(f"xn_{i}", [PT, TB], BF16) for i in range(DT)]
    mrow = sb("m_row", [1, TB])[0:1, :]
    vrow = sb("v_row", [1, TB])[0:1, :]
    rrow = sb("r_row", [1, TB])[0:1, :]
    xnr = sb("xnr_row", [1, TB])[0:1, :]
    srow = sb("s_row", [1, TB])[0:1, :]
    aff_sb = sb("aff_sb", [8, TB])
    den_sb = sb("den_sb", [8, TB])
    asum8 = sb("asum8", [8, 1])
    acc8 = sb("acc8", [8, 1])
    ones8f = sb("ones8f", [8, 1])
    one1 = sb("one1", [1, 1])

    pstat_s = psf("pstat_s", [1, TB])
    pstat_q = psf("pstat_q", [1, TB])
    pbc = psf("pbc", [PT, TB])
    pbc2 = psf("pbc2", [PT, TB])
    pnum = psf("pnum", [8, TB])
    pden = psf("pden", [8, TB])
    pwn = psf("pwn", [8, 1])
    ptot = psf("ptot", [1, 1])

    p = Prog(nc)
    for k in range(DT):
        p.emit("sync", lambda s, k=k: s.dma_start(
            out=waf[k][:, :], in_=waffT[k * PT:(k + 1) * PT, :]), dma=True)
    gw = p.gate("sync")
    wts = [gw] * DT
    tco = p.emit("vector", lambda v: v.memset(ones_col[:, :], 1.0))
    tro = p.emit("vector", lambda v: v.memset(ones_row[:, :], 1.0))
    t8f = p.emit("vector", lambda v: v.memset(ones8f[:, :], 1.0))
    tep = p.emit("vector", lambda v: v.memset(eps_col[:, :], LN_EPS))
    tz8 = p.emit("vector", lambda v: v.memset(acc8[:, :], 0.0))
    mt = None
    for k in range(DT):
        tq = p.emit("scalar", lambda s, k=k: s.activation(
            wsq[:, :], waf[k][:, :], AF.Square), deps=[wts[k]] + ([mt] if mt else []))
        mt = p.emit("tensor", lambda t, k=k: t.matmul(
            pwn[:, 0:1], wsq[:, :], ones_col[:, 0:1],
            start=(k == 0), stop=(k == DT - 1)), deps=[tq, tco])
    tw1 = p.emit("scalar", lambda s: s.activation(wn8[:, :], pwn[:, 0:1], AF.Sqrt),
                 deps=[mt])

    prev_end = []
    asum_toks = []
    prev_stat_s = prev_stat_q = None
    xn_toks = []
    afft = dent = None
    for blk in range(NB):
        ts_ = slice(blk * TB, (blk + 1) * TB)
        for i in range(DT):
            p.emit("sync", lambda s, i=i, ts_=ts_: s.dma_start(
                out=x_sb[i][:, :], in_=x_in[i * PT:(i + 1) * PT, ts_]),
                deps=prev_end, dma=True)
        gx = p.gate("sync")
        xt = [gx] * DT
        prev_end = []
        stat_toks = []
        for i in range(DT):
            c2 = p.emit("scalar", lambda s, i=i: s.activation(
                xsq[:, :], x_sb[i][:, :], AF.Square),
                deps=[xt[i]] + ([stat_toks[-1]] if stat_toks else []))
            m1 = p.emit("tensor", lambda t, i=i: t.matmul(
                pstat_s[0:1, :], x_sb[i][:, 0:1].bitcast(F32) if False else ones_col[:, 0:1],
                x_sb[i][:, :], start=(i == 0), stop=(i == DT - 1)),
                deps=[xt[i], tco] + ([prev_stat_s] if i == 0 and blk > 0 else []))
            m2 = p.emit("tensor", lambda t, i=i: t.matmul(
                pstat_q[0:1, :], ones_col[:, 0:1], xsq[:, :],
                start=(i == 0), stop=(i == DT - 1)),
                deps=[c2] + ([prev_stat_q] if i == 0 and blk > 0 else []))
            stat_toks.append(m2)
        st = stat_toks[-1]
        e2 = p.emit("vector", lambda v: v.tensor_single_scalar(
            mrow, pstat_s[0:1, :], 1.0 / D, ALU.mult), deps=[st])
        prev_stat_s = e2
        e3 = p.emit("scalar", lambda s: s.activation(srow, mrow, AF.Square), deps=[e2])
        e4 = p.emit("vector", lambda v: v.tensor_single_scalar(
            vrow, pstat_q[0:1, :], 1.0 / D, ALU.mult), deps=[st])
        prev_stat_q = e4
        e5 = p.emit("vector", lambda v: v.tensor_sub(vrow, vrow, srow), deps=[e4, e3])
        e6 = p.emit("scalar", lambda s: s.activation(
            srow, vrow, AF.Sqrt, bias=eps_col[0:1, 0:1]), deps=[e5, tep])
        e7 = p.emit("vector", lambda v: v.reciprocal(rrow, srow), deps=[e6])
        e8 = p.emit("scalar", lambda s: s.activation(srow, rrow, AF.Square), deps=[e7])
        e9 = p.emit("vector", lambda v: v.tensor_mul(xnr, vrow, srow), deps=[e8, e5])
        e10 = p.emit("vector", lambda v: v.tensor_single_scalar(
            xnr, xnr, float(D), ALU.mult), deps=[e9])
        e11 = p.emit("scalar", lambda s: s.activation(xnr, xnr, AF.Sqrt), deps=[e10])
        bc1 = p.emit("tensor", lambda t: t.matmul(
            pbc[:, :], ones_row[0:1, :], mrow, start=True, stop=True),
            deps=[e2, tro] + ([xn_toks[-1]] if blk > 0 else []))
        bc2 = p.emit("tensor", lambda t: t.matmul(
            pbc2[:, :], ones_row[0:1, :], rrow, start=True, stop=True),
            deps=[e7] + ([xn_toks[-1]] if blk > 0 else []))
        bc3 = p.emit("tensor", lambda t: t.matmul(
            pden[0:8, :], ones_row[0:1, 0:8], xnr, start=True, stop=True),
            deps=[e11] + ([dent] if blk > 0 else []))
        xn_toks = []
        numt = None
        for i in range(DT):
            s1 = p.emit("vector", lambda v, i=i: v.tensor_sub(
                x_sb[i][:, :], x_sb[i][:, :], pbc[:, :]), deps=[bc1, xt[i]])
            s2 = p.emit("vector", lambda v, i=i: v.tensor_mul(
                xnf[:, :], x_sb[i][:, :], pbc2[:, :]),
                deps=[bc2, s1] + ([numt] if numt else []))
            s3 = p.emit("vector", lambda v, i=i: v.tensor_copy(
                xn16[i][:, :], xnf[:, :]), deps=[s2])
            xn_toks.append(s3)
            p.emit("gpsimd", lambda g, i=i, ts_=ts_: g.dma_start(
                out=xn_d[i * PT:(i + 1) * PT, ts_], in_=xn16[i][:, :]),
                deps=[s3], dma=True)
            numt = p.emit("tensor", lambda t, i=i: t.matmul(
                pnum[0:8, :], waf[i][:, :], xnf[:, :],
                start=(i == 0), stop=(i == DT - 1)),
                deps=[s2, wts[i]] + ([afft] if blk > 0 and i == 0 else []))
        d1 = p.emit("vector", lambda v: v.tensor_scalar(
            den_sb[0:8, :], pden[0:8, :], wn8[:, 0:1], None, ALU.mult),
            deps=[bc3, tw1])
        dent = d1
        d2 = p.emit("vector", lambda v: v.tensor_single_scalar(
            den_sb[0:8, :], den_sb[0:8, :], 1e-9, ALU.add), deps=[d1])
        d3 = p.emit("vector", lambda v: v.reciprocal(den_sb[0:8, :], den_sb[0:8, :]),
                    deps=[d2])
        a1 = p.emit("vector", lambda v: v.tensor_mul(
            aff_sb[0:8, :], pnum[0:8, :], den_sb[0:8, :]), deps=[numt, d3])
        afft = a1
        p.emit("sync", lambda s, ts_=ts_: s.dma_start(
            out=aff_d[:, ts_], in_=aff_sb[0:8, :]), deps=[a1], dma=True)
        prev_end = [p.gate("gpsimd"), p.gate("sync")]
        a2 = p.emit("vector", lambda v: v.tensor_scalar(
            den_sb[0:8, :], aff_sb[0:8, :], 0.0, 0.0, ALU.add, ALU.add,
            accum_out=asum8[:, 0:1]), deps=[a1, d3])
        a3 = p.emit("vector", lambda v: v.tensor_add(
            acc8[:, :], acc8[:, :], asum8[:, 0:1]), deps=[a2, tz8])
        asum_toks.append(a3)
    tt = p.emit("tensor", lambda t: t.matmul(
        ptot[0:1, 0:1], acc8[:, 0:1], ones8f[:, 0:1], start=True, stop=True),
        deps=[asum_toks[-1], t8f])
    tv = p.emit("vector", lambda v: v.tensor_copy(one1[0:1, 0:1], ptot[0:1, 0:1]),
                deps=[tt])
    p.emit("gpsimd", lambda g: g.dma_start(out=asum_d[0:1, 0:1], in_=one1[0:1, 0:1]),
           deps=[tv], dma=True)

    toks = p.barrier_tokens()
    p.emit("sync", lambda s: s.nop(), deps=toks)
    with nc.Block() as block:
        p.finalize(block)
    return nc


def build_moe_main(dbg=False):
    E = 8
    FH = 1024
    Tb = 512
    NB = T // Tb
    TT_ = T // PT
    SCAP = 2048.0
    MINC = 4.0

    nc = bass.Bass("TRN2", target_bir_lowering=False)
    x_in = nc.dram_tensor("x_in", [D, T], F32, kind="ExternalInput")
    xn_din = nc.dram_tensor("xn_din", [D, T], BF16, kind="ExternalInput")
    aff_din = nc.dram_tensor("aff_din", [E, T], F32, kind="ExternalInput")
    parts = nc.dram_tensor("parts", [E, E], F32, kind="ExternalInput")
    thr_in = nc.dram_tensor("thr_in", [E, 1], F32, kind="ExternalInput")
    W1 = nc.dram_tensor("W1", [E, D, 4096], F32, kind="ExternalInput")
    W2 = nc.dram_tensor("W2", [E, 4096, D], F32, kind="ExternalInput")
    b1c = nc.dram_tensor("b1c", [PT, E * 32], F32, kind="ExternalInput")
    b2r = nc.dram_tensor("b2r", [E, D], F32, kind="ExternalInput")
    iota8_in = nc.dram_tensor("iota8_in", [PT, E], F32, kind="ExternalInput")
    iotaB_in = nc.dram_tensor("iotaB_in", [PT, E], F32, kind="ExternalInput")
    ident_in = nc.dram_tensor("ident_in", [PT, PT], F32, kind="ExternalInput")
    x_out = nc.dram_tensor("x_out", [D, T], F32, kind="ExternalOutput")
    if dbg:
        dbg_rows = nc.dram_tensor("dbg_rows", [8, 8], F32, kind="ExternalOutput")
        dbg_mtok = nc.dram_tensor("dbg_mtok", [PT, 256], F32, kind="ExternalOutput")
        dbg_acc = nc.dram_tensor("dbg_acc", [PT, D], F32, kind="ExternalOutput")
        dbg_csum = nc.dram_tensor("dbg_csum", [1, D], F32, kind="ExternalOutput")

    from contextlib import ExitStack
    es = nc.ctx
    sb = lambda n, s, d=F32: es.enter_context(nc.sbuf_tensor(n, s, d))
    psf = lambda n, s, d=F32: es.enter_context(nc.psum_tensor(n, s, d))

    xn16 = [sb(f"xn_{i}", [PT, T], BF16) for i in range(DT)]
    acc16 = [sb(f"acc_{tt}", [PT, D], BF16) for tt in range(TT_)]
    m_tok = sb("m_tok", [PT, TT_ * E])
    b1s = sb("b1s", [PT, E * 32])
    g1all = sb("g1all", [PT, E * 32], BF16)
    csum = sb("csum", [1, D])[0:1, :]
    crow = sb("crow", [1, D])[0:1, :]
    negc16 = sb("negc16", [1, D], BF16)[0:1, :]
    ones_row = sb("ones_rowM", [1, PT], BF16)
    ident16 = sb("identM", [PT, PT])
    rows8 = {}
    for nm in ["cap8", "lo8", "hi8", "mid8", "cnt8", "ge8", "t8a", "t8b",
               "rem8", "gt8"]:
        rows8[nm] = sb(nm, [8, 1])
    parts_sb = sb("parts_sb", [E, E])
    thr8 = sb("thr8", [E, 1])
    tot8 = sb("tot8", [E, 1])
    cri8 = sb("cri8", [E, 1])

    psA = psf("psA", [PT, Tb])
    psB = psf("psB", [PT, Tb])
    psC = psf("psC", [PT, Tb])
    psD = psf("psD", [PT, Tb])
    psE = psf("psE", [PT, Tb])
    psF = psf("psF", [PT, Tb])
    psR = psf("psR", [1, 512])
    psT = psf("psT", [PT, PT])

    p = Prog(nc)

    for i in range(DT):
        p.emit("sync", lambda s, i=i: s.dma_start(
            out=xn16[i][:, :], in_=xn_din[i * PT:(i + 1) * PT, :]), dma=True)
    p.emit("sync", lambda s: s.dma_start(out=b1s[:, :], in_=b1c[:, :]), dma=True)
    p.emit("sync", lambda s: s.dma_start(out=ident16[:, :], in_=ident_in[:, :]),
           dma=True)
    p.emit("sync", lambda s: s.dma_start(out=parts_sb[:, :], in_=parts[:, :]),
           dma=True)
    p.emit("sync", lambda s: s.dma_start(out=thr8[:, :], in_=thr_in[:, :]),
           dma=True)
    g0 = p.gate("sync")
    xnt = [g0] * DT
    tb1 = tid = tpt = tth = g0
    tg1 = p.emit("scalar", lambda s: s.activation(g1all[:, :], b1s[:, :], AF.Gelu),
                 deps=[tb1])
    tro = p.emit("vector", lambda v: v.memset(ones_row[:, :], 1.0))
    tcz = p.emit("vector", lambda v: v.memset(csum, 0.0))

    rs = ExitStack()
    aff_sb = rs.enter_context(nc.sbuf_tensor("aff_sb", [E, T], F32))
    eqm = rs.enter_context(nc.sbuf_tensor("eqm", [E, T], F32))
    pfx = rs.enter_context(nc.sbuf_tensor("pfx", [E, T], F32))
    gtm = rs.enter_context(nc.sbuf_tensor("gtm", [E, T], F32))
    affT = rs.enter_context(nc.sbuf_tensor("affT", [PT, TT_ * E], F32))
    ecrT = rs.enter_context(nc.sbuf_tensor("ecrT", [PT, TT_ * E], F32))
    io8 = rs.enter_context(nc.sbuf_tensor("io8", [PT, E], F32))
    ioB = rs.enter_context(nc.sbuf_tensor("ioB", [PT, E], F32))
    tcv = rs.enter_context(nc.sbuf_tensor("tcv", [PT, E], F32))
    mx1 = rs.enter_context(nc.sbuf_tensor("mx1", [PT, 1], F32))

    p.emit("sync", lambda s: s.dma_start(out=aff_sb[:, :], in_=aff_din[:, :]),
           dma=True)
    p.emit("sync", lambda s: s.dma_start(out=io8[:, :], in_=iota8_in[:, :]),
           dma=True)
    p.emit("sync", lambda s: s.dma_start(out=ioB[:, :], in_=iotaB_in[:, :]),
           dma=True)
    g1r = p.gate("sync")
    ta = ti1 = ti2 = g1r
    s1 = p.emit("vector", lambda v: v.tensor_reduce(
        tot8[:, 0:1], parts_sb[:, :], AX.X, ALU.add), deps=[tpt])
    s2 = p.emit("vector", lambda v: v.tensor_single_scalar(
        tot8[:, 0:1], tot8[:, 0:1], 1.0 / (NCORES * T * E), ALU.mult), deps=[s1])
    s3 = p.emit("vector", lambda v: v.tensor_sub(
        tot8[:, 0:1], tot8[:, 0:1], thr8[:, 0:1]), deps=[s2, tth])
    s4 = p.emit("scalar", lambda s: s.activation(
        tot8[:, 0:1], tot8[:, 0:1], AF.Sigmoid), deps=[s3])
    s5 = p.emit("vector", lambda v: v.tensor_single_scalar(
        tot8[:, 0:1], tot8[:, 0:1], SCAP, ALU.mult), deps=[s4])
    s6 = p.emit("vector", lambda v: v.tensor_copy(
        thr8[:, 0:1].bitcast(mybir.dt.int32), tot8[:, 0:1]), deps=[s5, s3])
    s7 = p.emit("vector", lambda v: v.tensor_copy(
        cri8[:, 0:1], thr8[:, 0:1].bitcast(mybir.dt.int32)), deps=[s6])
    s8 = p.emit("vector", lambda v: v.tensor_tensor(
        thr8[:, 0:1], cri8[:, 0:1], tot8[:, 0:1], ALU.is_gt), deps=[s7, s5])
    s9 = p.emit("vector", lambda v: v.tensor_sub(
        rows8["cap8"][:, 0:1], cri8[:, 0:1], thr8[:, 0:1]), deps=[s8])
    s10 = p.emit("vector", lambda v: v.tensor_single_scalar(
        rows8["cap8"][:, 0:1], rows8["cap8"][:, 0:1], MINC, ALU.max), deps=[s9])
    cp8 = p.emit("vector", lambda v: v.tensor_single_scalar(
        rows8["cap8"][:, 0:1], rows8["cap8"][:, 0:1], SCAP, ALU.min), deps=[s10])
    l0 = p.emit("vector", lambda v: v.tensor_reduce(
        rows8["lo8"][:, 0:1], aff_sb[:, :], AX.X, ALU.min), deps=[ta])
    l1 = p.emit("vector", lambda v: v.tensor_single_scalar(
        rows8["lo8"][:, 0:1], rows8["lo8"][:, 0:1], 1e-3, ALU.subtract), deps=[l0])
    h1 = p.emit("vector", lambda v: v.tensor_reduce(
        rows8["hi8"][:, 0:1], aff_sb[:, :], AX.X, ALU.max), deps=[ta, l0])
    last = h1
    for _ in range(34):
        b1_ = p.emit("vector", lambda v: v.tensor_add(
            rows8["mid8"][:, 0:1], rows8["lo8"][:, 0:1], rows8["hi8"][:, 0:1]),
            deps=[last])
        b2_ = p.emit("vector", lambda v: v.tensor_single_scalar(
            rows8["mid8"][:, 0:1], rows8["mid8"][:, 0:1], 0.5, ALU.mult), deps=[b1_])
        b3_ = p.emit("vector", lambda v: v.tensor_scalar(
            eqm[:, :], aff_sb[:, :], rows8["mid8"][:, 0:1], 0.0,
            ALU.is_gt, ALU.add, accum_out=rows8["cnt8"][:, 0:1]), deps=[b2_, ta])
        b4_ = p.emit("vector", lambda v: v.tensor_tensor(
            rows8["ge8"][:, 0:1], rows8["cnt8"][:, 0:1], rows8["cap8"][:, 0:1],
            ALU.is_ge), deps=[b3_, cp8])
        b5_ = p.emit("vector", lambda v: v.tensor_sub(
            rows8["t8a"][:, 0:1], rows8["mid8"][:, 0:1], rows8["lo8"][:, 0:1]),
            deps=[b4_])
        b6_ = p.emit("vector", lambda v: v.tensor_mul(
            rows8["t8a"][:, 0:1], rows8["t8a"][:, 0:1], rows8["ge8"][:, 0:1]),
            deps=[b5_])
        b7_ = p.emit("vector", lambda v: v.tensor_add(
            rows8["lo8"][:, 0:1], rows8["lo8"][:, 0:1], rows8["t8a"][:, 0:1]),
            deps=[b6_])
        b8_ = p.emit("vector", lambda v: v.tensor_sub(
            rows8["t8b"][:, 0:1], rows8["hi8"][:, 0:1], rows8["mid8"][:, 0:1]),
            deps=[b4_])
        b9_ = p.emit("vector", lambda v: v.tensor_mul(
            rows8["t8b"][:, 0:1], rows8["t8b"][:, 0:1], rows8["ge8"][:, 0:1]),
            deps=[b8_])
        b10_ = p.emit("vector", lambda v: v.tensor_add(
            rows8["hi8"][:, 0:1], rows8["mid8"][:, 0:1], rows8["t8b"][:, 0:1]),
            deps=[b9_, b7_])
        last = b10_
    g1_ = p.emit("vector", lambda v: v.tensor_scalar(
        gtm[:, :], aff_sb[:, :], rows8["hi8"][:, 0:1], 0.0,
        ALU.is_gt, ALU.add, accum_out=rows8["gt8"][:, 0:1]), deps=[last])
    g2_ = p.emit("vector", lambda v: v.tensor_sub(
        rows8["rem8"][:, 0:1], rows8["cap8"][:, 0:1], rows8["gt8"][:, 0:1]),
        deps=[g1_])
    g3_ = p.emit("vector", lambda v: v.tensor_scalar(
        eqm[:, :], aff_sb[:, :], rows8["hi8"][:, 0:1], None,
        ALU.is_equal), deps=[g1_])
    g4_ = p.emit("vector", lambda v: v.tensor_tensor_scan(
        pfx[:, :], eqm[:, :], eqm[:, :], 0.0, ALU.add, ALU.bypass), deps=[g3_])
    g5_ = p.emit("vector", lambda v: v.tensor_scalar(
        pfx[:, :], pfx[:, :], rows8["rem8"][:, 0:1], None, ALU.is_le), deps=[g4_, g2_])
    g6_ = p.emit("vector", lambda v: v.tensor_mul(pfx[:, :], pfx[:, :], eqm[:, :]),
                 deps=[g5_])
    g7_ = p.emit("vector", lambda v: v.tensor_add(gtm[:, :], gtm[:, :], pfx[:, :]),
                 deps=[g6_])
    mtoks = []
    for tt in range(TT_):
        t1_ = p.emit("tensor", lambda t, tt=tt: t.transpose(
            psT[:, 0:8], aff_sb[:, tt * PT:(tt + 1) * PT], ident16[0:8, 0:8]),
            deps=[ta, tid] + ([mtoks[-1]] if mtoks else []))
        t2_ = p.emit("vector", lambda v, tt=tt: v.tensor_copy(
            affT[:, tt * E:(tt + 1) * E], psT[:, 0:8]), deps=[t1_])
        t3_ = p.emit("tensor", lambda t, tt=tt: t.transpose(
            psT[:, 8:16], gtm[:, tt * PT:(tt + 1) * PT], ident16[0:8, 0:8]),
            deps=[g7_, t2_])
        t4_ = p.emit("vector", lambda v, tt=tt: v.tensor_copy(
            ecrT[:, tt * E:(tt + 1) * E], psT[:, 8:16]), deps=[t3_])
        t5_ = p.emit("vector", lambda v, tt=tt: v.tensor_reduce(
            mx1[:, 0:1], affT[:, tt * E:(tt + 1) * E], AX.X, ALU.max), deps=[t2_])
        t6_ = p.emit("vector", lambda v, tt=tt: v.tensor_scalar(
            tcv[:, :], affT[:, tt * E:(tt + 1) * E], mx1[:, 0:1], None,
            ALU.is_ge), deps=[t5_])
        t7_ = p.emit("vector", lambda v: v.tensor_mul(tcv[:, :], tcv[:, :], ioB[:, :]),
                     deps=[t6_, ti2])
        t8_ = p.emit("vector", lambda v: v.tensor_single_scalar(
            tcv[:, :], tcv[:, :], 999.0, ALU.add), deps=[t7_])
        t9_ = p.emit("vector", lambda v: v.tensor_reduce(
            mx1[:, 0:1], tcv[:, :], AX.X, ALU.min), deps=[t8_])
        t10_ = p.emit("vector", lambda v, tt=tt: v.tensor_scalar(
            tcv[:, :], io8[:, :], mx1[:, 0:1], None, ALU.is_equal),
            deps=[t9_, ti1])
        t11_ = p.emit("vector", lambda v, tt=tt: v.tensor_mul(
            m_tok[:, tt * E:(tt + 1) * E], tcv[:, :], ecrT[:, tt * E:(tt + 1) * E]),
            deps=[t10_, t4_])
        mtoks.append(t11_)
    route_done = mtoks[-1]
    if dbg:
        for j, nm in enumerate(["cap8", "hi8", "gt8", "rem8", "cnt8", "lo8"]):
            p.emit("gpsimd", lambda g, j=j, nm=nm: g.dma_start(
                out=dbg_rows[:, j:j + 1], in_=rows8[nm][:, 0:1],
                allow_slow_non_contiguous=True),
                deps=[route_done], dma=True)
        p.emit("gpsimd", lambda g: g.dma_start(
            out=dbg_mtok[:, :], in_=m_tok[:, :]), deps=[route_done], dma=True)
    rs.close()

    es2 = ExitStack()
    w1b = [es2.enter_context(nc.sbuf_tensor(f"w1b_{k}", [PT, FH], BF16)) for k in range(DT)]
    w2b = [es2.enter_context(nc.sbuf_tensor(f"w2b_{k}", [PT, D], BF16)) for k in range(8)]
    H16 = [es2.enter_context(nc.sbuf_tensor(f"H_{j}", [PT, Tb], BF16)) for j in range(8)]
    acc_init = []
    for tt in range(TT_):
        acc_init.append(p.emit("vector", lambda v, tt=tt: v.memset(
            acc16[tt][:, :], 0.0)))
    bankA = {"psA": None, "psB": None}
    bankG = {}
    w1_free, w2_free = [route_done], [route_done]
    csum_tok = tcz
    negc_tok = None
    for e in range(E):
        for fh in range(4):
            for k in range(DT):
                p.emit("gpsimd", lambda g, k=k, e=e, fh=fh: g.dma_start(
                    out=w1b[k][:, :], in_=W1[e, k * PT:(k + 1) * PT,
                                             fh * FH:(fh + 1) * FH]),
                    deps=w1_free, dma=True)
            for k in range(8):
                p.emit("gpsimd", lambda g, k=k, e=e, fh=fh: g.dma_start(
                    out=w2b[k][:, :], in_=W2[e, fh * FH + k * PT: fh * FH + (k + 1) * PT, :]),
                    deps=w2_free, dma=True)
            gwt = p.gate("gpsimd")
            wt1 = [gwt] * DT
            wt2 = [gwt] * 8
            w1_free, w2_free = [], []
            cv = None
            for dblk in range(2):
                mt = None
                for k in range(8):
                    mt = p.emit("tensor", lambda t, e=e, fh=fh, k=k, dblk=dblk: t.matmul(
                        psR[0:1, :],
                        g1all[:, e * 32 + fh * 8 + k:e * 32 + fh * 8 + k + 1],
                        w2b[k][:, dblk * 512:(dblk + 1) * 512],
                        start=(k == 0), stop=(k == 7)),
                        deps=[wt2[k], tg1] + ([csum_tok if dblk == 0 else cv]
                                              if k == 0 else []))
                cv = p.emit("vector", lambda v, dblk=dblk: v.tensor_copy(
                    crow[0:1, dblk * 512:(dblk + 1) * 512], psR[0:1, :]), deps=[mt])
            csum_tok = p.emit("vector", lambda v: v.tensor_add(csum, csum, crow),
                              deps=[cv, csum_tok])
            ng = p.emit("vector", lambda v: v.tensor_single_scalar(
                negc16, crow, -1.0, ALU.mult), deps=[cv] + ([negc_tok] if negc_tok else []))
            negc_tok = ng
            mm2_done = None
            for tb in range(NB):
                tsl = slice(tb * Tb, (tb + 1) * Tb)
                htoks = []
                for j in range(8):
                    bank = psA if j % 2 == 0 else psB
                    bname = "psA" if j % 2 == 0 else "psB"
                    mt = None
                    for k in range(DT):
                        mt = p.emit("tensor", lambda t, j=j, k=k, e=e, fh=fh, bank=bank, tsl=tsl: t.matmul(
                            bank[:, :], w1b[k][:, j * PT:(j + 1) * PT],
                            xn16[k][:, tsl], start=(k == 0), stop=(k == DT - 1)),
                            deps=[wt1[k], xnt[k]] +
                            ([bankA[bname]] if bankA[bname] and k == 0 else []))
                    ht = p.emit("scalar", lambda s, j=j, e=e, fh=fh, bank=bank: s.activation(
                        H16[j][:, :], bank[:, :], AF.Gelu,
                        bias=b1s[:, e * 32 + fh * 8 + j:e * 32 + fh * 8 + j + 1]),
                        deps=[mt, tb1] + ([mm2_done] if mm2_done else []))
                    bankA[bname] = ht
                    htoks.append(ht)
                w1_free.extend(htoks[-2:])
                for tt in range(4):
                    gtt = tb * 4 + tt
                    banks = [[psC, psD, psE, psF][(tt * 2 + dh) % 4] for dh in range(2)]
                    gnames = [f"b{(tt * 2 + dh) % 4}" for dh in range(2)]
                    gdeps = [([bankG[g]] if bankG.get(g) else []) for g in gnames]
                    mts = [None, None]
                    for k in range(8):
                        for dh in range(2):
                            mts[dh] = p.emit("tensor", lambda t, k=k, tt=tt, dh=dh, bank=banks[dh]: t.matmul(
                                bank[:, :], H16[k][:, tt * PT:(tt + 1) * PT],
                                w2b[k][:, dh * 512:(dh + 1) * 512],
                                start=(k == 0), stop=False),
                                deps=[htoks[k], wt2[k]] + (gdeps[dh] if k == 0 else []))
                    for dh in range(2):
                        mt = p.emit("tensor", lambda t, dh=dh, bank=banks[dh]: t.matmul(
                            bank[:, :], ones_row[0:1, :],
                            negc16[0:1, dh * 512:(dh + 1) * 512],
                            start=False, stop=True), deps=[ng, tro])
                        au = p.emit("vector", lambda v, gtt=gtt, dh=dh, e=e, bank=banks[dh]: v.scalar_tensor_tensor(
                            acc16[gtt][:, dh * 512:(dh + 1) * 512], bank[:, :],
                            m_tok[:, gtt * E + e:gtt * E + e + 1],
                            acc16[gtt][:, dh * 512:(dh + 1) * 512],
                            ALU.mult, ALU.add),
                            deps=[mt, route_done, acc_init[gtt]])
                        bankG[gnames[dh]] = au
                        mm2_done = au
                        if tb == NB - 1:
                            w2_free.append(au)
    if dbg:
        p.emit("gpsimd", lambda g: g.dma_start(
            out=dbg_acc[:, :], in_=acc16[0][:, :]),
            deps=list(bankG.values()), dma=True)
        p.emit("gpsimd", lambda g: g.dma_start(
            out=dbg_csum[0:1, :], in_=csum), deps=[csum_tok], dma=True)
    es2.close()
    expert_done = list(bankG.values()) + [csum_tok]
    b2r16 = sb("b2r16", [E, D], BF16)
    cst16 = sb("cst16", [1, D], BF16)[0:1, :]
    cstb = sb("cstb", [PT, D])
    xfin = [sb(f"xf_{i}", [PT, PT]) for i in range(DT)]
    tmpf = sb("tmpf", [PT, D])
    ones8c = sb("ones8c", [E, 1], BF16)
    p.emit("gpsimd", lambda g: g.dma_start(out=b2r16[:, :], in_=b2r[:, :]),
           deps=expert_done, dma=True)
    tb2 = p.gate("gpsimd")
    t8c = p.emit("vector", lambda v: v.memset(ones8c[:, :], 1.0))
    cv1 = None
    for dblk in range(2):
        mt = p.emit("tensor", lambda t, dblk=dblk: t.matmul(
            psR[0:1, :], ones8c[:, 0:1], b2r16[:, dblk * 512:(dblk + 1) * 512],
            start=True, stop=True),
            deps=[tb2, t8c, csum_tok] + ([cv1] if cv1 else []))
        cv1 = p.emit("vector", lambda v, dblk=dblk: v.tensor_add(
            crow[0:1, dblk * 512:(dblk + 1) * 512], csum[0:1, dblk * 512:(dblk + 1) * 512],
            psR[0:1, :]), deps=[mt, csum_tok])
    cv2 = p.emit("vector", lambda v: v.tensor_copy(cst16, crow), deps=[cv1])
    cb_toks = []
    for c_ in range(DT):
        bk = p.emit("tensor", lambda t, c_=c_: t.matmul(
            psT[:, 0:PT], ones_row[0:1, :], cst16[0:1, c_ * PT:(c_ + 1) * PT],
            start=True, stop=True),
            deps=[cv2, tro] + ([cb_toks[-1]] if cb_toks else []))
        ck = p.emit("vector", lambda v, c_=c_: v.tensor_copy(
            cstb[:, c_ * PT:(c_ + 1) * PT], psT[:, 0:PT]), deps=[bk])
        cb_toks.append(ck)
    cst_done = cb_toks[-1]
    pv_tok = None
    for tt in range(TT_):
        for i in range(DT):
            p.emit("sync", lambda s, i=i, tt=tt: s.dma_start(
                out=xfin[i][:, :], in_=x_in[i * PT:(i + 1) * PT, tt * PT:(tt + 1) * PT]),
                deps=([pv_tok] if pv_tok else expert_done), dma=True)
        gxt = p.gate("sync")
        xt = [gxt] * DT
        a0 = p.emit("vector", lambda v, tt=tt: v.tensor_add(
            tmpf[:, :], acc16[tt][:, :], cstb[:, :]),
            deps=[cst_done] + list(bankG.values()) +
            ([pv_tok] if pv_tok else []))
        otoks = []
        for i in range(DT):
            tr = p.emit("tensor", lambda t, i=i: t.transpose(
                psT[:, 0:PT], tmpf[:, i * PT:(i + 1) * PT], ident16[:, :]),
                deps=[a0] + ([otoks[-1][1]] if otoks else [cst_done]))
            o1 = p.emit("vector", lambda v, i=i: v.scalar_tensor_tensor(
                xfin[i][:, :], psT[:, 0:PT], 0.5, xfin[i][:, :], ALU.mult, ALU.add),
                deps=[tr, xt[i]])
            p.emit("gpsimd", lambda g, i=i, tt=tt: g.dma_start(
                out=x_out[i * PT:(i + 1) * PT, tt * PT:(tt + 1) * PT],
                in_=xfin[i][:, :]), deps=[o1], dma=True)
            otoks.append((tr, o1))
        pv_tok = p.gate("gpsimd")

    toks = p.barrier_tokens()
    p.emit("sync", lambda s: s.nop(), deps=toks)
    with nc.Block() as block:
        p.finalize(block)
    return nc


def _prep_moe_main_inputs(xcore, xn_core, aff_core, parts8, thr, W1, b1, W2, b2):
    E = 8
    b1c = np.ascontiguousarray(
        np.asarray(b1).reshape(E, 32, 128).transpose(2, 0, 1).reshape(128, E * 32))
    io8 = np.broadcast_to(np.arange(E, dtype=np.float32), (128, E)).copy()
    ioB = io8 - 999.0
    return {
        "x_in": np.ascontiguousarray(xcore),
        "xn_din": np.ascontiguousarray(xn_core),
        "aff_din": np.ascontiguousarray(aff_core),
        "parts": np.ascontiguousarray(
            np.broadcast_to(np.asarray(parts8, np.float32).reshape(1, 8), (8, 8))),
        "thr_in": np.ascontiguousarray(
            np.broadcast_to(np.asarray(thr, np.float32).reshape(1, 1), (8, 1))),
        "W1": np.ascontiguousarray(W1),
        "W2": np.ascontiguousarray(W2),
        "b1c": b1c,
        "b2r": np.ascontiguousarray(b2),
        "iota8_in": io8,
        "iotaB_in": ioB,
        "ident_in": np.eye(128, dtype=np.float32),
    }


def build_ssd(nch=None):
    DI = 2048
    IT = DI // PT
    HN = 1024
    HT = HN // PT
    NH = 16
    Tc = 512
    NCH = (T // Tc) if nch is None else nch
    QC = 32
    NSC = Tc // QC
    RTOT = 6208
    EPS2 = 1e-6
    DT_MIN, DT_MAX = 1e-4, 1.0

    nc = bass.Bass("TRN2", target_bir_lowering=False)
    x_in = nc.dram_tensor("x_in", [D, T], F32, kind="ExternalInput")
    ipT2 = nc.dram_tensor("ipT2", [D, RTOT], F32, kind="ExternalInput")
    convw_c = nc.dram_tensor("convw_c", [PT, IT * 4], F32, kind="ExternalInput")
    convb_c = nc.dram_tensor("convb_c", [PT, IT], F32, kind="ExternalInput")
    dtpT2 = nc.dram_tensor("dtpT2", [64, NH], F32, kind="ExternalInput")
    dtb_c = nc.dram_tensor("dtb_c", [NH, 1], F32, kind="ExternalInput")
    Alog_c = nc.dram_tensor("Alog_c", [PT, HT], F32, kind="ExternalInput")
    Dsk_c = nc.dram_tensor("Dsk_c", [PT, NH], F32, kind="ExternalInput")
    nw2_c = nc.dram_tensor("nw2_c", [PT, IT], F32, kind="ExternalInput")
    resT = nc.dram_tensor("resT", [D, DI], F32, kind="ExternalInput")
    opT2 = nc.dram_tensor("opT2", [DI, D], F32, kind="ExternalInput")
    indW_c = nc.dram_tensor("indW_c", [PT, HT * NH], F32, kind="ExternalInput")
    bsel_c = nc.dram_tensor("bsel_c", [NH, HT * PT], F32, kind="ExternalInput")
    hsel_c = nc.dram_tensor("hsel_c", [NH, NH * PT], F32, kind="ExternalInput")
    ident_c = nc.dram_tensor("ident_c", [PT, PT], F32, kind="ExternalInput")
    x_out = nc.dram_tensor("x_out", [D, T], F32, kind="ExternalOutput")

    es = nc.ctx
    sb = lambda n, s, d=F32: es.enter_context(nc.sbuf_tensor(n, s, d))
    psf = lambda n, s, d=F32: es.enter_context(nc.psum_tensor(n, s, d))
    APX = bass.AP

    convw = sb("convw", [PT, IT * 4])
    convb = sb("convb", [PT, IT])
    dtpTs = sb("dtpTs", [64, NH], BF16)
    dtb16 = sb("dtb16", [NH, 1])
    A_sb = sb("A_sb", [PT, HT])
    Dsk = sb("Dsk", [PT, NH])
    nw2 = sb("nw2", [PT, IT])
    indW = sb("indW", [PT, HT * NH], BF16)
    bsel32 = sb("bsel32", [NH, HT * PT])
    bsel16 = sb("bsel16", [NH, HT * PT], BF16)
    hsel = sb("hsel", [NH, NH * PT], BF16)
    id16 = sb("id16", [PT, PT], BF16)
    ones_col = sb("ones_colZ", [PT, 1], BF16)
    ones_row = sb("ones_rowZ", [1, PT], BF16)
    eps_col = sb("eps_colZ", [PT, 1])
    ipg = [[sb(f"ipg_{b}_{k}", [PT, 512], BF16) for k in range(DT)] for b in range(2)]
    x_sb = [sb(f"x_{i}", [PT, Tc]) for i in range(DT)]
    xn16 = [sb(f"xn_{i}", [PT, Tc], BF16) for i in range(DT)]
    xq16 = [sb(f"xq_{i}", [PT, Tc], BF16) for i in range(2)]
    u16 = [sb(f"u16_{i}", [PT, Tc], BF16) for i in range(IT)]
    z16 = [sb(f"z16_{i}", [PT, Tc], BF16) for i in range(IT)]
    usb = [sb(f"usb_{i}", [PT, Tc + 4]) for i in range(IT)]
    halb = sb("halbZ", [PT, 3 * IT])
    B16 = [sb(f"B16_{i}", [PT, Tc], BF16) for i in range(HT)]
    C16 = [sb(f"C16_{i}", [PT, Tc], BF16) for i in range(HT)]
    dth16 = sb("dth16", [64, Tc], BF16)
    dtrow = sb("dtrow", [NH, Tc])
    dt16r = sb("dt16r", [NH, Tc], BF16)
    S_row = sb("S_row", [NH, Tc])
    scar = sb("scar", [NH, 1])
    S_b = [sb(f"S_b_{i}", [PT, Tc + 1]) for i in range(HT)]
    CW = [sb(f"CW_{i}", [PT, Tc]) for i in range(HT)]
    BwG = [sb(f"BwG_{i}", [PT, Tc], BF16) for i in range(HT)]
    Gam = [sb(f"Gam_{i}", [PT, NSC]) for i in range(HT)]
    e16 = sb("e16", [PT, Tc], BF16)
    w16 = sb("w16", [NH, Tc], BF16)
    hst = [sb(f"hst_{i}", [PT, PT]) for i in range(HT)]
    uT = [sb(f"uT_{pc}", [64, PT], BF16) for pc in range(8)]
    BwT = [sb(f"BwT_{pc}", [64, 64], BF16) for pc in range(8)]
    sc_a = sb("sc_aZ", [PT, Tc])
    sc_b = sb("sc_bZ", [PT, Tc])
    sc_a2 = sb("sc_a2Z", [PT, Tc])
    mrow = sb("m_rowZ", [1, Tc])[0:1, :]
    vrow = sb("v_rowZ", [1, Tc])[0:1, :]
    rrow = sb("r_rowZ", [1, Tc])[0:1, :]
    m16 = sb("m16Z", [1, Tc], BF16)[0:1, :]
    r16 = sb("r16Z", [1, Tc], BF16)[0:1, :]
    yac = [usb[i][:, 0:Tc] for i in range(IT)]

    pk = [psf(f"pk_{i}", [PT, Tc]) for i in range(4)]
    pw = psf("pw", [NH, Tc])
    pyh = psf("pyh", [PT, Tc])
    ptr16 = psf("ptr16", [64, PT], BF16)
    pbh = psf("pbh", [64, PT])

    p = Prog(nc)
    for dst, src_ in [(convw, convw_c), (convb, convb_c), (dtb16, dtb_c),
                      (A_sb, Alog_c), (Dsk, Dsk_c), (nw2, nw2_c),
                      (bsel32, bsel_c)]:
        p.emit("sync", lambda s, dst=dst, src_=src_: s.dma_start(
            out=dst[:, :], in_=src_[:, :]), dma=True)
    sg0 = p.gate("sync")
    for dst, src_ in [(dtpTs, dtpT2), (indW, indW_c), (hsel, hsel_c),
                      (id16, ident_c)]:
        p.emit("gpsimd", lambda g, dst=dst, src_=src_: g.dma_start(
            out=dst[:, :], in_=src_[:, :]), dma=True)
    rg = p.gate("gpsimd")
    t1 = p.emit("scalar", lambda s: s.activation(A_sb[:, :], A_sb[:, :], AF.Exp),
                deps=[sg0])
    tA = p.emit("vector", lambda v: v.tensor_single_scalar(
        A_sb[:, :], A_sb[:, :], -1.0, ALU.mult), deps=[t1])
    tb16 = p.emit("vector", lambda v: v.tensor_copy(bsel16[:, :], bsel32[:, :]),
                  deps=[sg0])
    tco = p.emit("vector", lambda v: v.memset(ones_col[:, :], 1.0))
    tro = p.emit("vector", lambda v: v.memset(ones_row[:, :], 1.0))
    tep = p.emit("vector", lambda v: v.memset(eps_col[:, :], LN_EPS))
    thal = p.emit("vector", lambda v: v.memset(halb[:, :], 0.0))
    tscz = p.emit("vector", lambda v: v.memset(scar[:, :], 0.0))
    hst_tok = [p.emit("vector", lambda v, i=i: v.memset(hst[i][:, :], 0.0))
               for i in range(HT)]
    tsb0 = [p.emit("vector", lambda v, i=i: v.memset(S_b[i][:, 0:1], 0.0))
            for i in range(HT)]

    bank_free = {}
    ipg_free = [[], []]
    prev_end = []

    def mm_bank(idx):
        return pk[idx], f"pk_{idx}"

    for ch in range(NCH):
        ts_ = slice(ch * Tc, (ch + 1) * Tc)
        for i in range(DT):
            p.emit("sync", lambda s, i=i, ts_=ts_: s.dma_start(
                out=x_sb[i][:, :], in_=x_in[i * PT:(i + 1) * PT, ts_]),
                deps=prev_end, dma=True)
        gx = p.gate("sync")
        stat_toks = []
        for i in range(DT):
            c1 = p.emit("vector", lambda v, i=i: v.tensor_copy(
                xq16[0][:, :], x_sb[i][:, :]),
                deps=[gx] + ([stat_toks[-1]] if stat_toks else []))
            c2 = p.emit("scalar", lambda s, i=i: s.activation(
                xq16[1][:, :], x_sb[i][:, :], AF.Square),
                deps=[gx] + ([stat_toks[-1]] if stat_toks else []))
            m1 = p.emit("tensor", lambda t, i=i: t.matmul(
                pk[2][0:1, :], ones_col[:, 0:1], xq16[0][:, :],
                start=(i == 0), stop=(i == DT - 1)),
                deps=[c1, tco] + ([bank_free["pk_2"]] if i == 0 and "pk_2" in bank_free else []))
            m2 = p.emit("tensor", lambda t, i=i: t.matmul(
                pk[3][0:1, :], ones_col[:, 0:1], xq16[1][:, :],
                start=(i == 0), stop=(i == DT - 1)),
                deps=[c2] + ([bank_free["pk_3"]] if i == 0 and "pk_3" in bank_free else []))
            stat_toks.append(m2)
        st = stat_toks[-1]
        e2 = p.emit("vector", lambda v: v.tensor_single_scalar(
            mrow, pk[2][0:1, :], 1.0 / D, ALU.mult), deps=[st])
        bank_free["pk_2"] = e2
        e3 = p.emit("scalar", lambda s: s.activation(sc_a[0:1, 0:Tc], mrow, AF.Square),
                    deps=[e2])
        e4 = p.emit("vector", lambda v: v.tensor_single_scalar(
            vrow, pk[3][0:1, :], 1.0 / D, ALU.mult), deps=[st])
        bank_free["pk_3"] = e4
        e5 = p.emit("vector", lambda v: v.tensor_sub(vrow, vrow, sc_a[0:1, 0:Tc]),
                    deps=[e4, e3])
        e6 = p.emit("scalar", lambda s: s.activation(
            sc_a[0:1, 0:Tc], vrow, AF.Sqrt, bias=eps_col[0:1, 0:1]), deps=[e5, tep])
        e7 = p.emit("vector", lambda v: v.reciprocal(rrow, sc_a[0:1, 0:Tc]), deps=[e6])
        c8 = p.emit("vector", lambda v: v.tensor_copy(m16, mrow), deps=[e2])
        c9 = p.emit("vector", lambda v: v.tensor_copy(r16, rrow), deps=[e7])
        bc1 = p.emit("tensor", lambda t: t.matmul(
            pk[0][:, :], ones_row[0:1, :], m16, start=True, stop=True),
            deps=[c8, tro] + ([bank_free["pk_0"]] if "pk_0" in bank_free else []))
        bc2 = p.emit("tensor", lambda t: t.matmul(
            pk[1][:, :], ones_row[0:1, :], r16, start=True, stop=True),
            deps=[c9] + ([bank_free["pk_1"]] if "pk_1" in bank_free else []))
        xnt = []
        last_s2 = None
        for i in range(DT):
            s1 = p.emit("vector", lambda v, i=i: v.tensor_sub(
                sc_a[:, :], x_sb[i][:, :], pk[0][:, :]),
                deps=[bc1, gx] + ([last_s2] if last_s2 else []))
            s2 = p.emit("vector", lambda v, i=i: v.tensor_mul(
                xn16[i][:, :], sc_a[:, :], pk[1][:, :]), deps=[bc2, s1])
            last_s2 = s2
            xnt.append(s2)
        bank_free["pk_0"] = last_s2
        bank_free["pk_1"] = last_s2
        prj_ev = {}
        for fg in range(ceil_div(RTOT, 512)):
            buf = fg % 2
            c0 = fg * 512
            cw = min(512, RTOT - c0)
            for k in range(DT):
                p.emit("gpsimd", lambda g, k=k, c0=c0, cw=cw, buf=buf: g.dma_start(
                    out=ipg[buf][k][:, 0:cw],
                    in_=ipT2[k * PT:(k + 1) * PT, c0:c0 + cw]),
                    deps=ipg_free[buf], dma=True)
            gfg = p.gate("gpsimd")
            ipg_free[buf] = []
            for jj in range(ceil_div(cw, PT)):
                r0 = c0 + jj * PT
                mw = min(PT, cw - jj * PT)
                bank, bname = mm_bank(2 + jj % 2)
                mt = None
                for k in range(DT):
                    mt = p.emit("tensor", lambda t, k=k, jj=jj, buf=buf, bank=bank, mw=mw: t.matmul(
                        bank[0:mw, :], ipg[buf][k][:, jj * PT:jj * PT + mw],
                        xn16[k][:, :], start=(k == 0), stop=(k == DT - 1)),
                        deps=[gfg, xnt[k]] +
                        ([bank_free[bname]] if bname in bank_free and k == 0 else []))
                ipg_free[buf].append(mt)
                ti = r0 // PT
                if r0 < DI:
                    ev = p.emit("scalar", lambda s, ti=ti, bank=bank: s.activation(
                        z16[ti][:, :], bank[:, :], AF.Silu), deps=[mt])
                elif r0 < 2 * DI:
                    ev = p.emit("vector", lambda v, ti=ti, bank=bank: v.tensor_copy(
                        usb[ti - 16][:, 3:3 + Tc], bank[:, :]), deps=[mt])
                elif r0 < 2 * DI + HN:
                    ev = p.emit("vector", lambda v, ti=ti, bank=bank: v.tensor_copy(
                        B16[ti - 32][:, :], bank[:, :]), deps=[mt])
                elif r0 < 2 * DI + 2 * HN:
                    ev = p.emit("scalar", lambda s, ti=ti, bank=bank: s.copy(
                        C16[ti - 40][:, :], bank[:, :]), deps=[mt])
                else:
                    ev = p.emit("vector", lambda v, bank=bank, mw=mw: v.tensor_copy(
                        dth16[0:mw, :], bank[0:mw, :]), deps=[mt])
                bank_free[bname] = ev
                prj_ev[ti] = ev
        conv_toks = []
        for it in range(IT):
            h0 = p.emit("vector", lambda v, it=it: v.tensor_copy(
                usb[it][:, 0:3], halb[:, it * 3:(it + 1) * 3]),
                deps=[prj_ev[16 + it], thal] + ([conv_toks[-1][0]] if conv_toks else []))
            a1 = p.emit("vector", lambda v, it=it: v.tensor_scalar(
                sc_a[:, :], usb[it][:, 0:Tc], convw[:, it * 4:it * 4 + 1], None,
                ALU.mult), deps=[h0, sg0])
            for kk in range(1, 4):
                a1 = p.emit("vector", lambda v, it=it, kk=kk: v.scalar_tensor_tensor(
                    sc_a[:, :], usb[it][:, kk:kk + Tc],
                    convw[:, it * 4 + kk:it * 4 + kk + 1],
                    sc_a[:, :], ALU.mult, ALU.add), deps=[a1])
            hs = p.emit("vector", lambda v, it=it: v.tensor_copy(
                halb[:, it * 3:(it + 1) * 3], usb[it][:, Tc:Tc + 3]),
                deps=[prj_ev[16 + it], h0])
            uu = p.emit("vector", lambda v, it=it: v.tensor_scalar(
                u16[it][:, :], sc_a[:, :], convb[:, it:it + 1], None, ALU.add),
                deps=[a1, sg0])
            conv_toks.append((uu, hs))
        mt = p.emit("tensor", lambda t: t.matmul(
            pw[0:NH, :], dtpTs[:, :], dth16[:, :], start=True, stop=True),
            deps=[prj_ev[48], rg] + ([bank_free["pw"]] if "pw" in bank_free else []))
        d1 = p.emit("vector", lambda v: v.tensor_scalar(
            dtrow[:, :], pw[0:NH, :], dtb16[:, 0:1], None, ALU.add), deps=[mt, sg0])
        bank_free["pw"] = d1
        d2 = p.emit("scalar", lambda s: s.activation(S_row[:, :], dtrow[:, :], AF.Abs),
                    deps=[d1])
        d3 = p.emit("scalar", lambda s: s.activation(S_row[:, :], S_row[:, :], AF.Exp,
                                                     scale=-1.0), deps=[d2])
        d4 = p.emit("scalar", lambda s: s.activation(S_row[:, :], S_row[:, :], AF.Ln,
                                                     bias=1.0), deps=[d3])
        d5 = p.emit("vector", lambda v: v.tensor_single_scalar(
            dtrow[:, :], dtrow[:, :], 0.0, ALU.max), deps=[d4, d1])
        d6 = p.emit("vector", lambda v: v.tensor_add(
            dtrow[:, :], dtrow[:, :], S_row[:, :]), deps=[d5])
        d7 = p.emit("vector", lambda v: v.tensor_single_scalar(
            dtrow[:, :], dtrow[:, :], DT_MIN, ALU.max), deps=[d6])
        d8 = p.emit("vector", lambda v: v.tensor_single_scalar(
            dtrow[:, :], dtrow[:, :], DT_MAX, ALU.min), deps=[d7])
        d9 = p.emit("vector", lambda v: v.tensor_copy(dt16r[:, :], dtrow[:, :]),
                    deps=[d8])
        sc1 = p.emit("vector", lambda v: v.tensor_tensor_scan(
            S_row[:, :], dtrow[:, :], dtrow[:, :], scar[:, 0:1],
            ALU.add, ALU.bypass), deps=[d9, tscz])
        sc2 = p.emit("vector", lambda v: v.tensor_copy(
            scar[:, 0:1], S_row[:, Tc - 1:Tc]), deps=[sc1])
        wv_toks = []
        for i in range(HT):
            bank, bname = mm_bank(i % 2)
            b1_ = p.emit("tensor", lambda t, i=i, bank=bank: t.matmul(
                bank[:, :], bsel32[:, i * PT:(i + 1) * PT], S_row[:, :],
                start=True, stop=True),
                deps=[sc1, sg0] + ([bank_free[bname]] if bname in bank_free else []))
            ev1 = p.emit("vector", lambda v, i=i, bank=bank: v.tensor_copy(
                S_b[i][:, 1:Tc + 1], bank[:, :]), deps=[b1_, tsb0[i]])
            bank_free[bname] = ev1
            bank2, bname2 = mm_bank(2 + i % 2)
            b2_ = p.emit("tensor", lambda t, i=i, bank2=bank2: t.matmul(
                bank2[:, :], bsel16[:, i * PT:(i + 1) * PT], dt16r[:, :],
                start=True, stop=True),
                deps=[d9, tb16] + ([bank_free[bname2]] if bname2 in bank_free else []))
            bd = p.emit("vector", lambda v, i=i, bank2=bank2: v.tensor_mul(
                B16[i][:, :], B16[i][:, :], bank2[:, :]), deps=[b2_, prj_ev[32 + i]])
            bank_free[bname2] = bd
            q1 = p.emit("vector", lambda v, i=i: v.tensor_sub(
                sc_a[:, :], S_b[i][:, 1:Tc + 1],
                APX(S_b[i].tensor if hasattr(S_b[i], 'tensor') else S_b[i], 0,
                    [[Tc + 1, PT], [QC, NSC], [0, QC]])), deps=[ev1] + wv_toks[-1:])
            q2 = p.emit("scalar", lambda s, i=i: s.activation(
                e16[:, :], sc_a[:, :], AF.Exp, scale=A_sb[:, i:i + 1]),
                deps=[q1, tA])
            q3 = p.emit("vector", lambda v, i=i: v.tensor_mul(
                CW[i][:, :], C16[i][:, :], e16[:, :]), deps=[q2, prj_ev[40 + i]])
            q4 = p.emit("vector", lambda v, i=i: v.tensor_sub(
                sc_a[:, :],
                APX(S_b[i].tensor if hasattr(S_b[i], 'tensor') else S_b[i], QC,
                    [[Tc + 1, PT], [QC, NSC], [0, QC]]),
                S_b[i][:, 1:Tc + 1]), deps=[q3, ev1])
            q5 = p.emit("scalar", lambda s, i=i: s.activation(
                e16[:, :], sc_a[:, :], AF.Exp, scale=A_sb[:, i:i + 1]),
                deps=[q4])
            q6 = p.emit("vector", lambda v, i=i: v.tensor_mul(
                BwG[i][:, :], B16[i][:, :], e16[:, :]), deps=[q5, bd])
            q7 = p.emit("vector", lambda v, i=i: v.tensor_tensor(
                Gam[i][:, :],
                APX(S_b[i].tensor if hasattr(S_b[i], 'tensor') else S_b[i], QC,
                    [[Tc + 1, PT], [QC, NSC]]),
                APX(S_b[i].tensor if hasattr(S_b[i], 'tensor') else S_b[i], 0,
                    [[Tc + 1, PT], [QC, NSC]]), ALU.subtract), deps=[ev1, q6])
            q8 = p.emit("scalar", lambda s, i=i: s.activation(
                Gam[i][:, :], Gam[i][:, :], AF.Exp, scale=A_sb[:, i:i + 1]),
                deps=[q7])
            wv_toks.append(q8)
        wv_done = wv_toks[-1]
        yinit = []
        for it in range(IT):
            q0 = p.emit("vector", lambda v, it=it: v.tensor_copy(
                sc_b[:, :], u16[it][:, :]),
                deps=[conv_toks[it][0], conv_toks[it][1]] + ([yinit[-1]] if yinit else [wv_done]))
            q1_ = p.emit("vector", lambda v, it=it: v.tensor_scalar(
                yac[it], sc_b[:, :], Dsk[:, it:it + 1], None, ALU.mult),
                deps=[q0, sg0])
            yinit.append(q1_)
        ysum_toks = []
        for h in range(NH):
            til = h // 2
            ro = (h % 2) * 64
            trt = []
            for pc in range(8):
                tr1 = p.emit("tensor", lambda t, h=h, pc=pc: t.transpose(
                    ptr16[0:64, :], u16[h][:, pc * 64:(pc + 1) * 64], id16[:, :]),
                    deps=[conv_toks[h][0], rg] + ([trt[-1][1]] if trt else ([ysum_toks[-1]] if ysum_toks else [])))
                tv1 = p.emit("vector", lambda v, pc=pc: v.tensor_copy(
                    uT[pc][:, :], ptr16[0:64, :]), deps=[tr1])
                tr2 = p.emit("tensor", lambda t, h=h, til=til, ro=ro, pc=pc: t.transpose(
                    ptr16[0:64, 0:64], BwG[til][ro:ro + 64, pc * 64:(pc + 1) * 64],
                    id16[ro:ro + 64, ro:ro + 64]), deps=[wv_done, tv1])
                tv2 = p.emit("vector", lambda v, pc=pc: v.tensor_copy(
                    BwT[pc][:, :], ptr16[0:64, 0:64]), deps=[tr2])
                trt.append((tr1, tv2))
            hupd = hst_tok[til]
            yiq = []
            for sc in range(NSC):
                yi = p.emit("tensor", lambda t, h=h, til=til, ro=ro, sc=sc: t.matmul(
                    pyh[:, sc * QC:(sc + 1) * QC], hst[til][ro:ro + 64, :],
                    CW[til][ro:ro + 64, sc * QC:(sc + 1) * QC],
                    start=True, stop=True),
                    deps=[hupd, wv_done] + ([bank_free["pyh"]] if sc == 0 and "pyh" in bank_free else []))
                bh = p.emit("tensor", lambda t, sc=sc: t.matmul(
                    pbh[0:64, :], BwT[sc // 2][(sc % 2) * 32:(sc % 2) * 32 + 32, :],
                    uT[sc // 2][(sc % 2) * 32:(sc % 2) * 32 + 32, :],
                    start=True, stop=True),
                    deps=[trt[sc // 2][1]] + ([hupd] if sc > 0 else [hst_tok[til]])),
                hupd = p.emit("vector", lambda v, til=til, ro=ro, sc=sc: v.scalar_tensor_tensor(
                    hst[til][ro:ro + 64, :], hst[til][ro:ro + 64, :],
                    Gam[til][ro:ro + 64, sc:sc + 1], pbh[0:64, :],
                    ALU.mult, ALU.add), deps=[bh[0], yi, wv_done])
                yiq.append(yi)
            hst_tok[til] = hupd
            ys = p.emit("vector", lambda v, h=h: v.tensor_add(
                yac[h], yac[h], pyh[:, :]), deps=[yiq[-1], yinit[h], hupd])
            bank_free["pyh"] = ys
            ysum_toks.append(ys)
        lag_last = ysum_toks[-1]
        wb_tok = {}
        for j in range(QC):
            wdt = []
            prev_t2 = {}
            for i in range(HT):
                par = i % 2
                sa = sc_a if par == 0 else sc_a2
                t1_ = p.emit("vector", lambda v, i=i, j=j, sa=sa: v.tensor_tensor(
                    APX(sa, j, [[Tc, PT], [QC, NSC], [1, QC - j]]),
                    APX(S_b[i], j + 1, [[Tc + 1, PT], [QC, NSC], [1, QC - j]]),
                    APX(S_b[i], 1, [[Tc + 1, PT], [QC, NSC], [1, QC - j]]),
                    ALU.subtract),
                    deps=[wv_done] + ([prev_t2[par]] if par in prev_t2 else [lag_last]))
                t2_ = p.emit("scalar", lambda s, i=i, j=j, sa=sa: s.activation(
                    APX(BwG[i], j, [[Tc, PT], [QC, NSC], [1, QC - j]]),
                    APX(sa, j, [[Tc, PT], [QC, NSC], [1, QC - j]]),
                    AF.Exp, scale=A_sb[:, i:i + 1]),
                    deps=[t1_, tA] + ([wb_tok["mtw"]] if "mtw" in wb_tok else []))
                prev_t2[par] = t2_
                t3_ = p.emit("gpsimd", lambda v, i=i, j=j: v.tensor_tensor(
                    APX(BwG[i], j, [[Tc, PT], [QC, NSC], [1, QC - j]]),
                    APX(BwG[i], j, [[Tc, PT], [QC, NSC], [1, QC - j]]),
                    APX(C16[i], j, [[Tc, PT], [QC, NSC], [1, QC - j]]),
                    ALU.mult), deps=[t2_])
                t4_ = p.emit("gpsimd", lambda v, i=i, j=j: v.tensor_tensor(
                    APX(BwG[i], j, [[Tc, PT], [QC, NSC], [1, QC - j]]),
                    APX(BwG[i], j, [[Tc, PT], [QC, NSC], [1, QC - j]]),
                    APX(B16[i], 0, [[Tc, PT], [QC, NSC], [1, QC - j]]),
                    ALU.mult), deps=[t3_])
                wb_tok[("G", i)] = t4_
                wdt.append(t4_)
            mtw = None
            for i in range(HT):
                mtw = p.emit("tensor", lambda t, i=i: t.matmul(
                    pw[0:NH, :], indW[:, i * NH:(i + 1) * NH], BwG[i][:, :],
                    start=(i == 0), stop=(i == HT - 1)),
                    deps=[wdt[i], rg] + ([bank_free["pw"]] if i == 0 and "pw" in bank_free else []))
            wc = p.emit("vector", lambda v: v.tensor_copy(w16[:, :], pw[0:NH, :]),
                        deps=[mtw])
            bank_free["pw"] = wc
            wb_tok["mtw"] = mtw
            yprev = None
            for h in range(NH):
                bank, bname = mm_bank(h % 4)
                wb = p.emit("tensor", lambda t, h=h, bank=bank: t.matmul(
                    bank[:, :], hsel[:, h * PT:(h + 1) * PT], w16[:, :],
                    start=True, stop=True),
                    deps=[wc, rg] + ([bank_free[bname]] if bname in bank_free else []))
                y1 = p.emit("vector", lambda v, h=h, j=j, bank=bank: v.tensor_tensor(
                    APX(sc_b, j, [[Tc, PT], [QC, NSC], [1, QC - j]]),
                    APX(u16[h], 0, [[Tc, PT], [QC, NSC], [1, QC - j]]),
                    APX(bank, j, [[Tc, PT], [QC, NSC], [1, QC - j]]),
                    ALU.mult),
                    deps=[wb, conv_toks[h][0]] + ([yprev] if yprev else []))
                y2 = p.emit("vector", lambda v, h=h, j=j: v.tensor_tensor(
                    APX(usb[h], j, [[Tc + 4, PT], [QC, NSC], [1, QC - j]]),
                    APX(usb[h], j, [[Tc + 4, PT], [QC, NSC], [1, QC - j]]),
                    APX(sc_b, j, [[Tc, PT], [QC, NSC], [1, QC - j]]),
                    ALU.add), deps=[y1, ysum_toks[h]])
                bank_free[bname] = y1
                yprev = y2
            lag_last = yprev
        res_ev = {}
        for fg in range(4):
            buf = fg % 2
            for k in range(DT):
                p.emit("gpsimd", lambda g, k=k, fg=fg, buf=buf: g.dma_start(
                    out=ipg[buf][k][:, :], in_=resT[k * PT:(k + 1) * PT,
                                                    fg * 512:(fg + 1) * 512]),
                    deps=ipg_free[buf], dma=True)
            gfg = p.gate("gpsimd")
            ipg_free[buf] = []
            for jj in range(4):
                it = fg * 4 + jj
                bank, bname = mm_bank(jj % 2)
                mt = None
                for k in range(DT):
                    mt = p.emit("tensor", lambda t, k=k, jj=jj, buf=buf, bank=bank: t.matmul(
                        bank[:, :], ipg[buf][k][:, jj * PT:(jj + 1) * PT],
                        xn16[k][:, :], start=(k == 0), stop=(k == DT - 1)),
                        deps=[gfg, xnt[k]] +
                        ([bank_free[bname]] if bname in bank_free and k == 0 else []))
                ipg_free[buf].append(mt)
                g1 = p.emit("vector", lambda v, it=it: v.tensor_mul(
                    sc_a[:, :], yac[it], z16[it][:, :]),
                    deps=[lag_last, prj_ev[it]] + ([res_ev[it - 1]] if it > 0 else []))
                g2 = p.emit("vector", lambda v, it=it, bank=bank: v.tensor_add(
                    z16[it][:, :], sc_a[:, :], bank[:, :]), deps=[g1, mt])
                bank_free[bname] = g2
                res_ev[it] = g2
        stat2 = []
        for it in range(IT):
            cq = p.emit("scalar", lambda s, it=it: s.activation(
                xq16[1][:, :], z16[it][:, :], AF.Square),
                deps=[res_ev[it]] + ([stat2[-1]] if stat2 else []))
            mq = p.emit("tensor", lambda t, it=it: t.matmul(
                pw[0:1, :], ones_col[:, 0:1], xq16[1][:, :],
                start=(it == 0), stop=(it == IT - 1)),
                deps=[cq, tco] + ([bank_free["pw"]] if it == 0 and "pw" in bank_free else []))
            stat2.append(mq)
        r1 = p.emit("vector", lambda v: v.tensor_single_scalar(
            vrow, pw[0:1, :], 1.0 / DI, ALU.mult), deps=[stat2[-1]])
        bank_free["pw"] = r1
        r2 = p.emit("vector", lambda v: v.tensor_single_scalar(
            vrow, vrow, EPS2, ALU.add), deps=[r1])
        r3 = p.emit("scalar", lambda s: s.activation(sc_a[0:1, 0:Tc], vrow, AF.Sqrt),
                    deps=[r2])
        r4 = p.emit("vector", lambda v: v.reciprocal(rrow, sc_a[0:1, 0:Tc]), deps=[r3])
        r5 = p.emit("vector", lambda v: v.tensor_copy(r16, rrow), deps=[r4])
        bcr = p.emit("tensor", lambda t: t.matmul(
            pk[0][:, :], ones_row[0:1, :], r16, start=True, stop=True),
            deps=[r5, tro] + ([bank_free["pk_0"]] if "pk_0" in bank_free else []))
        gn_toks = []
        for it in range(IT):
            n1 = p.emit("vector", lambda v, it=it: v.tensor_mul(
                sc_a[:, :], z16[it][:, :], pk[0][:, :]),
                deps=[bcr, res_ev[it]] + ([gn_toks[-1]] if gn_toks else []))
            n2 = p.emit("vector", lambda v, it=it: v.tensor_scalar(
                z16[it][:, :], sc_a[:, :], nw2[:, it:it + 1], None, ALU.mult),
                deps=[n1, sg0])
            gn_toks.append(n2)
        bank_free["pk_0"] = gn_toks[-1]
        prev_end = []
        for dh in range(2):
            mtg = [None] * 4
            for kh in range(2):
                buf = kh % 2
                for k in range(DT):
                    p.emit("gpsimd", lambda g, k=k, kh=kh, dh=dh, buf=buf: g.dma_start(
                        out=ipg[buf][k][:, :],
                        in_=opT2[kh * 1024 + k * PT:kh * 1024 + (k + 1) * PT,
                                 dh * 512:(dh + 1) * 512]),
                        deps=ipg_free[buf], dma=True)
            gop = p.gate("gpsimd")
            ipg_free[0] = []
            ipg_free[1] = []
            for kh in range(2):
                buf = kh % 2
                for jj in range(4):
                    bank, bname = mm_bank(jj)
                    for k in range(DT):
                        kg = kh * 8 + k
                        mtg[jj] = p.emit("tensor", lambda t, k=k, kg=kg, jj=jj, buf=buf, bank=bank: t.matmul(
                            bank[:, :], ipg[buf][k][:, jj * PT:(jj + 1) * PT],
                            z16[kg][:, :], start=(kg == 0), stop=(kg == IT - 1)),
                            deps=[gop, gn_toks[kg]] +
                            ([bank_free[bname]] if bname in bank_free and kg == 0 else []))
                    if kh == 1:
                        ipg_free[0].append(mtg[jj])
                        ipg_free[1].append(mtg[jj])
            for jj in range(4):
                i = dh * 4 + jj
                bank, bname = mm_bank(jj)
                o1 = p.emit("vector", lambda v, i=i, bank=bank: v.scalar_tensor_tensor(
                    x_sb[i][:, :], bank[:, :], 1.0, x_sb[i][:, :], ALU.mult, ALU.add),
                    deps=[mtg[jj]])
                bank_free[bname] = o1
                p.emit("gpsimd", lambda s, i=i, ts_=ts_: s.dma_start(
                    out=x_out[i * PT:(i + 1) * PT, ts_], in_=x_sb[i][:, :]),
                    deps=[o1], dma=True)
        prev_end = [p.gate("gpsimd")]

    toks = p.barrier_tokens()
    p.emit("sync", lambda s: s.nop(), deps=toks)
    with nc.Block() as block:
        p.finalize(block)
    return nc


def _prep_ssd_inputs(xcore, in_proj_w, dt_proj_w, conv_w, conv_b, A_log, Dskip,
                     dt_bias, norm_weight, out_proj_w, res_proj_w):
    DI, NH, DS = 2048, 16, 64
    ip = np.asarray(in_proj_w)
    order = np.concatenate([
        np.arange(0, 2 * DI),
        np.arange(2 * DI + 64, 2 * DI + 64 + 1024),
        np.arange(2 * DI + 64 + 1024, 6208),
        np.arange(2 * DI, 2 * DI + 64),
    ])
    ipr = ip[order]
    indW = np.zeros((128, 8 * 16), np.float32)
    bsel = np.zeros((16, 8 * 128), np.float32)
    hsel = np.zeros((16, 16 * 128), np.float32)
    for ti in range(8):
        for pp in range(128):
            h = (ti * 128 + pp) // 64
            indW[pp, ti * 16 + h] = 1.0
            bsel[h, ti * 128 + pp] = 1.0
    for h in range(16):
        hsel[h, h * 128:(h + 1) * 128] = 1.0
    return {
        "x_in": np.ascontiguousarray(xcore),
        "ipT2": np.ascontiguousarray(ipr.T),
        "convw_c": np.ascontiguousarray(
            np.asarray(conv_w).reshape(16, 128, 4).transpose(1, 0, 2).reshape(128, 64)),
        "convb_c": np.ascontiguousarray(np.asarray(conv_b).reshape(16, 128).T),
        "dtpT2": np.ascontiguousarray(np.asarray(dt_proj_w).T),
        "dtb_c": np.asarray(dt_bias, np.float32).reshape(16, 1),
        "Alog_c": np.ascontiguousarray(
            np.repeat(np.asarray(A_log).reshape(16, 64), 2, axis=0)
            .reshape(8, 2, 64).reshape(8, 128).T) if False else
            np.ascontiguousarray(np.asarray(A_log).reshape(8, 128).T),
        "Dsk_c": np.ascontiguousarray(np.asarray(Dskip).reshape(16, 128).T),
        "nw2_c": np.ascontiguousarray(np.asarray(norm_weight).reshape(16, 128).T),
        "resT": np.ascontiguousarray(np.asarray(res_proj_w).T),
        "opT2": np.ascontiguousarray(np.asarray(out_proj_w).T),
        "indW_c": indW,
        "bsel_c": bsel,
        "hsel_c": hsel,
        "ident_c": np.eye(128, dtype=np.float32),
    }


def kernel(**d):
    C = NCORES
    xs = [np.ascontiguousarray(d["L0_attn_x"][c].T) for c in range(C)]

    ffn_w = dict(w1=np.ascontiguousarray(d["L0_ffn_w1"]),
                 b1c=np.ascontiguousarray(d["L0_ffn_b1"].reshape(32, 128).T),
                 w2=np.ascontiguousarray(d["L0_ffn_w2"]),
                 b2c=np.ascontiguousarray(d["L0_ffn_b2"].reshape(8, 128).T))
    mam_args = dict(
        norm_w=d["L0_attn_norm_w"], in_proj_w=d["L0_attn_in_proj_w"],
        conv_w=d["L0_attn_conv_w"], conv_b=d["L0_attn_conv_b"],
        x_proj_w=d["L0_attn_x_proj_w"], dt_proj_w=d["L0_attn_dt_proj_w"],
        dt_proj_b=d["L0_attn_dt_proj_b"], A_log=d["L0_attn_A_log"],
        D_param=d["L0_attn_D_param"], out_proj_w=d["L0_attn_out_proj_w"])
    ssd_args = dict(
        in_proj_w=d["L1_attn_in_proj_w"], dt_proj_w=d["L1_attn_dt_proj_w"],
        conv_w=d["L1_attn_conv_w"], conv_b=d["L1_attn_conv_b"],
        A_log=d["L1_attn_A_log"], Dskip=d["L1_attn_Dskip"],
        dt_bias=d["L1_attn_dt_bias"], norm_weight=d["L1_attn_norm_weight"],
        out_proj_w=d["L1_attn_out_proj_w"], res_proj_w=d["L1_attn_res_proj_w"])
    waffT = np.ascontiguousarray(d["L1_ffn_waff"].T)
    thr = d["L1_ffn_affinity_threshold"]

    def launch(name, maps):
        res = run_bass_kernel_spmd(_get_prog(name), maps, core_ids=list(range(C)))
        return res.results

    def run_ffn(xs):
        r = launch("ffn0", [{"x_in": xs[c], **ffn_w} for c in range(C)])
        return [np.ascontiguousarray(r[c]["x_out"]) for c in range(C)]

    def run_mamba(xs):
        r = launch("mamba1", [_prep_mamba1_inputs(xs[c], **mam_args)
                              for c in range(C)])
        return [np.ascontiguousarray(r[c]["x_out"]) for c in range(C)]

    def run_moe(xs):
        rs = launch("moe_stats", [{"x_in": xs[c], "waffT": waffT}
                                  for c in range(C)])
        parts = np.array([rs[c]["asum_d"][0, 0] for c in range(C)], np.float32)
        rm = launch("moe_main", [_prep_moe_main_inputs(
            xs[c], rs[c]["xn_d"], rs[c]["aff_d"], parts, thr,
            d["L1_ffn_W1"], d["L1_ffn_b1"], d["L1_ffn_W2"], d["L1_ffn_b2"])
            for c in range(C)])
        return [np.ascontiguousarray(rm[c]["x_out"]) for c in range(C)]

    def run_ssd(xs):
        r = launch("ssd", [_prep_ssd_inputs(xs[c], **ssd_args)
                           for c in range(C)])
        return [np.ascontiguousarray(r[c]["x_out"]) for c in range(C)]

    xs = run_ffn(xs)
    xs = run_mamba(xs)
    xs = run_ffn(xs)
    xs = run_moe(xs)
    xs = run_ssd(xs)
    xs = run_moe(xs)
    return np.stack([x.T for x in xs]).astype(np.float32)
